# version 05
# baseline (speedup 1.0000x reference)
import numpy as np
import ml_dtypes
from contextlib import ExitStack

import concourse.bass as bass
import concourse.mybir as mybir
from concourse.bass import broadcast_tensor_aps
from concourse.bass_core_rust import IndirectOffsetOnAxis
from concourse.bass_utils import run_bass_kernel_spmd

F32 = mybir.dt.float32
BF16 = mybir.dt.bfloat16
AF = mybir.ActivationFunctionType
ALU = mybir.AluOpType
AX = mybir.AxisListType

P = 128
D = 1024
NCH = 8
TLOC = 4096
NT = TLOC // 512
NTG = 4
USE_FUSED = True
SPARSE_L1 = True
ATTN_NCHK = None
ATTN_NG = None
ATTN_SQR = True
ATTN_APPLY = True
ATTN_GRAM = True
ATTN_FIN = True
ATTN_OY = True
ATTN_KS2 = 3
NCORES = 8
LN_EPS = 1e-5
BF = ml_dtypes.bfloat16

ENGS = ("sync", "tensor", "vector", "scalar", "gpsimd")


class Prog:
    def __init__(self, nc):
        self.nc = nc
        self.ops = {e: [] for e in ENGS}
        self.es = ExitStack()
        self.sems = {}
        self.uid = 0
        self.last_marker = {}

    def sem(self, name):
        if name not in self.sems:
            self.sems[name] = [self.es.enter_context(self.nc.semaphore(name)), 0]
        return self.sems[name]

    def cnt(self, name):
        return self.sem(name)[1]

    def bump(self, name, n):
        rec = self.sem(name)
        rec[1] += n
        return rec[1]

    def op(self, eng, fn, waits=(), inc=None, n=1, is_dma=False):
        waits = tuple((w, v) for w, v in waits)
        if getattr(self, "_scope_waits", None) and eng not in \
                getattr(self, "_scope_seen", set()):
            self._scope_seen.add(eng)
            waits = tuple(self._scope_waits) + waits
        for w, _ in waits:
            self.sem(w)
        chain = None
        if not is_dma and eng in ("vector", "gpsimd", "scalar"):
            lm = self.last_marker.get(eng)
            if lm is not None:
                waits = waits + (lm,)
            if inc is not None:
                self.bump(inc, n)
                self.last_marker[eng] = (inc, self.cnt(inc))
                inc_done = True
            else:
                chain = f"_cq_{eng}"
                self.bump(chain, 1)
                self.last_marker[eng] = (chain, self.cnt(chain))
            for w, _ in waits:
                self.sem(w)
        else:
            if inc is not None:
                self.bump(inc, n)
        sems = self.sems

        def run(e, fn=fn, waits=waits, inc=inc, n=n, chain=chain):
            for w, v in waits:
                e.wait_ge(sems[w][0], v)
            ins = fn(e)
            upd = inc if inc is not None else chain
            if upd is not None:
                un = n if inc is not None else 1
                if ins is None:
                    e.sem_inc(sems[upd][0], un)
                else:
                    ins.then_inc(sems[upd][0], un)

        self.ops[eng].append(run)
        return self.cnt(inc) if inc is not None else None

    def wait(self, eng, waits):
        self.op(eng, lambda e: None, waits=waits)

    def snapshot(self):
        return [(name, rec[1]) for name, rec in self.sems.items()
                if rec[1] > 0]

    def begin_scope(self, waits):
        self._scope_waits = list(waits)
        self._scope_seen = set()

    def scope_es(self):
        es = ExitStack()
        self._scope_stack = es
        return es

    def dma(self, eng, out, in_, waits=(), inc=None, **kw):
        return self.op(eng, lambda e: e.dma_start(out=out, in_=in_, **kw),
                       waits=waits, inc=inc, n=16, is_dma=True)

    def sbuf(self, name, shape, dtype):
        self.uid += 1
        es = getattr(self, "_scope_stack", None) or self.es
        return es.enter_context(
            self.nc.sbuf_tensor(f"{name}_{self.uid}", shape, dtype))

    def psum(self, name, shape, dtype=F32):
        self.uid += 1
        return self.es.enter_context(
            self.nc.psum_tensor(f"{name}_{self.uid}", shape, dtype))

    def finalize(self):
        ops = self.ops
        with self.nc.Block() as block:
            @block.sync
            def _(e):
                for f in ops["sync"]:
                    f(e)

            @block.tensor
            def _(e):
                for f in ops["tensor"]:
                    f(e)

            @block.vector
            def _(e):
                for f in ops["vector"]:
                    f(e)

            @block.scalar
            def _(e):
                for f in ops["scalar"]:
                    f(e)

            @block.gpsimd
            def _(e):
                for f in ops["gpsimd"]:
                    f(e)
        self.es.close()
        return self.nc


def xc3(xc_ap, tt=None):
    v = xc_ap.rearrange("(c p) t -> p c t", p=P)
    if tt is None:
        return v
    return v[:, :, tt * 512:(tt + 1) * 512]


def sb3(sb, tt=None, w=512):
    v = sb.rearrange("p (c t) -> p c t", c=NCH)
    if tt is None:
        return v
    return v[:, :, tt * w:(tt + 1) * w]



def make_consts():
    cb = np.zeros((160, 2688), dtype=BF)
    cb[0:128, 0:128] = np.eye(128, dtype=BF)
    cb[128, 0:128] = 1
    rs = np.zeros((8, 8, 128), dtype=BF)
    for e in range(8):
        rs[e, e, :] = 1
    cb[129:137, 0:1024] = rs.transpose(1, 0, 2).reshape(8, 1024)
    gsel = np.zeros((128, 8), dtype=BF)
    for e in range(8):
        gsel[16 * e:16 * e + 16, e] = 1
    cb[0:128, 1024:1032] = gsel
    cb[129:137, 1024:1152] = gsel.T
    gb = np.zeros((8, 128, 128), dtype=BF)
    for e in range(8):
        for j in range(16):
            for e2 in range(8):
                gb[e, 16 * e + j, 16 * e2 + j] = 1
    cb[0:128, 1152:2176] = gb.transpose(1, 0, 2).reshape(128, 1024)
    low = np.tril(np.ones((8, 8)), -1).T.astype(BF)
    cb[129:137, 2176:2184] = low
    ii = np.arange(128)
    cb[0:128, 2304:2432] = (ii[:, None] >= ii[None, :]).astype(BF)
    cb[0:128, 2432:2560] = (ii[:, None] > ii[None, :]).astype(BF)
    cb[0:128, 2560:2688] = (ii[:, None] < ii[None, :]).astype(BF)
    return cb


def make_constsf():
    cf = np.zeros((128, 532), dtype=np.float32)
    cf[:, 144:272] = np.eye(128, dtype=np.float32)
    ii = np.arange(128).astype(np.float32)
    d = ii[:, None] - ii[None, :]
    cf[:, 272:400] = np.where(d >= 0, d, 1e30)
    cf[:, 400:528] = np.where(-d >= 0, -d, 1e30)
    cf[:, 528] = ii + 1
    cf[:, 529] = 127 - ii
    cf[:, 530] = 128
    cf[:, 531] = ii
    gsel = np.zeros((128, 8), dtype=np.float32)
    for e in range(8):
        gsel[16 * e:16 * e + 16, e] = 1
    cf[:, 0:8] = gsel
    cf[0:8, 8:136] = gsel.T
    cf[0:8, 136:144] = np.eye(8, dtype=np.float32)
    return cf


class Common:
    def __init__(self, p: Prog, cb, cf=None):
        self.ident = p.sbuf("c_ident", [P, P], BF16)
        self.ones_row = p.sbuf("c_ones", [1, P], BF16)
        self.ones_f = p.sbuf("c_onesf", [P, 1], F32)
        self.ones_rf = p.sbuf("c_onesrf", [1, P], F32)

        self.gsel = p.sbuf("c_gsel", [P, 8], BF16)
        self.gselT = p.sbuf("c_gselT", [8, P], BF16)
        self.scr = [p.sbuf(f"scrf_{i}", [P, TLOC], F32) for i in range(4)]
        self.rowp = p.sbuf("rowp", [P, 2 * TLOC], F32)
        self.pp = [p.psum(f"bank{i}", [P, 512], F32) for i in range(8)]
        p.dma("sync", self.ident[:], cb[0:128, 0:128], inc="qC")
        p.dma("sync", self.ones_row[:], cb[128:129, 0:128], inc="qC")
        self.rowsel = self.rowp[32:40, 0:512].bitcast(BF16)
        p.dma("sync", self.rowsel, cb[129:137, 0:1024], inc="qC")
        p.dma("sync", self.gsel[:], cb[0:128, 1024:1032], inc="qC")
        p.dma("sync", self.gselT[:], cb[129:137, 1024:1152], inc="qC")
        self.lowT = self.rowp[32:40, 4608:4612].bitcast(BF16)
        self.cb = cb
        p.dma("sync", self.lowT, cb[129:137, 2176:2184], inc="qC")
        if cf is not None:
            self.cf = p.sbuf("c_cf", [P, 532], F32)
            p.dma("sync", self.cf[:], cf, inc="qC")
            self.gself = self.cf[:, 0:8]
            self.gselTf = self.cf[0:8, 8:136]
            self.oh8 = self.cf[0:8, 136:144]
            self.identf = self.cf[:, 144:272]
            self.idxdl = self.cf[:, 272:400]
            self.idxdlt = self.cf[:, 400:528]
            self.idx1 = self.cf[:, 528:529]
            self.revidx = self.cf[:, 529:530]
            self.c128 = self.cf[:, 530:531]
            self.rowidx = self.cf[:, 531:532]
            self.tri_incl = cb[0:128, 2304:2432]
            self.tri_strict = cb[0:128, 2432:2560]
        self.ready = [("qC", p.cnt("qC"))]
        p.op("vector", lambda v: v.memset(self.ones_f[:], 1.0),
             waits=self.ready)
        p.op("vector", lambda v: v.memset(self.ones_rf[:], 1.0), inc="vS")
        self.readyv = [("vS", p.cnt("vS"))]



def emit_ln(p: Prog, com: Common, xc_ap, xn_sb, tag, x_ready,
            fuse_double=False, gvec=None, bvec=None):
    mrow = com.rowp[0:1, 0:TLOC]
    rrow = com.rowp[0:1, TLOC:2 * TLOC]
    xt = [com.scr[0], com.scr[1]]
    xsq = com.scr[2]
    crow = xsq[0:1, 0:TLOC]
    Rrow = rrow if not fuse_double else crow
    pst = com.pp[0]
    pst2 = com.pp[1]
    pbc = com.pp[2]
    pbc2 = com.pp[3]

    sums = mrow
    ssq = rrow
    tS_after = []
    for tt in range(NT):
        tb = tt % 2
        ld = p.dma("sync", xt[tb][:], xc3(xc_ap, tt),
                   waits=(list(x_ready) if tt == 0 else
                          ([("tS", tS_after[tt - 2])] if tt >= 2 else [])),
                   inc=f"qL{tb}")
        p.op("vector",
             lambda v, tb=tb: v.tensor_tensor(
                 sb3(xsq), sb3(xt[tb]), sb3(xt[tb]), ALU.mult),
             waits=[(f"qL{tb}", ld)] + list(com.readyv) +
                   ([("tS", tS_after[tt - 1])] if tt >= 1 else []),
             inc="vS")
        for c in range(NCH):
            p.op("tensor",
                 lambda t, c=c, tb=tb: t.matmul(
                     pst[0:1, :], com.ones_f[:], sb3(xt[tb])[:, c, :],
                     start=(c == 0), stop=(c == NCH - 1)),
                 waits=([("vS", p.cnt("vS"))] +
                        ([("sS", p.cnt("sS"))] if tt > 0 else []))
                 if c == 0 else ())
        for c in range(NCH):
            p.op("tensor",
                 lambda t, c=c: t.matmul(
                     pst2[0:1, :], com.ones_f[:], sb3(xsq)[:, c, :],
                     start=(c == 0), stop=(c == NCH - 1)),
                 inc="tS" if c == NCH - 1 else None)
        tS_after.append(p.cnt("tS"))
        sl = slice(tt * 512, (tt + 1) * 512)
        p.op("scalar", lambda s, sl=sl: s.copy(sums[:, sl], pst[0:1, :]),
             waits=[("tS", p.cnt("tS"))])
        p.op("scalar", lambda s, sl=sl: s.copy(ssq[:, sl], pst2[0:1, :]),
             inc="sS")
    p.op("vector", lambda v: v.tensor_scalar_mul(mrow[:], sums[:], 1.0 / D),
         waits=[("sS", p.cnt("sS"))])
    p.op("vector", lambda v: v.tensor_mul(crow[:], mrow[:], mrow[:]))
    p.op("vector", lambda v: v.scalar_tensor_tensor(
        rrow[:], ssq[:], 1.0 / D, crow[:], ALU.mult, ALU.subtract))
    p.op("vector", lambda v: v.tensor_scalar_add(rrow[:], rrow[:], LN_EPS),
         inc="vS")
    p.op("scalar", lambda s: s.activation(rrow[:], rrow[:], AF.Sqrt),
         waits=[("vS", p.cnt("vS"))], inc="sS")
    p.op("vector", lambda v: v.reciprocal(rrow[:], rrow[:]),
         waits=[("sS", p.cnt("sS"))], inc="vS")
    if fuse_double:
        p.op("vector", lambda v: v.tensor_mul(Rrow[:], rrow[:], rrow[:]))
        p.op("vector", lambda v: v.tensor_scalar(
            Rrow[:], Rrow[:], -LN_EPS, 1.0 + LN_EPS, ALU.mult, ALU.add),
            inc="vS")
        p.op("scalar", lambda s: s.activation(Rrow[:], Rrow[:], AF.Sqrt),
             waits=[("vS", p.cnt("vS"))], inc="sS")
        p.op("vector", lambda v: v.reciprocal(Rrow[:], Rrow[:]),
             waits=[("sS", p.cnt("sS"))])
        p.op("vector", lambda v: v.tensor_mul(Rrow[:], Rrow[:], rrow[:]),
             inc="vS")

    v2_after = []
    for tt in range(NT):
        tb = tt % 2
        ld = p.dma("sync", xt[tb][:], xc3(xc_ap, tt),
                   waits=[("vS2", v2_after[tt - 2])] if tt >= 2 else
                         [("tS", p.cnt("tS"))],
                   inc=f"qL{tb}")
        sl = slice(tt * 512, (tt + 1) * 512)
        p.op("tensor", lambda t, sl=sl: t.matmul(
            pbc[:], com.ones_rf[:], mrow[:, sl], start=True, stop=True),
            waits=[("vS", p.cnt("vS"))] +
                  ([("vS2", v2_after[tt - 1])] if tt >= 1 else []))
        p.op("tensor", lambda t, sl=sl: t.matmul(
            pbc2[:], com.ones_rf[:], Rrow[:, sl], start=True, stop=True),
            inc="tS")
        mb3 = pbc.ap().rearrange("p (o t) -> p o t", o=1)
        rb3 = pbc2.ap().rearrange("p (o t) -> p o t", o=1)
        tmp3 = sb3(xt[tb])

        def sub_pass(v, tb=tb, mb3=mb3, tmp3=tmp3):
            a, b = broadcast_tensor_aps(sb3(xt[tb]), mb3)
            return v.tensor_tensor(tmp3, a, b, ALU.subtract)
        p.op("vector", sub_pass,
             waits=[("tS", p.cnt("tS")), (f"qL{tb}", ld)])

        if not fuse_double:
            def mul_pass(v, tt=tt, rb3=rb3, tmp3=tmp3):
                a, b = broadcast_tensor_aps(tmp3, rb3)
                return v.tensor_tensor(sb3(xn_sb, tt), a, b, ALU.mult)
            p.op("vector", mul_pass, inc="vS2")
        else:
            def mul_pass(v, rb3=rb3, tmp3=tmp3):
                a, b = broadcast_tensor_aps(tmp3, rb3)
                return v.tensor_tensor(tmp3, a, b, ALU.mult)
            p.op("vector", mul_pass, inc="vS2")
            for c in range(NCH):
                def aff(s, c=c, tt=tt, tmp3=tmp3):
                    return s.activation(
                        sb3(xn_sb, tt)[:, c, :], tmp3[:, c, :],
                        AF.Identity, bias=bvec[:, c:c + 1],
                        scale=gvec[:, c:c + 1])

                p.op("scalar", aff,
                     waits=[("vS2", p.cnt("vS2"))] if c == 0 else (),
                     inc="vS2" if c == NCH - 1 else None)
        v2_after.append(p.cnt("vS2"))
    done = [("vS2", p.cnt("vS2"))]
    return mrow, rrow, done



def emit_l0_attn(p: Prog, com: Common, xc_ap, xn_sb, xn_done, wts, carry_sb,
                 carry_ready, tag="l0a", update_x=True):
    wdt = p.sbuf(f"{tag}_wdt", [P, NCH * P], BF16)
    wit = p.sbuf(f"{tag}_wit", [P, NCH * P], BF16)
    bd = p.sbuf(f"{tag}_bd", [P, NCH], F32)
    bi = p.sbuf(f"{tag}_bi", [P, NCH], F32)
    a_sb = com.scr[0]
    b_sb = com.scr[1]
    s_sb = com.scr[2]
    xt = [com.scr[3], com.scr[3]]
    fin = p.sbuf(f"{tag}_fin", [P, NCH], F32)
    pdec = [com.pp[4], com.pp[5]]
    pinj = [com.pp[6], com.pp[7]]

    p.dma("sync", bd[:], wts["bd"], inc=f"qB{tag}")
    p.dma("sync", bi[:], wts["bi"], inc=f"qB{tag}")
    bias_rdy = p.cnt(f"qB{tag}")

    upd_after = []
    for ec in range(NCH):
        p.dma("gpsimd",
              wdt.rearrange("p (c k) -> p c k", c=NCH),
              wts["WdT"][:, ec * P:(ec + 1) * P].rearrange(
                  "(c p) k -> p c k", p=P),
              waits=[("tS", p.cnt("tS"))] if ec > 0 else (), inc="qW")
        p.dma("gpsimd",
              wit.rearrange("p (c k) -> p c k", c=NCH),
              wts["WiT"][:, ec * P:(ec + 1) * P].rearrange(
                  "(c p) k -> p c k", p=P),
              inc="qW")
        wrdy = p.cnt("qW")
        tb = ec % 2
        if update_x:
            xrdy = p.dma("sync", xt[0][:], xc_ap[ec * P:(ec + 1) * P, :],
                         waits=([("qX0", p.cnt("qX0"))] if ec >= 1
                                else list(xn_done)),
                         inc="qA0")
        for tt in range(NT):
            pb = tt % 2
            for c in range(NCH):
                p.op("tensor",
                     lambda t, c=c, tt=tt, pb=pb: t.matmul(
                         pdec[pb][:], wdt[:, c * P:(c + 1) * P],
                         sb3(xn_sb, tt)[:, c, :],
                         start=(c == 0), stop=(c == NCH - 1)),
                     waits=(list(xn_done) + [("qW", wrdy)] +
                            [("sS", p.cnt("sS"))])
                     if c == 0 else ())
            for c in range(NCH):
                p.op("tensor",
                     lambda t, c=c, tt=tt, pb=pb: t.matmul(
                         pinj[pb][:], wit[:, c * P:(c + 1) * P],
                         sb3(xn_sb, tt)[:, c, :],
                         start=(c == 0), stop=(c == NCH - 1)),
                     inc="tS" if c == NCH - 1 else None)
            sl = slice(tt * 512, (tt + 1) * 512)
            p.op("scalar",
                 lambda s, sl=sl, pb=pb, ec=ec: s.activation(
                     a_sb[:, sl], pdec[pb][:], AF.Sigmoid,
                     bias=bd[:, ec:ec + 1]),
                 waits=[("tS", p.cnt("tS")), (f"qB{tag}", bias_rdy),
                        ("vS3", upd_after[ec - 1] if ec > 0 else 0)])
            p.op("scalar",
                 lambda s, sl=sl, pb=pb, ec=ec: s.activation(
                     b_sb[:, sl], pinj[pb][:], AF.Identity,
                     bias=bi[:, ec:ec + 1]),
                 inc="sS")
        p.op("vector",
             lambda v, ec=ec: v.tensor_tensor_scan(
                 s_sb[:], a_sb[:], b_sb[:], carry_sb[:, ec:ec + 1],
                 ALU.mult, ALU.add),
             waits=[("sS", p.cnt("sS"))] + list(carry_ready), inc="vScn")
        p.op("vector",
             lambda v, ec=ec: v.tensor_copy(fin[:, ec:ec + 1],
                                            s_sb[:, TLOC - 1:TLOC]),
             waits=[("vScn", p.cnt("vScn"))])
        if update_x:
            p.op("vector",
                 lambda v: v.tensor_add(xt[0][:], xt[0][:], s_sb[:]),
                 waits=[("qA0", xrdy)], inc="vS3")
            upd_after.append(p.cnt("vS3"))
            p.dma("gpsimd", xc_ap[ec * P:(ec + 1) * P, :], xt[0][:],
                  waits=[("vS3", p.cnt("vS3"))], inc="qX0")
        else:
            p.op("vector", lambda v: v.engine_nop(), inc="vS3")
            upd_after.append(p.cnt("vS3"))
    done = ([("qX0", p.cnt("qX0"))] if update_x else []) +         [("vS3", p.cnt("vS3"))]
    return fin, done



def emit_l0_ffn(p: Prog, com: Common, xc_ap, xn_sb, xn_done, wts, ln_rows,
                tag="l0f", res_scale=0.5, gelu_tanh=False, mask_sb=None,
                mask_ready=(), extra_res=None, acc_ext=None, wb16=None,
                stash_rdy=0):
    T = TLOC
    if mask_sb is None:
        wg_sb = p.sbuf(f"{tag}_wg", [P, 8 * 8], F32)
        bgt_sb = p.sbuf(f"{tag}_bgt", [1, 256], BF16)
    if gelu_tanh:
        gtmp = p.sbuf(f"{tag}_gtmp", [P, 1024], F32)
        wb_sb = gtmp[:, 0:512].bitcast(BF16)
    b1sb = p.sbuf(f"{tag}_b1", [P, 8 * 32], F32)
    b2sb = p.sbuf(f"{tag}_b2", [P, 8 * 8], F32)
    acc = acc_ext if acc_ext is not None else p.sbuf(
        f"{tag}_acc", [P, NCH * 1024], F32)
    if gelu_tanh:
        cmb0 = gtmp[:, 512:1024]
        cmb = [cmb0, cmb0]
    gt = acc[:, 0:512]
    g32 = acc[:, 512:640]
    w_bf = acc[:, 640:768].bitcast(BF16)
    wT_sb = com.rowp[32:40, 512:2560].bitcast(BF16)
    w1t = [p.sbuf(f"{tag}_w1t{i}", [P, NCH * P], BF16) for i in range(2)]
    w2t = [p.sbuf(f"{tag}_w2t{i}", [P, 512], BF16) for i in range(2)]
    if not gelu_tanh:
        wb_sb = p.sbuf(f"{tag}_wb", [P, 1024], BF16)
    if not gelu_tanh:
        cmb0 = p.sbuf(f"{tag}_cmb0", [P, 512], F32)
        cmb = [cmb0, cmb0]
    xf0 = com.scr[0][:, 0:1024]
    xf = [xf0, xf0]
    pp = com.pp

    if mask_sb is None:
        p.dma("gpsimd", wg_sb[:], wts["Wg_l"], inc=f"qG{tag}")
        p.dma("gpsimd", bgt_sb[:], wts["bg_t"], inc=f"qG{tag}")
    p.dma("sync", b1sb[:], wts["b1l"], inc=f"qB{tag}")
    p.dma("sync", b2sb[:], wts["b2l"], inc=f"qB{tag}")
    small_rdy = p.cnt(f"qB{tag}")
    gate_w_rdy = p.cnt(f"qG{tag}")

    if mask_sb is None:
        mrow, rrow = ln_rows
        pl = pp[2]
        xg = com.scr[1]
        bgb_sb = acc[:, 768:1024]
        p.op("tensor", lambda t: t.matmul(
            pp[3][:, 0:256], com.ones_row[:], bgt_sb[:],
            start=True, stop=True),
            waits=[(f"qG{tag}", gate_w_rdy)] + list(xn_done) +
            list(com.ready), inc="tS")
        p.op("scalar", lambda s: s.copy(bgb_sb, pp[3][:, 0:256]),
             waits=[("tS", p.cnt("tS"))], inc="sS")
        bgb_done = p.cnt("sS")
        for tt in range(NT):
            ldg = p.dma("sync", xg[:], xc3(xc_ap, tt),
                        waits=[("tS", p.cnt("tS"))] + list(xn_done),
                        inc="qGX")
            sl = slice(tt * 512, (tt + 1) * 512)
            p.op("tensor", lambda t, sl=sl: t.matmul(
                pp[0][:], com.ones_rf[:], mrow[:, sl],
                start=True, stop=True),
                waits=[("vS4", p.cnt("vS4"))])
            p.op("tensor", lambda t, sl=sl: t.matmul(
                pp[1][:], com.ones_rf[:], rrow[:, sl],
                start=True, stop=True), inc="tS")
            mb3 = pp[0].ap().rearrange("p (o t) -> p o t", o=1)
            rb3 = pp[1].ap().rearrange("p (o t) -> p o t", o=1)

            def subp(v, mb3=mb3):
                a, b = broadcast_tensor_aps(sb3(xg), mb3)
                return v.tensor_tensor(sb3(xg), a, b, ALU.subtract)

            def mulp(v, rb3=rb3):
                a, b = broadcast_tensor_aps(sb3(xg), rb3)
                return v.tensor_tensor(sb3(xg), a, b, ALU.mult)
            p.op("vector", subp,
                 waits=[("tS", p.cnt("tS")), ("qGX", ldg)])
            p.op("vector", mulp, inc="vS4")
            for blk in range(4):
                for c in range(NCH):
                    p.op("tensor",
                         lambda t, tt=tt, blk=blk, c=c: t.matmul(
                             pl[:, (tt * 4 + blk) * 8:
                                (tt * 4 + blk) * 8 + 8],
                             sb3(xg)[:, c, blk * P:(blk + 1) * P],
                             wg_sb[:, c * 8:(c + 1) * 8],
                             start=(c == 0), stop=(c == NCH - 1)),
                         waits=[("vS4", p.cnt("vS4"))]
                         if blk == 0 and c == 0 else (),
                         inc="tS" if (blk == 3 and c == NCH - 1) else None)
        exps = gt[:, 0:256]
        probs = gt[:, 256:512]
        knock = gt[:, 0:256]
        wql = gt[:, 0:256]
        sums = g32[:, 0:32]
        rr = g32[:, 32:64]
        m1 = g32[:, 64:96]
        m2 = g32[:, 96:128]
        p.op("vector", lambda v: v.tensor_add(probs[:, :], pl[:, 0:256],
                                              bgb_sb),
             waits=[("tS", p.cnt("tS")), ("sS", bgb_done)], inc="vS4")
        p.op("scalar", lambda s: s.activation(exps[:, :], probs[:, :],
                                              AF.Exp),
             waits=[("vS4", p.cnt("vS4"))], inc="sS")
        gate_evac = p.cnt("sS")

        def r3(ap2):
            return ap2.rearrange("p (g f) -> p g f", g=32)

        def bc3(ap2):
            v = ap2.rearrange("p (g f) -> p g f", f=1)
            return v

        def tt_b(v, out, a, b, op):
            ab, bb = broadcast_tensor_aps(r3(a), bc3(b))
            return v.tensor_tensor(r3(out), ab, bb, op)

        p.op("vector", lambda v: v.reduce_sum(sums[:, :], r3(exps),
                                              axis=AX.X),
             waits=[("sS", p.cnt("sS"))])
        p.op("vector", lambda v: v.reciprocal(rr[:, :], sums[:, :]))
        p.op("vector", lambda v: tt_b(v, probs, exps, rr, ALU.mult))
        p.op("vector", lambda v: v.reduce_max(m1[:, :], r3(probs),
                                              axis=AX.X))
        p.op("vector", lambda v: tt_b(v, knock, probs, m1, ALU.is_ge))
        p.op("vector", lambda v: v.scalar_tensor_tensor(
            knock[:, :], knock[:, :], -2.0, probs[:, :], ALU.mult, ALU.add))
        p.op("vector", lambda v: v.reduce_max(m2[:, :], r3(knock),
                                              axis=AX.X))
        p.op("vector", lambda v: tt_b(v, wql, probs, m2, ALU.is_ge))
        p.op("vector", lambda v: v.tensor_mul(wql[:, :], wql[:, :],
                                              probs[:, :]))
        p.op("vector", lambda v: v.tensor_copy(w_bf[:, :], wql[:, :]),
             inc="vS4")
        for grp in range(8):
            for j in range(4):
                blk = grp * 4 + j
                p.op("tensor",
                     lambda t, blk=blk, j=j: t.matmul(
                         pp[3][0:8, j * P:(j + 1) * P],
                         w_bf[:, blk * 8:(blk + 1) * 8], com.ident[:],
                         start=True, stop=True),
                     waits=([("vS4", p.cnt("vS4"))] if grp == 0 and j == 0
                            else ([("sS", p.cnt("sS"))] if j == 0 else ())),
                     inc="tS" if j == 3 else None)
            p.op("scalar",
                 lambda s, grp=grp: s.copy(
                     wT_sb[0:8, grp * 512:(grp + 1) * 512], pp[3][0:8, :]),
                 waits=[("tS", p.cnt("tS"))], inc="sS")
        wT_rdy_s = p.cnt("sS")
        wT_src = wT_sb
    else:
        gate_evac = p.cnt("sS")
        wT_rdy_s = p.cnt("sS")
        wT_src = mask_sb

    h_bf = [sc.bitcast(BF16) for sc in com.scr[0:4]]

    def h_ap(hc, th):
        return h_bf[hc // 8][:, (hc % 8) * 1024 + th * 512:
                             (hc % 8) * 1024 + (th + 1) * 512]

    p.acc_dbg = acc
    use_hist = {"F0": 0, "F1": 0, "G0": 0, "G1": 0}
    for tg in range(NTG):
        p.op("gpsimd", lambda g: g.memset(acc[:, :], 0.0),
             waits=[("vX", p.cnt("vX")), ("sS", wT_rdy_s),
                    ("vS4", p.cnt("vS4")), ("qDB", p.cnt("qDB"))] +
             list(mask_ready), inc="pM")
        memset_cnt = p.cnt("pM")
        for e in range(8):
            cmb_prev = p.cnt("vC")
            sS_hist = {0: None, 1: None}
            for hc in range(32):
                hb = hc % 2
                if wb16 is None:
                    ldw = p.dma(
                        "gpsimd",
                        w1t[hb].rearrange("p (c k) -> p c k", c=NCH),
                        wts["W1"][e][:, hc * P:(hc + 1) * P].rearrange(
                            "(c p) k -> p c k", p=P),
                        waits=[("tS", use_hist[f"F{hb}"])]
                        if (tg, e, hc) > (0, 0, 1) else [],
                        inc=f"qF{hb}")
                    qf = f"qF{hb}"
                else:
                    ldw = p.dma(
                        "sync",
                        w1t[hb].rearrange("p (c k) -> p c k", c=NCH),
                        wb16[0][e][:, hc * P:(hc + 1) * P].rearrange(
                            "(c p) k -> p c k", p=P),
                        waits=[("tS", use_hist[f"F{hb}"]),
                               ("qStash", stash_rdy)]
                        if (tg, e, hc) > (0, 0, 1) else
                        [("qStash", stash_rdy)],
                        inc=f"qE{hb}")
                    qf = f"qE{hb}"
                for c in range(NCH):
                    for th in range(2):
                        w0 = []
                        if th == 0 and c == 0:
                            w0 = [(qf, ldw), (list(xn_done)[0])]
                            if sS_hist[hb] is not None:
                                w0.append(("sS", sS_hist[hb]))
                            if hc < 2:
                                w0 += [("vC", cmb_prev), ("sS", wT_rdy_s),
                                       ("vX", p.cnt("vX")),
                                       ("qXF0", p.cnt("qXF0")),
                                       ("sS", p.cnt("sS"))]
                        p.op("tensor",
                             lambda t, hc=hc, th=th, c=c, hb=hb, tg=tg:
                             t.matmul(
                                 pp[hb * 2 + th][:],
                                 w1t[hb][:, c * P:(c + 1) * P],
                                 xn_sb[:, c * T + tg * 1024 + th * 512:
                                       c * T + tg * 1024 + (th + 1) * 512],
                                 start=(c == 0), stop=(c == NCH - 1)),
                             waits=w0,
                             inc="tS" if (th == 1 and c == NCH - 1)
                             else None)
                use_hist[f"F{hb}"] = p.cnt("tS")
                for th in range(2):
                    if not gelu_tanh:
                        p.op("scalar",
                             lambda s, hc=hc, th=th, hb=hb, e=e:
                             s.activation(
                                 h_ap(hc, th), pp[hb * 2 + th][:],
                                 AF.Gelu, bias=b1sb[:, e * 32 + hc:
                                                   e * 32 + hc + 1]),
                             waits=([("tS", p.cnt("tS")),
                                     (f"qB{tag}", small_rdy)] if th == 0
                                    else ()),
                             inc="sS" if th == 1 else None)
                    else:
                        xb_ = gtmp[:, 0:512]
                        t2_ = cmb[0]
                        C0 = 0.7978845608028654
                        C1 = 0.044715
                        p.op("scalar",
                             lambda s, hc=hc, th=th, hb=hb, e=e, xb_=xb_:
                             s.activation(
                                 xb_, pp[hb * 2 + th][:],
                                 AF.Identity, bias=b1sb[:, e * 32 + hc:
                                                       e * 32 + hc + 1]),
                             waits=[("tS", p.cnt("tS")),
                                    (f"qB{tag}", small_rdy),
                                    ("vH", p.cnt("vH")),
                                    ("pA", p.cnt("pA"))],
                             inc="sS")
                        p.op("gpsimd",
                             lambda g, xb_=xb_, t2_=t2_: g.tensor_mul(
                                 t2_[:], xb_, xb_),
                             waits=[("sS", p.cnt("sS")),
                                    ("pA", p.cnt("pA"))], inc="pG")
                        p.op("vector",
                             lambda v, t2_=t2_: v.tensor_scalar(
                                 t2_[:], t2_[:], C1, 1.0, ALU.mult,
                                 ALU.add),
                             waits=[("pG", p.cnt("pG"))])
                        p.op("vector",
                             lambda v, xb_=xb_, t2_=t2_: v.tensor_mul(
                                 t2_[:], t2_[:], xb_), inc="vH")
                        p.op("scalar",
                             lambda s, t2_=t2_: s.activation(
                                 t2_[:], t2_[:], AF.Tanh, scale=C0),
                             waits=[("vH", p.cnt("vH"))], inc="sS")
                        p.op("vector",
                             lambda v, t2_=t2_: v.tensor_scalar(
                                 t2_[:], t2_[:], 0.5, 0.5, ALU.mult,
                                 ALU.add),
                             waits=[("sS", p.cnt("sS"))], inc="vG2")
                        p.op("gpsimd",
                             lambda g, hc=hc, th=th, xb_=xb_, t2_=t2_:
                             g.tensor_mul(h_ap(hc, th), t2_[:], xb_),
                             waits=[("vG2", p.cnt("vG2"))], inc="vH")
                sS_hist[hb] = p.cnt("sS")
            h_done = p.cnt("sS")
            for th in range(2):
                p.op("tensor",
                     lambda t, th=th, e=e, tg=tg: t.matmul(
                         pp[4][:], com.rowsel[:, e * P:(e + 1) * P],
                         wT_src[0:8, tg * 1024 + th * 512:
                                tg * 1024 + (th + 1) * 512],
                         start=True, stop=True),
                     waits=[("sS", p.cnt("sS")), ("vC", cmb_prev),
                            ("vH", p.cnt("vH"))] +
                     (list(mask_ready) if mask_sb is not None else [])
                     if th == 0 else [("sS", p.cnt("sS"))],
                     inc="tS")
                p.op("scalar",
                     lambda s, th=th: s.copy(
                         wb_sb[:, th * 512:(th + 1) * 512], pp[4][:]),
                     waits=[("tS", p.cnt("tS"))], inc="sS")
            wb_done = p.cnt("sS")
            for pas in range(2):
                for hc in range(32):
                    gb = hc % 2
                    if wb16 is None:
                        ldw2 = p.dma(
                            "gpsimd", w2t[gb][:],
                            wts["W2"][e][hc * P:(hc + 1) * P,
                                         pas * 512:(pas + 1) * 512],
                            waits=[("tS", use_hist[f"G{gb}"])]
                            if (tg, e, pas, hc) > (0, 0, 0, 1) else [],
                            inc=f"qG{gb}")
                        qn = f"qG{gb}"
                    else:
                        ldw2 = p.dma(
                            "sync", w2t[gb][:],
                            wb16[1][e][hc * P:(hc + 1) * P,
                                       pas * 512:(pas + 1) * 512],
                            waits=[("tS", use_hist[f"G{gb}"]),
                                   ("qStash", stash_rdy)]
                            if (tg, e, pas, hc) > (0, 0, 0, 1) else
                            [("qStash", stash_rdy)],
                            inc=f"qH{gb}")
                        qn = f"qH{gb}"
                    for dq in range(4):
                        for th in range(2):
                            w0 = []
                            if dq == 0 and th == 0:
                                w0 = [(qn, ldw2)]
                                if hc == 0:
                                    w0 += [("sS", wb_done),
                                           ("vC", p.cnt("vC")),
                                           ("vH", p.cnt("vH"))]
                            p.op("tensor",
                                 lambda t, hc=hc, dq=dq, th=th, gb=gb:
                                 t.matmul(
                                     pp[dq * 2 + th][:],
                                     w2t[gb][:, dq * P:(dq + 1) * P],
                                     h_ap(hc, th),
                                     start=(hc == 0), stop=(hc == 31)),
                                 waits=w0,
                                 inc="tS" if (dq == 3 and th == 1)
                                 else None)
                    use_hist[f"G{gb}"] = p.cnt("tS")
                mm_done = p.cnt("tS")
                for dq in range(4):
                    dc = pas * 4 + dq
                    for th in range(2):
                        cb = (dq * 2 + th) % 2
                        p.op("vector",
                             lambda v, dq=dq, th=th, cb=cb, e=e, dc=dc:
                             v.scalar_tensor_tensor(
                                 cmb[cb][:], pp[dq * 2 + th][:],
                                 b2sb[:, e * 8 + dc:e * 8 + dc + 1],
                                 wb_sb[:, th * 512:(th + 1) * 512],
                                 ALU.add, ALU.mult),
                             waits=[("tS", mm_done),
                                    ("pA", p.cnt("pA")),
                                    (f"qB{tag}", small_rdy)],
                             inc="vC")
                        p.op("gpsimd",
                             lambda g, dc=dc, th=th, cb=cb, tg=tg:
                             g.tensor_add(
                                 acc[:, dc * 1024 + th * 512:
                                     dc * 1024 + (th + 1) * 512],
                                 acc[:, dc * 1024 + th * 512:
                                     dc * 1024 + (th + 1) * 512],
                                 cmb[cb][:]),
                             waits=[("vC", p.cnt("vC")),
                                    ("pM", memset_cnt)], inc="pA")
        pool_done = p.cnt("pA")
        if getattr(p, "dbg_acc0", None) is not None and tg == 0:
            p.dma("gpsimd", p.dbg_acc0, acc[:],
                  waits=[("pA", pool_done)], inc="qDB")
        for c in range(NCH):
            xb = c % 2
            xb = 0
            ldx = p.dma("sync", xf[xb][:],
                        xc_ap[c * P:(c + 1) * P, tg * 1024:(tg + 1) * 1024],
                        waits=[("qXF0", p.cnt("qXF0")), ("pA", pool_done)],
                        inc="qLX0")
            if extra_res is None:
                p.op("vector",
                     lambda v, c=c, xb=xb, tg=tg: v.scalar_tensor_tensor(
                         xf[xb][:], acc[:, c * 1024:(c + 1) * 1024],
                         res_scale, xf[xb][:], ALU.mult, ALU.add),
                     waits=[("qLX0", ldx), ("pA", pool_done)],
                     inc="vX")
            else:
                mrow, rrow = extra_res
                if c == 0:
                    p.op("tensor",
                         lambda t, tg=tg: t.matmul(
                             pp[0][:], com.ones_rf[:],
                             mrow[:, tg * 1024:tg * 1024 + 512],
                             start=True, stop=True),
                         waits=[("pA", pool_done), ("vC", p.cnt("vC"))])
                    p.op("tensor",
                         lambda t, tg=tg: t.matmul(
                             pp[1][:], com.ones_rf[:],
                             mrow[:, tg * 1024 + 512:(tg + 1) * 1024],
                             start=True, stop=True))
                    p.op("tensor",
                         lambda t, tg=tg: t.matmul(
                             pp[2][:], com.ones_rf[:],
                             rrow[:, tg * 1024:tg * 1024 + 512],
                             start=True, stop=True))
                    p.op("tensor",
                         lambda t, tg=tg: t.matmul(
                             pp[3][:], com.ones_rf[:],
                             rrow[:, tg * 1024 + 512:(tg + 1) * 1024],
                             start=True, stop=True), inc="tS")
                for th in range(2):
                    p.op("vector",
                         lambda v, xb=xb, th=th: v.tensor_sub(
                             cmb[th][:],
                             xf[xb][:, th * 512:(th + 1) * 512],
                             pp[th][:]),
                         waits=[("tS", p.cnt("tS")), ("qLX0", ldx)])
                    p.op("vector",
                         lambda v, th=th: v.tensor_mul(
                             cmb[th][:], cmb[th][:], pp[2 + th][:]))
                    p.op("vector",
                         lambda v, c=c, th=th: v.tensor_add(
                             cmb[th][:], cmb[th][:],
                             acc[:, c * 1024 + th * 512:
                                 c * 1024 + (th + 1) * 512]))
                    p.op("vector",
                         lambda v, xb=xb, th=th: v.scalar_tensor_tensor(
                             xf[xb][:, th * 512:(th + 1) * 512],
                             cmb[th][:], res_scale,
                             xf[xb][:, th * 512:(th + 1) * 512],
                             ALU.mult, ALU.add),
                         inc="vX")
            p.dma("gpsimd",
                  xc_ap[c * P:(c + 1) * P, tg * 1024:(tg + 1) * 1024],
                  xf[xb][:], waits=[("vX", p.cnt("vX"))], inc="qXF0")
        vX_after_tg = p.cnt("vX")
    done = [("qXF0", p.cnt("qXF0"))]
    return done



def emit_l1_scores(p: Prog, com: Common, xn_sb, xn_done, centT_l_ap,
                   scT_sb, tag="l1s"):
    ct_sb = p.sbuf(f"{tag}_ct", [P, 64], BF16)
    p.dma("gpsimd", ct_sb[:], centT_l_ap, inc=f"qB{tag}")
    crdy = p.cnt(f"qB{tag}")
    pb = com.pp[2]
    for tt in range(NT):
        for c in range(NCH):
            p.op("tensor",
                 lambda t, tt=tt, c=c: t.matmul(
                     pb[0:8, :], ct_sb[:, c * 8:(c + 1) * 8],
                     xn_sb[:, c * TLOC + tt * 512:c * TLOC + (tt + 1) * 512],
                     start=(c == 0), stop=(c == NCH - 1)),
                 waits=(list(xn_done) + [(f"qB{tag}", crdy)]
                        + ([("sS", p.cnt("sS"))] if tt > 0 else []))
                 if c == 0 else (),
                 inc="tS" if c == NCH - 1 else None)
        p.op("scalar",
             lambda sc, tt=tt: sc.copy(scT_sb[0:8, tt * 512:(tt + 1) * 512],
                                       pb[0:8, :]),
             waits=[("tS", p.cnt("tS"))], inc="sS")
    return [("sS", p.cnt("sS"))]


def emit_l1_route(p: Prog, com: Common, gath_sb, gath_ready, scT_sb,
                  scT_ready, mask_sb, acc, tag="l1r", cap=4096, iters=34):
    pp = com.pp
    dump = acc[:, 1024:2048].bitcast(BF16)
    bis = acc[0:8, 2048:2112]
    lo = bis[:, 0:1]
    hi = bis[:, 1:2]
    mid = bis[:, 2:3]
    cnt8 = bis[:, 3:4]
    ge8 = bis[:, 4:5]
    t8 = bis[:, 5:6]
    thv = bis[:, 6:7]
    am8 = bis[:, 7:8]
    rowc = acc[:, 2112:2113]
    midb = acc[:, 2113:2114]
    arow = acc[0:1, 2120:2248]
    m_bf = acc[32:40, 2304:4352].bitcast(BF16)
    cum_bf = acc[32:40, 4352:6400].bitcast(BF16)
    gbsel = acc[:, 7424:7936].bitcast(BF16)
    p.dma("sync", gbsel, com.cb[0:128, 1152:2176], inc=f"qB{tag}")
    gb_rdy = p.cnt(f"qB{tag}")

    p.op("vector", lambda v: v.memset(thv[:], 0.0))
    p.op("vector", lambda v: v.reduce_max(rowc, gath_sb, axis=AX.X,
                                          apply_absolute_value=True),
         waits=list(gath_ready), inc="vR")
    p.op("tensor", lambda t: t.transpose(pp[3][0:1, 0:128], rowc,
                                         com.identf[:]),
         waits=[("vR", p.cnt("vR"))], inc="tS")
    p.op("scalar", lambda sc: sc.copy(arow, pp[3][0:1, 0:128]),
         waits=[("tS", p.cnt("tS"))], inc="sS")
    p.op("vector", lambda v: v.reduce_max(acc[0:1, 2248:2249], arow,
                                          axis=AX.X),
         waits=[("sS", p.cnt("sS"))])
    p.op("vector", lambda v: v.engine_nop(), inc="vR")
    p.op("tensor", lambda t: t.matmul(
        pp[4][0:8, 0:1], com.ones_rf[0:1, 0:8], acc[0:1, 2248:2249],
        start=True, stop=True), waits=[("vR", p.cnt("vR"))], inc="tS")
    p.op("scalar", lambda sc: sc.activation(
        am8[:], pp[4][0:8, 0:1], AF.Identity, scale=1.001),
        waits=[("tS", p.cnt("tS"))], inc="sS")
    amax_rdy_s = p.cnt("sS")

    for e in range(8):
        p.op("vector", lambda v: v.tensor_scalar_mul(lo[:], am8[:], -1.0),
             waits=[("sS", amax_rdy_s)])
        p.op("vector", lambda v: v.tensor_scalar_mul(hi[:], am8[:], 1.0))
        p.op("vector", lambda v: v.memset(mid[:], 0.0))
        p.op("vector", lambda v: v.memset(midb[:], 0.0), inc="vR")
        for it in range(iters):
            p.op("vector",
                 lambda v: v.tensor_scalar(dump[:, :], gath_sb[:, :],
                                           midb, 0.0, ALU.is_gt,
                                           op1=ALU.add, accum_out=rowc),
                 waits=[("vR", p.cnt("vR")), ("sS", p.cnt("sS"))],
                 inc="vR")
            p.op("tensor", lambda t: t.matmul(
                pp[4][0:8, 0:1], com.gself, rowc, start=True, stop=True),
                waits=[("vR", p.cnt("vR"))], inc="tS")
            p.op("vector", lambda v: v.tensor_copy(cnt8[:], pp[4][0:8, 0:1]),
                 waits=[("tS", p.cnt("tS"))], inc="sS")
            p.op("vector", lambda v: v.tensor_single_scalar(
                ge8[:], cnt8[:], float(cap), ALU.is_ge),
                waits=[("sS", p.cnt("sS"))])
            p.op("vector", lambda v: v.tensor_sub(t8[:], mid[:], lo[:]))
            p.op("vector", lambda v: v.tensor_mul(t8[:], t8[:], ge8[:]))
            p.op("vector", lambda v: v.tensor_add(lo[:], lo[:], t8[:]))
            p.op("vector", lambda v: v.tensor_sub(t8[:], hi[:], mid[:]))
            p.op("vector", lambda v: v.tensor_mul(t8[:], t8[:], ge8[:]))
            p.op("vector", lambda v: v.tensor_add(hi[:], mid[:], t8[:]))
            p.op("vector", lambda v: v.scalar_tensor_tensor(
                mid[:], lo[:], 0.5, hi[:], ALU.mult, ALU.bypass)
                if False else v.tensor_add(mid[:], lo[:], hi[:]))
            p.op("vector", lambda v: v.tensor_scalar_mul(mid[:], mid[:],
                                                         0.5), inc="vR")
            p.op("tensor", lambda t: t.matmul(
                pp[4][:, 0:1], com.gselTf, mid[:], start=True, stop=True),
                waits=[("vR", p.cnt("vR"))], inc="tS")
            p.op("vector", lambda v: v.tensor_copy(midb, pp[4][:, 0:1]),
                 waits=[("tS", p.cnt("tS"))], inc="sS")
        p.op("vector", lambda v, e=e: v.tensor_mul(
            t8[:], lo[:], com.oh8[:, e:e + 1]),
            waits=[("sS", p.cnt("sS"))])
        p.op("vector", lambda v: v.tensor_add(thv[:], thv[:], t8[:])
             if True else None)
        p.op("tensor", lambda t: t.matmul(
            pp[4][:, 0:1], com.gselTf, lo[:], start=True, stop=True),
            waits=[("sS", p.cnt("sS")), ("vR", p.cnt("vR"))], inc="tS")
        p.op("scalar", lambda sc: sc.copy(midb, pp[4][:, 0:1]),
             waits=[("tS", p.cnt("tS"))], inc="sS")
        p.op("vector",
             lambda v: v.tensor_scalar(dump[:, :], gath_sb[:, :], midb,
                                       None, ALU.is_gt),
             waits=[("sS", p.cnt("sS"))], inc="vR")
        for q in range(4):
            p.op("tensor",
                 lambda t, q=q, e=e: t.matmul(
                     pp[5][:, :], gbsel[:, e * P:(e + 1) * P],
                     dump[:, q * 512:(q + 1) * 512],
                     start=True, stop=True),
                 waits=[("vR", p.cnt("vR")), (f"qB{tag}", gb_rdy)] + (
                     [("vR2", p.cnt("vR2"))] if (e, q) > (0, 0) else []),
                 inc="tS")
            p.op("vector",
                 lambda v, q=q: v.scalar_tensor_tensor(
                     gath_sb[:, q * 512:(q + 1) * 512], pp[5][:, :],
                     -1e30, gath_sb[:, q * 512:(q + 1) * 512],
                     ALU.mult, ALU.add),
                 waits=[("tS", p.cnt("tS"))], inc="vR2")
    p.op("vector", lambda v: v.tensor_scalar(
        m_bf[:, :], scT_sb[:, :], thv, None, ALU.is_gt),
        waits=list(scT_ready) + [("vR2", p.cnt("vR2"))], inc="vR")
    for q in range(NT):
        p.op("tensor", lambda t, q=q: t.matmul(
            pp[6][0:8, :], com.lowT[:], m_bf[:, q * 512:(q + 1) * 512],
            start=True, stop=True),
            waits=[("vR", p.cnt("vR"))] + (
                [("sS", p.cnt("sS"))] if q > 0 else []),
            inc="tS")
        p.op("scalar", lambda sc, q=q: sc.copy(
            cum_bf[:, q * 512:(q + 1) * 512], pp[6][0:8, :]),
            waits=[("tS", p.cnt("tS"))], inc="sS")
    p.op("vector", lambda v: v.tensor_single_scalar(
        cum_bf[:, :], cum_bf[:, :], 0.5, ALU.is_lt),
        waits=[("sS", p.cnt("sS"))])
    p.op("vector", lambda v: v.tensor_mul(mask_sb[:, :], m_bf[:, :],
                                          cum_bf[:, :]), inc="vR")
    if getattr(p, "dbg_bis", None) is not None:
        p.dma("gpsimd", p.dbg_bis, acc[0:8, 2048:2112],
              waits=[("vR", p.cnt("vR"))], inc="qDB")
    return [("vR", p.cnt("vR"))]



def emit_l1_attn(p: Prog, com: Common, xc_ap, xn_sb, xn_done, wts, sin_sb,
                 sin_ready, tag="l1a", update_x=True, do_tanh=True):
    C = P
    NCHK = (TLOC // C) if ATTN_NCHK is None else ATTN_NCHK
    HG = 4
    NG = (16 // HG) if ATTN_NG is None else ATTN_NG

    wt_sb = p.sbuf(f"{tag}_wt", [P, NCH * 1024], BF16)
    sca_sb = p.sbuf(f"{tag}_sca", [1, 16], F32)
    tri_i = p.sbuf(f"{tag}_tri", [P, P], BF16)
    tri_s = p.sbuf(f"{tag}_trs", [P, P], BF16)
    tri_sT = p.sbuf(f"{tag}_trsT", [P, P], BF16)
    grow = p.sbuf(f"{tag}_gr", [1, 1024], F32)
    brow = p.sbuf(f"{tag}_br", [1, 1024], F32)
    gb_sb = p.sbuf(f"{tag}_gb", [P, 1024], F32)
    bb_sb = p.sbuf(f"{tag}_bb", [P, 1024], F32)
    mst = p.sbuf(f"{tag}_mst", [P, P], BF16)
    mstT = p.sbuf(f"{tag}_mstT", [P, P], BF16)
    oim = p.sbuf(f"{tag}_oim", [P, P], BF16)
    etmp = p.sbuf(f"{tag}_etmp", [P, P], F32)
    cols = p.sbuf(f"{tag}_cols", [P, 8], F32)
    lamvec = cols[:, 0:1]
    mblam = cols[:, 1:2]
    lamrev = cols[:, 2:3]
    betac = cols[:, 3:4]
    lnlamc = cols[:, 4:5]
    sc8 = p.sbuf(f"{tag}_sc8", [1, 16], F32)
    lam1 = sc8[:, 0:1]
    beta1 = sc8[:, 1:2]
    lnlam1 = sc8[:, 2:3]
    lamC1 = sc8[:, 3:4]
    one1 = sc8[:, 4:5]
    lamC64 = p.sbuf(f"{tag}_lc64", [64, 1], F32)
    s_all = p.sbuf(f"{tag}_sall", [64, 16 * 64], F32)
    s_bf = p.sbuf(f"{tag}_sbf", [64, 16 * 64], BF16)

    vraw = com.scr[1][:, 0:1024]
    vb = com.scr[1][:, 1024:2048]
    y_sb = com.scr[1][:, 2048:3072]
    xfrag = com.scr[1][:, 3072:4096]
    zf = com.scr[2][:, 0:256]
    yf = com.scr[2][:, 256:512]
    osb = com.scr[2][:, 512:768]
    usb = com.scr[2][:, 768:1024]
    lnm = com.scr[2][:, 1024:1025]
    lnr = com.scr[2][:, 1025:1026]
    lns = com.scr[2][:, 1026:1027]
    ysq = com.scr[2][:, 2048:3072]
    s0b = com.scr[0].bitcast(BF16)
    nst = [s0b[:, k * 512:(k + 1) * 512] for k in range(7)]
    a4 = s0b[:, 3584:4096]
    at4 = s0b[:, 4096:4608]
    oit4 = s0b[:, 4608:5120]
    mcur = s0b[:, 5120:5632]
    kstage = s0b[0:64, 5632:6144]
    ubf = s0b[:, 6144:6400]
    zbf = s0b[:, 6400:6656]
    k2bf = s0b[:, 6656:6912]
    pp = com.pp

    p.dma("gpsimd", wt_sb.rearrange("p (c k) -> p c k", c=NCH),
          wts["WT"].rearrange("(c p) k -> p c k", p=P), inc=f"qW{tag}")
    p.dma("sync", sca_sb[:, 0:2], wts["sca"], inc=f"qB{tag}")
    p.dma("sync", tri_i[:], com.cb[0:128, 2304:2432], inc=f"qB{tag}")
    p.dma("sync", tri_s[:], com.cb[0:128, 2432:2560], inc=f"qB{tag}")
    p.dma("sync", tri_sT[:], com.cb[0:128, 2560:2688], inc=f"qB{tag}")
    p.dma("sync", grow[:], wts["grow"], inc=f"qB{tag}")
    p.dma("sync", brow[:], wts["brow"], inc=f"qB{tag}")
    ld_rdy = p.cnt(f"qB{tag}")
    ldw_rdy = p.cnt(f"qW{tag}")

    p.op("scalar", lambda sc: sc.activation(lam1, sca_sb[:, 1:2],
                                            AF.Sigmoid),
         waits=[(f"qB{tag}", ld_rdy)], inc="sS")
    p.op("vector", lambda v: v.memset(one1, 1.0),
         waits=[("sS", p.cnt("sS"))])
    p.op("vector", lambda v: v.tensor_sub(beta1, one1, lam1))
    p.op("vector", lambda v: v.tensor_mul(beta1, beta1, sca_sb[:, 0:1]))
    p.op("vector", lambda v: v.tensor_scalar_mul(beta1, beta1, 1.0 / 64),
         inc="vS5")
    p.op("scalar", lambda sc: sc.activation(lnlam1, lam1, AF.Ln),
         waits=[("vS5", p.cnt("vS5"))], inc="sS")
    p.op("tensor", lambda t: t.matmul(pp[0][0:128, 0:2],
                                      com.ones_rf[:], sc8[:, 1:3],
                                      start=True, stop=True),
         waits=[("sS", p.cnt("sS"))], inc="tS")
    p.op("scalar", lambda sc: sc.copy(cols[:, 5:7], pp[0][0:128, 0:2]),
         waits=[("tS", p.cnt("tS"))], inc="sS")
    p.op("vector", lambda v: v.tensor_copy(betac, cols[:, 5:6]),
         waits=[("sS", p.cnt("sS"))])
    p.op("vector", lambda v: v.tensor_copy(lnlamc, cols[:, 6:7]),
         inc="vS5")
    p.op("scalar", lambda sc: sc.activation(lamvec, com.idx1, AF.Exp,
                                            scale=lnlamc),
         waits=[("vS5", p.cnt("vS5"))])
    p.op("scalar", lambda sc: sc.activation(lamrev, com.revidx, AF.Exp,
                                            scale=lnlamc))
    p.op("scalar", lambda sc: sc.activation(cols[:, 7:8], com.c128, AF.Exp,
                                            scale=lnlamc), inc="sS")
    p.op("vector", lambda v: v.tensor_mul(mblam, cols[:, 5:6], lamvec),
         waits=[("sS", p.cnt("sS"))])
    p.op("vector", lambda v: v.tensor_scalar_mul(mblam, mblam, -1.0))
    p.op("vector", lambda v: v.tensor_copy(lamC64[:], cols[0:64, 7:8]),
         inc="vS5")
    p.op("scalar", lambda sc: sc.activation(etmp[:], com.idxdl,
                                            AF.Exp, scale=lnlamc),
         waits=[("vS5", p.cnt("vS5")), (f"qB{tag}", ld_rdy)], inc="sS")
    p.op("vector", lambda v: v.tensor_mul(mst[:], etmp[:], tri_s[:]),
         waits=[("sS", p.cnt("sS"))])
    p.op("vector", lambda v: v.tensor_scalar(mst[:], mst[:], betac, None,
                                             ALU.mult))
    p.op("vector", lambda v: v.tensor_mul(oim[:], etmp[:], tri_i[:]),
         inc="vS5")
    p.op("scalar", lambda sc: sc.activation(etmp[:], com.idxdlt,
                                            AF.Exp, scale=lnlamc),
         waits=[("vS5", p.cnt("vS5"))], inc="sS")
    p.op("vector", lambda v: v.tensor_mul(mstT[:], etmp[:], tri_sT[:]),
         waits=[("sS", p.cnt("sS"))])
    p.op("vector", lambda v: v.tensor_scalar(mstT[:], mstT[:], betac, None,
                                             ALU.mult), inc="vS5")
    for half in range(2):
        p.op("tensor", lambda t, half=half: t.matmul(
            pp[2][:, :], com.ones_rf[:], grow[:, half * 512:(half + 1) * 512],
            start=True, stop=True),
            waits=[("vS5", p.cnt("vS5")), ("sS", p.cnt("sS"))], inc="tS")
        p.op("scalar", lambda sc, half=half: sc.copy(
            gb_sb[:, half * 512:(half + 1) * 512], pp[2][:, :]),
            waits=[("tS", p.cnt("tS"))], inc="sS")
        p.op("tensor", lambda t, half=half: t.matmul(
            pp[2][:, :], com.ones_rf[:], brow[:, half * 512:(half + 1) * 512],
            start=True, stop=True),
            waits=[("sS", p.cnt("sS"))], inc="tS")
        p.op("scalar", lambda sc, half=half: sc.copy(
            bb_sb[:, half * 512:(half + 1) * 512], pp[2][:, :]),
            waits=[("tS", p.cnt("tS"))], inc="sS")
    sin_ap = sin_sb[:, :] if hasattr(sin_sb, "ap") else sin_sb
    p.op("vector", lambda v: v.tensor_copy(s_all[:, :], sin_ap),
         waits=list(sin_ready))
    p.op("vector", lambda v: v.tensor_copy(s_bf[:, :], sin_ap),
         inc="vS5")
    pre_done_v = p.cnt("vS5")
    pre_done_s = p.cnt("sS")

    for c in (range(NCH) if do_tanh else []):
        p.op("scalar",
             lambda sc, c=c: sc.activation(
                 xn_sb[:, c * TLOC:(c + 1) * TLOC],
                 xn_sb[:, c * TLOC:(c + 1) * TLOC], AF.Tanh),
             waits=list(xn_done) if c == 0 else (),
             inc="sS" if c == NCH - 1 else None)
    xa_done = p.cnt("sS")

    def kT(h, ck):
        c = h // 2
        b = 64 * (h % 2)
        return xn_sb[b:b + 64,
                     c * TLOC + ck * C:c * TLOC + (ck + 1) * C]

    for ck in range(NCHK):
        for half in range(2):
            for c in range(NCH):
                p.op("tensor",
                     lambda t, ck=ck, half=half, c=c: t.matmul(
                         pp[half][:, :],
                         xn_sb[:, c * TLOC + ck * C:c * TLOC + (ck + 1) * C],
                         wt_sb[:, c * 1024 + half * 512:
                               c * 1024 + (half + 1) * 512],
                         start=(c == 0), stop=(c == NCH - 1)),
                     waits=[("sS", p.cnt("sS")), ("vS5", pre_done_v),
                            (f"qW{tag}", ldw_rdy)]
                     if (half == 0 and c == 0) else (),
                     inc="tS" if (half == 1 and c == NCH - 1) else None)
        p.op("scalar", lambda sc: sc.copy(vraw[:, 0:512], pp[0][:, :]),
             waits=[("tS", p.cnt("tS")), ("vO", p.cnt("vO")),
                    ("vG", p.cnt("vG")), ("vY", p.cnt("vY"))])
        p.op("scalar", lambda sc: sc.copy(vraw[:, 512:1024], pp[1][:, :]))
        p.op("scalar", lambda sc: sc.activation(vb[:, 0:512], pp[0][:, :],
                                                AF.Identity, scale=betac))
        p.op("scalar", lambda sc: sc.activation(vb[:, 512:1024], pp[1][:, :],
                                                AF.Identity, scale=betac),
             inc="sS")
        v_done = p.cnt("sS")
        for g in range(NG):
            heads = range(g * HG, (g + 1) * HG)
            for j, h in enumerate(heads):
                p.op("vector",
                     lambda v, h=h, ck=ck, j=j: v.tensor_copy(
                         kstage[:, j * 128:(j + 1) * 128], kT(h, ck)),
                     waits=[("sS", v_done), ("tS", p.cnt("tS"))]
                     if j == 0 else (),
                     inc="vK" if j == HG - 1 else None)
            kst = p.cnt("vK")
            for j, h in (list(enumerate(heads)) if ATTN_GRAM else []):
                p.op("tensor",
                     lambda t, j=j: t.matmul(
                         pp[2][:, j * 128:(j + 1) * 128],
                         kstage[:, j * 128:(j + 1) * 128],
                         kstage[:, j * 128:(j + 1) * 128],
                         start=True, stop=True),
                     waits=[("vK", kst), ("vG", p.cnt("vG")),
                            ("vY", p.cnt("vY"))]
                     if j == 0 else ())
            for j, h in (list(enumerate(heads)) if ATTN_GRAM else []):
                p.op("tensor",
                     lambda t, j=j, h=h: t.matmul(
                         pp[5][:, j * 64:(j + 1) * 64],
                         kstage[:, j * 128:(j + 1) * 128],
                         s_bf[:, h * 64:(h + 1) * 64],
                         start=True, stop=True),
                     waits=[("vS5", p.cnt("vS5")), ("vO", p.cnt("vO"))]
                     if j == 0 else (),
                     inc="tS" if j == HG - 1 else None)
            if not ATTN_GRAM:
                p.op("tensor", lambda t: None, inc="tS")
            gram_done = p.cnt("tS")
            def mk_mask(v, out, mask, gd=gram_done):
                i0 = pp[2].ap().rearrange("p (g t) -> p g t", g=HG)
                i1 = mask.rearrange("p (o t) -> p o t", o=1)
                a, b = broadcast_tensor_aps(i0, i1)
                return v.tensor_tensor(
                    out.rearrange("p (g t) -> p g t", g=HG), a, b, ALU.mult)
            if ATTN_GRAM:
                p.op("vector", lambda v: mk_mask(v, a4, mst[:]),
                     waits=[("tS", gram_done)])
                p.op("vector", lambda v: mk_mask(v, at4, mstT[:]))
                p.op("vector", lambda v: mk_mask(v, oit4, oim[:]),
                     inc="vG")
            else:
                p.op("vector", lambda v: v.engine_nop(),
                     waits=[("tS", gram_done)], inc="vG")
            mask_done = p.cnt("vG")
            def mkY(v, g=g):
                return v.scalar_tensor_tensor(
                    yf[:, :], pp[5][:, 0:256], mblam,
                    vb[:, g * 256:(g + 1) * 256], ALU.mult, ALU.add)
            if not ATTN_GRAM:
                p.op("vector", lambda v: v.engine_nop(),
                     waits=[("tS", gram_done)], inc="vG")
            p.op("vector", (mkY if ATTN_GRAM else
                            (lambda v: v.engine_nop())),
                 waits=[("tS", gram_done)])
            p.op("vector", lambda v: v.tensor_copy(zf[:, :], yf[:, :]))
            p.op("vector", lambda v: v.tensor_copy(zbf[:, :], yf[:, :]),
                 inc="vG")
            p.op("vector", lambda v: v.tensor_copy(mcur[:, :], a4[:, :]),
                 waits=[("vG", p.cnt("vG"))])
            p.op("vector", lambda v: v.tensor_copy(nst[0][:, :], at4[:, :]),
                 inc="vG")
            for lev in (range(1, 7) if ATTN_SQR else []):
                for j in range(HG):
                    p.op("tensor",
                         lambda t, j=j, lev=lev: t.matmul(
                             pp[3][:, j * 128:(j + 1) * 128],
                             nst[lev - 1][:, j * 128:(j + 1) * 128],
                             mcur[:, j * 128:(j + 1) * 128],
                             start=True, stop=True),
                         waits=[("vG", p.cnt("vG"))] if j == 0 else ())
                for j in range(HG):
                    p.op("tensor",
                         lambda t, j=j, lev=lev: t.matmul(
                             pp[4][:, j * 128:(j + 1) * 128],
                             mcur[:, j * 128:(j + 1) * 128],
                             nst[lev - 1][:, j * 128:(j + 1) * 128],
                             start=True, stop=True),
                         inc="tS" if j == HG - 1 else None)
                p.op("vector", lambda v, lev=lev: v.tensor_copy(
                    mcur[:, :], pp[3][:, :]),
                    waits=[("tS", p.cnt("tS"))])
                p.op("vector", lambda v, lev=lev: v.tensor_copy(
                    nst[lev][:, :], pp[4][:, :]), inc="vG")
            for lev in (range(6, 0, -1) if ATTN_APPLY else []):
                for j in range(HG):
                    p.op("tensor",
                         lambda t, j=j, lev=lev: t.matmul(
                             pp[3][:, j * 64:(j + 1) * 64],
                             nst[lev][:, j * 128:(j + 1) * 128],
                             zbf[:, j * 64:(j + 1) * 64],
                             start=True, stop=True),
                         waits=[("vG", p.cnt("vG"))] if j == 0 else (),
                         inc="tS" if j == HG - 1 else None)
                p.op("vector", lambda v: v.tensor_add(zf[:, :], zf[:, :],
                                                      pp[3][:, 0:256]),
                     waits=[("tS", p.cnt("tS"))])
                p.op("vector", lambda v: v.tensor_copy(zbf[:, :], zf[:, :]),
                     inc="vG")
            for j in (range(HG) if ATTN_FIN else []):
                p.op("tensor",
                     lambda t, j=j: t.matmul(
                         pp[3][:, j * 64:(j + 1) * 64],
                         nst[0][:, j * 128:(j + 1) * 128],
                         zbf[:, j * 64:(j + 1) * 64],
                         start=True, stop=True),
                     waits=[("vG", p.cnt("vG"))] if j == 0 else (),
                     inc="tS" if j == HG - 1 else None)
            if ATTN_FIN:
                p.op("vector", lambda v: v.tensor_sub(usb[:, :], zf[:, :],
                                                      pp[3][:, 0:256]),
                     waits=[("tS", p.cnt("tS"))])
            p.op("vector", lambda v: v.tensor_copy(ubf[:, :], usb[:, :]),
                 inc="vG")
            u_done = p.cnt("vG")
            if update_x and ATTN_OY:
                for j in range(HG):
                    p.op("tensor",
                         lambda t, j=j: t.matmul(
                             pp[6][:, j * 64:(j + 1) * 64],
                             oit4[:, j * 128:(j + 1) * 128],
                             ubf[:, j * 64:(j + 1) * 64],
                             start=True, stop=True),
                         waits=[("vG", u_done)] if j == 0 else (),
                         inc="tS" if j == HG - 1 else None)
                p.op("scalar", lambda sc: sc.copy(osb[:, :], pp[6][:, 0:256]),
                     waits=[("tS", p.cnt("tS"))], inc="sS")
                p.op("vector",
                     lambda v: v.scalar_tensor_tensor(
                         osb[:, :], pp[5][:, 0:256], lamvec, osb[:, :],
                         ALU.mult, ALU.add),
                     waits=[("sS", p.cnt("sS"))])
                p.op("vector",
                     lambda v, g=g: v.tensor_add(
                         y_sb[:, g * 256:(g + 1) * 256],
                         vraw[:, g * 256:(g + 1) * 256], osb[:, :]),
                     inc="vO")
            else:
                p.op("vector", lambda v: v.engine_nop(), inc="vO")
            for j, h in (list(enumerate(heads)) if ATTN_KS2 >= 1 else []):
                b = 64 * (h % 2)
                p.op("tensor",
                     lambda t, j=j, h=h, b=b, ck=ck: t.matmul(
                         (pp[7] if b == 0 else pp[6])[:, j * 64:(j + 1) * 64],
                         kT(h, ck), com.ident[b:b + 64, b:b + 64],
                         start=True, stop=True),
                     waits=[("vO", p.cnt("vO")), ("sS", p.cnt("sS"))]
                     if j == 0 else (),
                     inc="tS" if j == HG - 1 else None)
            if not ATTN_KS2:
                p.op("vector", lambda v: v.engine_nop(), inc="vS5")
                continue
            for j, h in enumerate(heads):
                b = 64 * (h % 2)
                p.op("scalar",
                     lambda sc, j=j, b=b: sc.activation(
                         k2bf[:, j * 64:(j + 1) * 64],
                         (pp[7] if b == 0 else pp[6])[:, j * 64:(j + 1) * 64],
                         AF.Identity, scale=lamrev),
                     waits=[("tS", p.cnt("tS"))] if j == 0 else (),
                     inc="sS" if j == HG - 1 else None)
            k2_done = p.cnt("sS")
            for j, h in (list(enumerate(heads)) if ATTN_KS2 >= 3 else []):
                p.op("tensor",
                     lambda t, j=j: t.matmul(
                         pp[7][0:64, j * 64:(j + 1) * 64],
                         k2bf[:, j * 64:(j + 1) * 64],
                         ubf[:, j * 64:(j + 1) * 64],
                         start=True, stop=True),
                     waits=[("sS", k2_done)] if j == 0 else (),
                     inc="tS" if j == HG - 1 else None)
            if ATTN_KS2 >= 3:
                p.op("vector",
                     lambda v, g=g: v.scalar_tensor_tensor(
                         s_all[:, g * 256:(g + 1) * 256],
                         s_all[:, g * 256:(g + 1) * 256], lamC64[:],
                         pp[7][0:64, 0:256], ALU.mult, ALU.add),
                     waits=[("tS", p.cnt("tS"))])
            p.op("vector",
                 lambda v, g=g: v.tensor_copy(
                     s_bf[:, g * 256:(g + 1) * 256],
                     s_all[:, g * 256:(g + 1) * 256]), inc="vS5")
        if update_x:
            yl_prev = p.cnt("vY")
            p.op("vector", lambda v: v.reduce_sum(lnm, y_sb, axis=AX.X),
                 waits=[("vO", p.cnt("vO"))])
            p.op("vector", lambda v: v.tensor_scalar_mul(lnm, lnm, 1.0 / D))
            p.op("vector", lambda v: v.tensor_mul(ysq, y_sb, y_sb))
            p.op("vector", lambda v: v.reduce_sum(lns, ysq, axis=AX.X))
            p.op("vector", lambda v: v.tensor_mul(lnr, lnm, lnm))
            p.op("vector", lambda v: v.scalar_tensor_tensor(
                lnr, lns, 1.0 / D, lnr, ALU.mult, ALU.subtract))
            p.op("vector", lambda v: v.tensor_scalar_add(lnr, lnr, LN_EPS),
                 inc="vY")
            p.op("scalar", lambda sc: sc.activation(lnr, lnr, AF.Sqrt),
                 waits=[("vY", p.cnt("vY"))], inc="sS")
            p.op("vector", lambda v: v.reciprocal(lnr, lnr),
                 waits=[("sS", p.cnt("sS"))])
            p.op("vector", lambda v: v.tensor_scalar(
                ysq, y_sb, lnm, lnr, ALU.subtract, ALU.mult))
            p.op("vector", lambda v: v.tensor_mul(ysq, ysq, gb_sb[:, :]))
            p.op("vector", lambda v: v.tensor_add(ysq, ysq, bb_sb[:, :]),
                 inc="vY")
            yn_done = p.cnt("vY")
            ldx = p.dma("sync", xfrag.rearrange("p (c k) -> p c k", c=NCH),
                        xc_ap[:, ck * C:(ck + 1) * C].rearrange(
                            "(c p) k -> p c k", p=P),
                        waits=[("vY", yl_prev), ("qAXW", p.cnt("qAXW"))] +
                        list(xn_done),
                        inc="qAX")
            for c in range(NCH):
                p.op("tensor",
                     lambda t, c=c: t.transpose(
                         pp[2 if c % 2 == 0 else 3][:, 0:128],
                         ysq[:, c * 128:(c + 1) * 128], com.identf[:]),
                     waits=([("vY", yn_done), ("vG", p.cnt("vG")),
                             ("tS", p.cnt("tS"))] if c <= 1 else
                            [("vY", p.cnt("vY"))]),
                     inc="tS")
                p.op("vector",
                     lambda v, c=c: v.tensor_add(
                         xfrag[:, c * 128:(c + 1) * 128],
                         xfrag[:, c * 128:(c + 1) * 128],
                         pp[2 if c % 2 == 0 else 3][:, 0:128]),
                     waits=[("tS", p.cnt("tS")), ("qAX", ldx)],
                     inc="vY")
            p.dma("gpsimd", xc_ap[:, ck * C:(ck + 1) * C].rearrange(
                "(c p) k -> p c k", p=P),
                xfrag.rearrange("p (c k) -> p c k", c=NCH),
                waits=[("vY", p.cnt("vY"))], inc="qAXW")
        else:
            p.op("vector", lambda v: v.engine_nop(), inc="vY")
    p.op("vector", lambda v: v.engine_nop(), inc="vS5")
    done = ([("qAXW", p.cnt("qAXW"))] if update_x else []) + \
        [("vS5", p.cnt("vS5"))]
    return s_all, done


def one_bc(com):
    return com.ones_rf



def chunk_cols(v):
    return np.ascontiguousarray(np.asarray(v, np.float32).reshape(NCH, P).T)


def shard_x(x):
    xt = np.asarray(x, dtype=np.float32).reshape(-1, D)
    return [np.ascontiguousarray(xt[i * TLOC:(i + 1) * TLOC].T)
            for i in range(NCORES)]


def unshard_x(parts):
    xt = np.concatenate([np.asarray(pp, np.float32).T for pp in parts], axis=0)
    return np.ascontiguousarray(xt.reshape(4, 8192, D))



CAP = 768
CT = 8 * CAP
I32 = mybir.dt.int32


def emit_l1s_pre(p: Prog, com: Common, xn_sb, xn_done, sc_done, dscr, acc,
                 tag):
    sh = {}
    gtmp = p.sbuf(f"{tag}_gtmp", [P, 1024], F32)
    zbuf = acc[:, 0:1024]
    sh["gtmp"] = gtmp
    sh["zbuf"] = zbuf
    xn_tm = dscr["xn_tm"]
    gtb = gtmp.bitcast(BF16)
    p.op("vector", lambda v: v.memset(gtmp[0:49, 0:128], float(TLOC)),
         waits=list(com.ready), inc="vPz")
    p.op("vector", lambda v: v.tensor_scalar(
        gtmp[0:49, 0:128], gtmp[0:49, 0:128], com.rowidx[0:49, :], None,
        ALU.add), inc="vPz")
    p.op("vector", lambda v: v.tensor_copy(
        gtmp[0:49, 128:256].bitcast(I32), gtmp[0:49, 0:128]), inc="vPz")
    p.dma("sync", dscr["gidx"].rearrange("(r k) o -> r (k o)", k=P),
          gtmp[0:49, 128:256].bitcast(I32),
          waits=[("vPz", p.cnt("vPz"))], inc="qPF")
    sh["qPF"] = p.cnt("qPF")
    p.op("vector", lambda v: v.memset(zbuf[:], 0.0), inc="vPz")
    sh["vPz"] = p.cnt("vPz")
    ev_hist = [0, 0]
    qtm_hist = [0, 0]
    for tt in range(32):
        half = tt % 2
        bkb = com.pp[half].ap().bitcast(BF16)
        for c in range(NCH):
            w0 = []
            if c == 0:
                w0 = [("sT", ev_hist[half])]
                if tt == 0:
                    w0 += list(xn_done)
            p.op("tensor",
                 lambda t, tt=tt, c=c, bkb=bkb: t.transpose(
                     bkb[:, c * P:(c + 1) * P],
                     xn_sb[:, c * TLOC + tt * P:c * TLOC + (tt + 1) * P],
                     com.ident[:]),
                 waits=w0, inc="tT" if c == NCH - 1 else None)
        dst = gtb[:, half * 1024:(half + 1) * 1024]
        wv = [("tT", p.cnt("tT")), (f"qTM{half}", qtm_hist[half])]
        if tt == 0:
            wv.append(("qPF", sh["qPF"]))
        p.op("scalar",
             lambda s, dst=dst, bkb=bkb: s.copy(dst[:], bkb[:]),
             waits=wv, inc="sT")
        ev_hist[half] = p.cnt("sT")
        p.dma("sync", xn_tm[tt * P:(tt + 1) * P, :], dst,
              waits=[("sT", ev_hist[half])], inc=f"qTM{half}")
        qtm_hist[half] = p.cnt(f"qTM{half}")
    p.dma("sync", xn_tm[TLOC:TLOC + P, :], zbuf.bitcast(BF16)[:, 0:1024],
          waits=[("vPz", sh["vPz"])], inc="qTMz")
    sh["qTM"] = [("qTM0", qtm_hist[0]), ("qTM1", qtm_hist[1]),
                 ("qTMz", p.cnt("qTMz"))]
    sh["sT"] = p.cnt("sT")
    sh["tT"] = p.cnt("tT")
    return sh


def emit_l1_ffn_sparse(p: Prog, com: Common, xc_ap, xn_sb, wts, mask_sb,
                       mask_ready, mrow, rrow, wb16, stash_rdy, dscr, sh,
                       acc, tag):
    pp = com.pp
    xn_tm, xdelta, gidx_d = dscr["xn_tm"], dscr["xdelta"], dscr["gidx"]
    gtmp, zbuf = sh["gtmp"], sh["zbuf"]
    gtb = gtmp.bitcast(BF16)
    scr3 = com.scr[3]
    b1sb = p.sbuf(f"{tag}_b1", [P, 256], F32)
    b2sb = p.sbuf(f"{tag}_b2", [P, 64], F32)
    w1t = [p.sbuf(f"{tag}_w1t{i}", [P, NCH * P], BF16) for i in range(2)]
    w2t = [p.sbuf(f"{tag}_w2t{i}", [P, 256], BF16) for i in range(2)]
    gidx_sb = p.sbuf(f"{tag}_gi", [P, 49], I32)
    p.dma("sync", b1sb[:], wts["b1l"], inc=f"qB{tag}")
    p.dma("sync", b2sb[:], wts["b2l"], inc=f"qB{tag}")
    small_rdy = p.cnt(f"qB{tag}")

    for zt in range(33):
        p.dma("scalar", xdelta[zt * P:(zt + 1) * P, :], zbuf[:],
              waits=[("vPz", sh["vPz"])] if zt == 0 else (), inc="qZ")
    qz_total = p.cnt("qZ")

    ones8 = acc[32:40, 0:4096]
    cums = acc[32:40, 4096:8192]
    keep = ones8
    ecap = scr3[32:40, 2048:2049]
    srow = scr3[0:1, 0:512]
    arow = scr3[0:1, 512:1024]
    dtmp = scr3[0:1, 1024:1152]
    slotf = acc[:, 1024:1056]
    sloti = acc[:, 1056:1088].bitcast(I32)
    tokf = acc[:, 1088:1120]
    toki = acc[:, 1120:1152].bitcast(I32)

    p.op("vector", lambda v: v.memset(ones8[:], 1.0),
         waits=list(mask_ready) + [("qZ", qz_total)] + sh["qTM"],
         inc="vI")
    p.op("vector", lambda v: v.tensor_tensor_scan(
        cums[:], ones8[:], mask_sb[:, :], 0.0, ALU.mult, ALU.add), inc="vI")
    p.op("vector", lambda v: v.tensor_single_scalar(
        keep[:], cums[:], CAP + 0.5, ALU.is_lt), inc="vI")
    p.op("vector", lambda v: v.tensor_mul(keep[:], keep[:], mask_sb[:, :]),
         inc="vI")
    p.op("vector", lambda v: v.tensor_scalar(
        ecap, com.rowidx[32:40, :], -32.0, float(CAP), ALU.add, ALU.mult))
    p.op("vector", lambda v: v.tensor_scalar(
        cums[:], cums[:], ecap, -1.0, ALU.add, ALU.add))
    p.op("vector", lambda v: v.tensor_mul(cums[:], cums[:], keep[:]),
         inc="vI")
    for q in range(NT):
        p.op("tensor", lambda t, q=q: t.matmul(
            pp[4][0:1, 0:512], com.ones_f[32:40, :],
            cums[:, q * 512:(q + 1) * 512], start=True, stop=True),
            waits=[("vI", p.cnt("vI"))] + (
                [("vI2", p.cnt("vI2"))] if q > 0 else []))
        p.op("tensor", lambda t, q=q: t.matmul(
            pp[5][0:1, 0:512], com.ones_f[32:40, :],
            keep[:, q * 512:(q + 1) * 512], start=True, stop=True),
            inc="tS")
        mmq = p.cnt("tS")
        p.op("vector", lambda v: v.tensor_copy(srow, pp[4][0:1, 0:512]),
             waits=[("tS", mmq)])
        p.op("vector", lambda v: v.tensor_copy(arow, pp[5][0:1, 0:512]))
        p.op("vector", lambda v: v.scalar_tensor_tensor(
            srow, arow, -float(CT), srow, ALU.mult, ALU.add))
        p.op("vector", lambda v: v.tensor_single_scalar(
            srow, srow, float(CT), ALU.add), inc="vI2")
        for b4 in range(4):
            blk = q * 4 + b4
            sb4 = srow[:, b4 * P:(b4 + 1) * P]
            p.op("vector", lambda v, sb4=sb4: v.tensor_single_scalar(
                dtmp, sb4, float(CT) - 0.5, ALU.is_ge))
            p.op("vector", lambda v, sb4=sb4, blk=blk:
                 v.scalar_tensor_tensor(
                     sb4, dtmp, float(blk), sb4, ALU.mult, ALU.add),
                 inc="vI2")
            p.op("tensor", lambda t, q=q, b4=b4, sb4=sb4: t.matmul(
                pp[3][:, q * 4 + b4:q * 4 + b4 + 1],
                sb4, com.ones_rf[0:1, 0:1],
                start=True, stop=True),
                waits=[("vI2", p.cnt("vI2"))],
                inc="tS" if b4 == 3 else None)
    p.op("vector", lambda v: v.tensor_copy(slotf[:, 0:32], pp[3][:, 0:32]),
         waits=[("tS", p.cnt("tS"))])
    p.op("vector", lambda v: v.tensor_copy(sloti[:, 0:32], slotf[:, 0:32]),
         inc="vI")
    for blk in range(32):
        p.op("vector", lambda v, blk=blk: v.tensor_single_scalar(
            tokf[:, blk:blk + 1], com.rowidx, float(blk * P), ALU.add))
    p.op("vector", lambda v: v.tensor_copy(toki[:, 0:32], tokf[:, 0:32]),
         inc="vI")
    idx_rdy = p.cnt("vI")
    for blk in range(32):
        p.op("gpsimd", lambda g, blk=blk: g.indirect_dma_start(
            out=gidx_d[:, :],
            out_offset=IndirectOffsetOnAxis(sloti[:, blk:blk + 1], 0),
            in_=toki[:, blk:blk + 1], in_offset=None),
            waits=[("qPF", sh["qPF"]), ("vI", idx_rdy)] if blk == 0 else (),
            inc="qSC", n=16, is_dma=True)
    sc_total = p.cnt("qSC")
    for g in range(48):
        p.dma("sync", gidx_sb[:, g:g + 1], gidx_d[g * P:(g + 1) * P, :],
              waits=[("qSC", sc_total)] if g == 0 else (), inc="qGI")
    ld_gi = p.cnt("qGI")

    s2b = com.scr[2].bitcast(BF16)
    xeT3 = s2b[:, 0:6144].rearrange("p (c s) -> p c s", c=NCH)

    def outT(dc):
        return com.scr[dc // 4][:, (dc % 4) * 768:(dc % 4) * 768 + 768]

    def hsp(hc):
        return xn_sb[:, hc * 1024:hc * 1024 + 768]

    otm = [scr3[:, 0:1024], scr3[:, 1024:2048]]
    xb_ = scr3[:, 2048:2560]
    t2_ = scr3[:, 2560:3072]
    C0 = 0.7978845608028654
    C1 = 0.044715

    use_hist = {"F0": 0, "F1": 0, "G0": 0, "G1": 0}
    gbuf_hist = [0, 0]
    tin_ev_hist = [0, 0]
    ga_hist = {}
    mm1_last = 0
    outT_free = 0
    otm_free = [0, 0]
    evac_all = 0
    otm_ev_prev = 0
    first_scat = True

    def emit_gather(gg):
        ee, jj = divmod(gg, 6)
        half = jj % 2
        w0 = [("tS", gbuf_hist[half])]
        if gg < 2:
            w0 += [("qGI", ld_gi), ("qSC", sc_total)] + sh["qTM"]
        p.op("gpsimd", lambda g, gg=gg, half=half:
             g.indirect_dma_start(
                 out=gtb[:, half * 1024:(half + 1) * 1024],
                 out_offset=None, in_=xn_tm[:, :],
                 in_offset=IndirectOffsetOnAxis(
                     gidx_sb[:, gg:gg + 1], 0)),
             waits=w0, inc=f"qGA{half}", n=16, is_dma=True)
        ga_hist[(ee, jj)] = p.cnt(f"qGA{half}")

    emit_gather(0)
    emit_gather(1)
    for e in range(8):
        for j in range(6):
            half = j % 2
            bkb = pp[half].ap().bitcast(BF16)
            for c in range(NCH):
                w0 = []
                if c == 0:
                    w0 = [(f"qGA{half}", ga_hist[(e, j)]),
                          ("vT", tin_ev_hist[half])]
                    if j < 2:
                        w0.append(("tS", mm1_last))
                    if e == 0 and j < 2:
                        w0 += [("sT", sh["sT"]), ("vI2", p.cnt("vI2")),
                               ("vI", idx_rdy)]
                p.op("tensor",
                     lambda t, c=c, bkb=bkb, half=half: t.transpose(
                         bkb[:, c * P:(c + 1) * P],
                         gtb[:, half * 1024 + c * P:
                             half * 1024 + (c + 1) * P],
                         com.ident[:]),
                     waits=w0, inc="tS" if c == NCH - 1 else None)
            gbuf_hist[half] = p.cnt("tS")
            p.op("vector", lambda v, j=j, bkb=bkb: v.tensor_copy(
                xeT3[:, :, j * P:(j + 1) * P],
                bkb.rearrange("p (c s) -> p c s", c=NCH)),
                waits=[("tS", p.cnt("tS"))], inc="vT")
            tin_ev_hist[half] = p.cnt("vT")
            if e * 6 + j + 2 < 48:
                emit_gather(e * 6 + j + 2)
        xe_rdy = p.cnt("vT")

        sS_hist = {0: None, 1: None}
        for hc in range(32):
            hb = hc % 2
            ldw = p.dma("sync",
                        w1t[hb].rearrange("p (c k) -> p c k", c=NCH),
                        wb16[0][e][:, hc * P:(hc + 1) * P].rearrange(
                            "(c p) k -> p c k", p=P),
                        waits=[("tS", use_hist[f"F{hb}"]),
                               ("qStash", stash_rdy)], inc=f"qE{hb}")
            for c in range(NCH):
                for ti, (o0, n0) in enumerate(((0, 512), (512, 256))):
                    w0 = []
                    if c == 0 and ti == 0:
                        w0 = [(f"qE{hb}", ldw), ("vT", xe_rdy)]
                        if sS_hist[hb] is not None:
                            w0.append(("sS", sS_hist[hb]))
                        if e == 0 and hc < 2:
                            w0.append(("vI", idx_rdy))
                    ps = (pp[hb * 2] if ti == 0 else pp[hb * 2 + 1])
                    p.op("tensor",
                         lambda t, hc=hc, c=c, hb=hb, o0=o0, n0=n0, ps=ps:
                         t.matmul(
                             ps[:, 0:n0], w1t[hb][:, c * P:(c + 1) * P],
                             xeT3[:, c, o0:o0 + n0],
                             start=(c == 0), stop=(c == NCH - 1)),
                         waits=w0,
                         inc="tS" if (c == NCH - 1 and ti == 1) else None)
            use_hist[f"F{hb}"] = p.cnt("tS")
            for ti, (o0, n0) in enumerate(((0, 512), (512, 256))):
                ps = (pp[hb * 2] if ti == 0 else pp[hb * 2 + 1])
                wv = [("tS", p.cnt("tS")), (f"qB{tag}", small_rdy),
                      ("vH", p.cnt("vH")), ("pG", p.cnt("pG"))]
                if e == 0 and hc == 0 and ti == 0:
                    wv.append(("tT", sh["tT"]))
                p.op("scalar",
                     lambda s, hc=hc, e=e, n0=n0, ps=ps: s.activation(
                         xb_[:, 0:n0], ps[:, 0:n0], AF.Identity,
                         bias=b1sb[:, e * 32 + hc:e * 32 + hc + 1]),
                     waits=wv, inc="sS")
                p.op("gpsimd",
                     lambda g, n0=n0: g.tensor_mul(
                         t2_[:, 0:n0], xb_[:, 0:n0], xb_[:, 0:n0]),
                     waits=[("sS", p.cnt("sS"))], inc="pG")
                p.op("vector",
                     lambda v, n0=n0: v.tensor_scalar(
                         t2_[:, 0:n0], t2_[:, 0:n0], C1, 1.0, ALU.mult,
                         ALU.add),
                     waits=[("pG", p.cnt("pG"))])
                p.op("vector",
                     lambda v, n0=n0: v.tensor_mul(
                         t2_[:, 0:n0], t2_[:, 0:n0], xb_[:, 0:n0]),
                     inc="vH")
                p.op("scalar",
                     lambda s, n0=n0: s.activation(
                         t2_[:, 0:n0], t2_[:, 0:n0], AF.Tanh, scale=C0),
                     waits=[("vH", p.cnt("vH"))], inc="sS")
                p.op("vector",
                     lambda v, n0=n0: v.tensor_scalar(
                         t2_[:, 0:n0], t2_[:, 0:n0], 0.5, 0.5, ALU.mult,
                         ALU.add),
                     waits=[("sS", p.cnt("sS"))], inc="vG2")
                p.op("gpsimd",
                     lambda g, hc=hc, o0=o0, n0=n0: g.tensor_mul(
                         hsp(hc)[:, o0:o0 + n0], t2_[:, 0:n0],
                         xb_[:, 0:n0]),
                     waits=[("vG2", p.cnt("vG2"))], inc="vH")
            sS_hist[hb] = p.cnt("sS")
        mm1_last = max(use_hist["F0"], use_hist["F1"])
        h_done = p.cnt("vH")

        for dcq in range(4):
            for hc in range(32):
                gb = hc % 2
                ldw2 = p.dma("sync", w2t[gb][:],
                             wb16[1][e][hc * P:(hc + 1) * P,
                                        dcq * 256:(dcq + 1) * 256],
                             waits=[("tS", use_hist[f"G{gb}"]),
                                    ("qStash", stash_rdy)], inc=f"qH{gb}")
                for du in range(2):
                    for ti, (o0, n0) in enumerate(((0, 512), (512, 256))):
                        w0 = []
                        if du == 0 and ti == 0:
                            w0 = [(f"qH{gb}", ldw2)]
                            if hc == 0:
                                w0 += [("vH", h_done), ("sE", p.cnt("sE")),
                                       ("vC", otm_ev_prev),
                                       ("tS", outT_free)]
                                if e == 0 and dcq == 0:
                                    w0 += [("vI2", p.cnt("vI2")),
                                           ("sT", sh["sT"])]
                        ps = (pp[4 + 2 * du] if ti == 0
                              else pp[5 + 2 * du])
                        p.op("tensor",
                             lambda t, hc=hc, du=du, gb=gb, o0=o0, n0=n0,
                             ps=ps: t.matmul(
                                 ps[:, 0:n0], w2t[gb][:, du * P:(du + 1) * P],
                                 hsp(hc)[:, o0:o0 + n0],
                                 start=(hc == 0), stop=(hc == 31)),
                             waits=w0,
                             inc="tS" if (du == 1 and ti == 1) else None)
                use_hist[f"G{gb}"] = p.cnt("tS")
            mm2q = p.cnt("tS")
            for du in range(2):
                dc = dcq * 2 + du
                bias = b2sb[:, e * 8 + dc:e * 8 + dc + 1]
                p.op("scalar",
                     lambda s, dc=dc, du=du, bias=bias: s.activation(
                         outT(dc)[:, 0:512], pp[4 + 2 * du][:],
                         AF.Identity, bias=bias),
                     waits=[("tS", mm2q), (f"qB{tag}", small_rdy)]
                     if du == 0 else (), inc="sE")
                p.op("scalar",
                     lambda s, dc=dc, du=du, bias=bias: s.activation(
                         outT(dc)[:, 512:768], pp[5 + 2 * du][:, 0:256],
                         AF.Identity, bias=bias), inc="sE")
        evac_all = p.cnt("sE")

        for j in range(6):
            jh = j % 2
            for dc in range(8):
                bk = pp[4] if dc < 4 else pp[5]
                w0 = []
                if dc == 0:
                    w0 = [("sE", evac_all)]
                    if j >= 1:
                        w0.append(("vC", p.cnt("vC")))
                p.op("tensor",
                     lambda t, dc=dc, j=j, bk=bk: t.transpose(
                         bk[:, (dc % 4) * P:(dc % 4 + 1) * P],
                         outT(dc)[:, j * P:(j + 1) * P], com.identf[:]),
                     waits=w0, inc="tS" if dc == 7 else None)
            tr_done = p.cnt("tS")
            wv = [("tS", tr_done), (f"qSD{jh}", otm_free[jh])]
            p.op("vector", lambda v, jh=jh: v.tensor_copy(
                otm[jh][:, 0:512], pp[4][:, 0:512]), waits=wv)
            p.op("vector", lambda v, jh=jh: v.tensor_copy(
                otm[jh][:, 512:1024], pp[5][:, 0:512]), inc="vC")
            ev = p.cnt("vC")
            ws = [("vC", ev)]
            if first_scat:
                ws.append(("qZ", qz_total))
                first_scat = False
            p.op("gpsimd", lambda g, e=e, j=j, jh=jh:
                 g.indirect_dma_start(
                     out=xdelta[:, :],
                     out_offset=IndirectOffsetOnAxis(
                         gidx_sb[:, e * 6 + j:e * 6 + j + 1], 0),
                     in_=otm[jh][:], in_offset=None),
                 waits=ws, inc=f"qSD{jh}", n=16, is_dma=True)
            otm_free[jh] = p.cnt(f"qSD{jh}")
        outT_free = p.cnt("tS")
        otm_ev_prev = p.cnt("vC")

    qsd_tot = [("qSD0", p.cnt("qSD0")), ("qSD1", p.cnt("qSD1"))]

    dve_hist = [0, 0]
    wb_hist = [0, 0]
    last_sS = p.cnt("sS")
    last_vT = p.cnt("vT")
    for tt in range(32):
        h = tt % 2
        xdl = [scr3[:, 0:1024], scr3[:, 1024:2048]][h]
        xfr = [scr3[:, 2048:3072], scr3[:, 3072:4096]][h]
        bA, bB = [(pp[0], pp[1]), (pp[2], pp[3])][h]
        ldd = p.dma("sync", xdl, xdelta[tt * P:(tt + 1) * P, :],
                    waits=(list(qsd_tot) if tt < 2 else []) +
                    [("vX", dve_hist[h]), ("vC", p.cnt("vC"))],
                    inc=f"qXD{h}")
        ldx = p.dma("sync", xfr.rearrange("p (c k) -> p c k", c=NCH),
                    xc_ap[:, tt * P:(tt + 1) * P].rearrange(
                        "(c p) k -> p c k", p=P),
                    waits=[(f"qXU{h}", wb_hist[h])], inc=f"qXD{h}")
        ld_all = p.cnt(f"qXD{h}")
        for c in range(NCH):
            bk = bA if c < 4 else bB
            w0 = []
            if c == 0:
                w0 = [(f"qXD{h}", ld_all), ("vX", dve_hist[h])]
                if tt < 2:
                    w0 += [("sS", last_sS), ("vT", last_vT)]
            p.op("tensor", lambda t, c=c, bk=bk, xdl=xdl: t.transpose(
                bk[:, (c % 4) * P:(c % 4 + 1) * P],
                xdl[:, c * P:(c + 1) * P], com.identf[:]),
                waits=w0, inc="tS" if c == NCH - 1 else None)
        p.op("tensor", lambda t, tt=tt: t.matmul(
            pp[6][:, 0:128], com.ones_rf[0:1, :],
            mrow[:, tt * P:(tt + 1) * P], start=True, stop=True),
            waits=[("vX", p.cnt("vX"))] +
            ([("sE", p.cnt("sE"))] if tt == 0 else []))
        p.op("tensor", lambda t, tt=tt: t.matmul(
            pp[6][:, 128:256], com.ones_rf[0:1, :],
            rrow[:, tt * P:(tt + 1) * P], start=True, stop=True), inc="tS")
        tdone = p.cnt("tS")
        mb3 = pp[6].ap()[:, 0:128].rearrange("p (o t) -> p o t", o=1)
        rb3 = pp[6].ap()[:, 128:256].rearrange("p (o t) -> p o t", o=1)
        xv = xfr.rearrange("p (c k) -> p c k", c=NCH)
        fv = xdl.rearrange("p (c k) -> p c k", c=NCH)

        def sub_p(v, xv=xv, fv=fv, mb3=mb3):
            a, b = broadcast_tensor_aps(xv, mb3)
            return v.tensor_tensor(fv, a, b, ALU.subtract)
        p.op("vector", sub_p, waits=[("tS", tdone)])

        def mul_p(v, fv=fv, rb3=rb3):
            a, b = broadcast_tensor_aps(fv, rb3)
            return v.tensor_tensor(fv, a, b, ALU.mult)
        p.op("vector", mul_p)
        p.op("vector", lambda v, xdl=xdl, bA=bA: v.tensor_add(
            xdl[:, 0:512], xdl[:, 0:512], bA.ap()[:, 0:512]))
        p.op("vector", lambda v, xdl=xdl, bB=bB: v.tensor_add(
            xdl[:, 512:1024], xdl[:, 512:1024], bB.ap()[:, 0:512]))
        p.op("vector", lambda v, xfr=xfr, xdl=xdl: v.scalar_tensor_tensor(
            xfr[:], xdl[:], 0.5, xfr[:], ALU.mult, ALU.add), inc="vX")
        dve_hist[h] = p.cnt("vX")
        p.dma("gpsimd", xc_ap[:, tt * P:(tt + 1) * P].rearrange(
            "(c p) k -> p c k", p=P),
            xfr.rearrange("p (c k) -> p c k", c=NCH),
            waits=[("vX", p.cnt("vX"))], inc=f"qXU{h}")
        wb_hist[h] = p.cnt(f"qXU{h}")
    return [("qXU0", p.cnt("qXU0")), ("qXU1", p.cnt("qXU1"))]



def _io(nc, outs=("xco",)):
    d = {}
    d["xc"] = nc.dram_tensor("xc", [D, TLOC], F32, kind="ExternalInput").ap()
    d["cb"] = nc.dram_tensor("cb", [160, 2688], BF16,
                             kind="ExternalInput").ap()
    d["cf"] = nc.dram_tensor("cf", [128, 532], F32,
                             kind="ExternalInput").ap()
    if "xco" in outs:
        d["xco"] = nc.dram_tensor("xco", [D, TLOC], F32,
                                  kind="ExternalOutput").ap()
        d["xs"] = nc.dram_tensor("xs", [D, TLOC], F32, kind="Internal").ap()
    return d


def build_l0_ffn():
    nc = bass.Bass("TRN2", target_bir_lowering=False)
    d = _io(nc)
    wgl = nc.dram_tensor("Wg_l", [P, 64], F32, kind="ExternalInput").ap()
    bgt = nc.dram_tensor("bg_t", [1, 256], F32, kind="ExternalInput").ap()
    w1 = nc.dram_tensor("W1", [8, D, 4096], F32, kind="ExternalInput").ap()
    b1l = nc.dram_tensor("b1l", [P, 256], F32, kind="ExternalInput").ap()
    w2 = nc.dram_tensor("W2", [8, 4096, D], F32, kind="ExternalInput").ap()
    b2l = nc.dram_tensor("b2l", [P, 64], F32, kind="ExternalInput").ap()
    p = Prog(nc)
    com = Common(p, d["cb"], d["cf"])
    xn_sb = p.sbuf("xn", [P, NCH * TLOC], BF16)
    p.dma("sync", d["xs"], d["xc"], inc="qIn")
    x_ready = [("qIn", p.cnt("qIn"))]
    mrow, rrow, ln_done = emit_ln(p, com, d["xs"], xn_sb, "a", x_ready)
    wts = dict(Wg_l=wgl, bg_t=bgt, W1=w1, b1l=b1l, W2=w2, b2l=b2l)
    done = emit_l0_ffn(p, com, d["xs"], xn_sb, ln_done, wts, (mrow, rrow))
    p.dma("gpsimd", d["xco"], d["xs"], waits=done, inc="qO")
    p.op("gpsimd", lambda e: None, waits=[("qO", p.cnt("qO"))])
    return p.finalize()


def build_l0_attn(update_x):
    nc = bass.Bass("TRN2", target_bir_lowering=False)
    d = _io(nc, outs=("xco",) if update_x else ())
    wdt = nc.dram_tensor("WdT", [D, D], F32, kind="ExternalInput").ap()
    wit = nc.dram_tensor("WiT", [D, D], F32, kind="ExternalInput").ap()
    bd = nc.dram_tensor("bd", [P, NCH], F32, kind="ExternalInput").ap()
    bi = nc.dram_tensor("bi", [P, NCH], F32, kind="ExternalInput").ap()
    carry = nc.dram_tensor("carry", [P, NCH], F32,
                           kind="ExternalInput").ap()
    fino = nc.dram_tensor("fino", [P, NCH], F32, kind="ExternalOutput").ap()
    p = Prog(nc)
    com = Common(p, d["cb"], d["cf"])
    xn_sb = p.sbuf("xn", [P, NCH * TLOC], BF16)
    carry_sb = p.sbuf("carry", [P, NCH], F32)
    xsrc = d.get("xs", d["xc"])
    if update_x:
        p.dma("sync", d["xs"], d["xc"], inc="qIn")
    p.dma("sync", carry_sb[:], carry, inc="qIn")
    x_ready = [("qIn", p.cnt("qIn"))]
    mrow, rrow, ln_done = emit_ln(p, com, xsrc, xn_sb, "a", x_ready)
    fin, done = emit_l0_attn(p, com, xsrc, xn_sb, ln_done,
                             dict(WdT=wdt, WiT=wit, bd=bd, bi=bi),
                             carry_sb, x_ready, update_x=update_x)
    if update_x:
        p.dma("gpsimd", d["xco"], d["xs"], waits=done, inc="qO")
    p.dma("gpsimd", fino, fin[:], waits=done, inc="qO")
    p.op("gpsimd", lambda e: None, waits=[("qO", p.cnt("qO"))])
    return p.finalize()


def build_l1_scores():
    nc = bass.Bass("TRN2", target_bir_lowering=False)
    d = _io(nc, outs=())
    ctl = nc.dram_tensor("centT_l", [P, 64], F32, kind="ExternalInput").ap()
    lng = nc.dram_tensor("lng", [P, NCH], F32, kind="ExternalInput").ap()
    lnb = nc.dram_tensor("lnb", [P, NCH], F32, kind="ExternalInput").ap()
    sco = nc.dram_tensor("sco", [8, TLOC], BF16, kind="ExternalOutput").ap()
    p = Prog(nc)
    com = Common(p, d["cb"], d["cf"])
    xn_sb = p.sbuf("xn", [P, NCH * TLOC], BF16)
    lng_sb = p.sbuf("lng", [P, NCH], F32)
    lnb_sb = p.sbuf("lnb", [P, NCH], F32)
    scT = p.sbuf("scT", [8, TLOC], BF16)
    p.dma("sync", lng_sb[:], lng, inc="qIn")
    p.dma("sync", lnb_sb[:], lnb, inc="qIn")
    rdy = [("qIn", p.cnt("qIn"))]
    mrow, rrow, ln_done = emit_ln(p, com, d["xc"], xn_sb, "a", rdy,
                                  fuse_double=True, gvec=lng_sb,
                                  bvec=lnb_sb)
    sc_done = emit_l1_scores(p, com, xn_sb, ln_done, ctl, scT)
    p.dma("gpsimd", sco, scT[:], waits=sc_done, inc="qO")
    p.op("gpsimd", lambda e: None, waits=[("qO", p.cnt("qO"))])
    return p.finalize()


def build_l1_ffn():
    nc = bass.Bass("TRN2", target_bir_lowering=False)
    d = _io(nc)
    lng = nc.dram_tensor("lng", [P, NCH], F32, kind="ExternalInput").ap()
    lnb = nc.dram_tensor("lnb", [P, NCH], F32, kind="ExternalInput").ap()
    scin = nc.dram_tensor("scin", [8, TLOC], BF16,
                          kind="ExternalInput").ap()
    gin = nc.dram_tensor("gin", [P, 2048], BF16, kind="ExternalInput").ap()
    w1 = nc.dram_tensor("W1", [8, D, 4096], F32, kind="ExternalInput").ap()
    b1l = nc.dram_tensor("b1l", [P, 256], F32, kind="ExternalInput").ap()
    w2 = nc.dram_tensor("W2", [8, 4096, D], F32, kind="ExternalInput").ap()
    b2l = nc.dram_tensor("b2l", [P, 64], F32, kind="ExternalInput").ap()
    p = Prog(nc)
    com = Common(p, d["cb"], d["cf"])
    xn_sb = p.sbuf("xn", [P, NCH * TLOC], BF16)
    lng_sb = p.sbuf("lng", [P, NCH], F32)
    lnb_sb = p.sbuf("lnb", [P, NCH], F32)
    acc = p.sbuf("l1f_acc", [P, NCH * 1024], F32)
    scT = acc[0:8, 2304:4352].bitcast(BF16)
    gath = acc[:, 6400:7424].bitcast(BF16)
    mask = com.rowp[32:40, 2560:4608].bitcast(BF16)
    p.dma("sync", d["xs"], d["xc"], inc="qIn")
    p.dma("sync", lng_sb[:], lng, inc="qIn")
    p.dma("sync", lnb_sb[:], lnb, inc="qIn")
    p.dma("sync", scT, scin, inc="qIn")
    p.dma("sync", gath, gin, inc="qIn")
    in_rdy = [("qIn", p.cnt("qIn"))]
    mrow, rrow, ln_done = emit_ln(p, com, d["xs"], xn_sb, "a", in_rdy,
                                  fuse_double=True, gvec=lng_sb,
                                  bvec=lnb_sb)
    route_done = emit_l1_route(p, com, gath, in_rdy, scT, in_rdy, mask, acc)
    wts = dict(W1=w1, b1l=b1l, W2=w2, b2l=b2l)
    done = emit_l0_ffn(p, com, d["xs"], xn_sb, ln_done, wts, (mrow, rrow),
                       tag="l1f", gelu_tanh=True, mask_sb=mask,
                       mask_ready=route_done, extra_res=(mrow, rrow),
                       acc_ext=acc)
    p.dma("gpsimd", d["xco"], d["xs"], waits=done, inc="qO")
    p.op("gpsimd", lambda e: None, waits=[("qO", p.cnt("qO"))])
    return p.finalize()


def build_l1_attn(update_x):
    nc = bass.Bass("TRN2", target_bir_lowering=False)
    d = _io(nc, outs=("xco",) if update_x else ())
    wt = nc.dram_tensor("WT", [D, D], F32, kind="ExternalInput").ap()
    grow = nc.dram_tensor("grow", [1, 1024], F32, kind="ExternalInput").ap()
    brow = nc.dram_tensor("brow", [1, 1024], F32, kind="ExternalInput").ap()
    sca = nc.dram_tensor("sca", [1, 2], F32, kind="ExternalInput").ap()
    sin = nc.dram_tensor("sin", [64, 1024], F32, kind="ExternalInput").ap()
    sfo = nc.dram_tensor("sfo", [64, 1024], F32, kind="ExternalOutput").ap()
    p = Prog(nc)
    com = Common(p, d["cb"], d["cf"])
    xn_sb = p.sbuf("xn", [P, NCH * TLOC], BF16)
    sin_sb = p.sbuf("sin", [64, 1024], F32)
    xsrc = d.get("xs", d["xc"])
    if update_x:
        p.dma("sync", d["xs"], d["xc"], inc="qIn")
    p.dma("sync", sin_sb[:], sin, inc="qIn")
    in_rdy = [("qIn", p.cnt("qIn"))]
    mrow, rrow, ln_done = emit_ln(p, com, xsrc, xn_sb, "a", in_rdy)
    wts = dict(WT=wt, grow=grow, brow=brow, sca=sca)
    sfin, done = emit_l1_attn(p, com, xsrc, xn_sb, ln_done, wts, sin_sb,
                              in_rdy, update_x=update_x)
    if update_x:
        p.dma("gpsimd", d["xco"], d["xs"], waits=done, inc="qO")
    p.dma("gpsimd", sfo, sfin[:], waits=done, inc="qO")
    p.op("gpsimd", lambda e: None, waits=[("qO", p.cnt("qO"))])
    return p.finalize()



_NC_CACHE = {}


def _get_nc(name, fn):
    if name not in _NC_CACHE:
        _NC_CACHE[name] = fn()
    return _NC_CACHE[name]


def _host_gather(scores):
    g = np.zeros((128, 2048), dtype=BF)
    for r in range(NCORES):
        for e in range(8):
            for h in range(2):
                g[e * 16 + r * 2 + h, :] = \
                    scores[r][e, h * 2048:(h + 1) * 2048]
    return g


def kernel(**inputs):
    if USE_FUSED:
        return kernel_fused(**inputs)
    gi = lambda k: np.asarray(inputs[k], np.float32)
    shards = shard_x(inputs["L0_attn_x_seq"])
    cbv = make_consts()
    cfv = make_constsf()
    base = dict(cb=cbv, cf=cfv)

    l0f_w = dict(
        Wg_l=np.ascontiguousarray(gi("L0_ffn_Wg").reshape(8, 128, 8)
                                  .transpose(1, 0, 2).reshape(128, 64)),
        bg_t=np.ascontiguousarray(np.tile(gi("L0_ffn_bg"), 32)[None, :]),
        W1=gi("L0_ffn_W1"), W2=gi("L0_ffn_W2"),
        b1l=np.ascontiguousarray(gi("L0_ffn_b1").reshape(8, 32, 128)
                                 .transpose(2, 0, 1).reshape(128, 256)),
        b2l=np.ascontiguousarray(gi("L0_ffn_b2").reshape(8, 8, 128)
                                 .transpose(2, 0, 1).reshape(128, 64)))
    l0a_w = dict(
        WdT=np.ascontiguousarray(gi("L0_attn_W_decay").T),
        WiT=np.ascontiguousarray(gi("L0_attn_W_input").T),
        bd=chunk_cols(gi("L0_attn_b_decay")),
        bi=chunk_cols(gi("L0_attn_b_input")))
    l1f_w = dict(
        lng=chunk_cols(gi("L1_ffn_ln_g")), lnb=chunk_cols(gi("L1_ffn_ln_b")),
        W1=gi("L1_ffn_w1"), W2=gi("L1_ffn_w2"),
        b1l=np.ascontiguousarray(gi("L1_ffn_b1").reshape(8, 32, 128)
                                 .transpose(2, 0, 1).reshape(128, 256)),
        b2l=np.ascontiguousarray(gi("L1_ffn_b2").reshape(8, 8, 128)
                                 .transpose(2, 0, 1).reshape(128, 64)))
    ctl = np.ascontiguousarray(gi("L1_ffn_centroids").T.reshape(8, 128, 8)
                               .transpose(1, 0, 2).reshape(128, 64))
    l1a_w = dict(
        WT=np.ascontiguousarray(gi("L1_attn_W").T),
        grow=np.ascontiguousarray(gi("L1_attn_ln_gamma")[None, :]),
        brow=np.ascontiguousarray(gi("L1_attn_ln_beta")[None, :]),
        sca=np.array([[float(np.asarray(inputs["L1_attn_eta"])),
                       float(np.asarray(inputs["L1_attn_lam_logit"]))]],
                     np.float32))
    cores = list(range(NCORES))

    def run(name, fn, per_core):
        nc = _get_nc(name, fn)
        res = run_bass_kernel_spmd(nc, per_core, core_ids=cores)
        return res.results

    def l0_ffn_step(xs):
        r = run("l0f", build_l0_ffn,
                [dict(xc=xs[i], **base, **l0f_w) for i in cores])
        return [np.asarray(r[i]["xco"], np.float32) for i in cores]

    def l0_attn_step(xs):
        zc = np.zeros((P, NCH), np.float32)
        ra = run("l0aA", lambda: build_l0_attn(False),
                 [dict(xc=xs[i], carry=zc, **base, **l0a_w) for i in cores])
        fins = [np.asarray(ra[i]["fino"], np.float32) for i in cores]
        carries = [zc if i % 2 == 0 else fins[i - 1] for i in cores]
        rb = run("l0aB", lambda: build_l0_attn(True),
                 [dict(xc=xs[i], carry=carries[i], **base, **l0a_w)
                  for i in cores])
        return [np.asarray(rb[i]["xco"], np.float32) for i in cores]

    def l1_ffn_step(xs):
        rs = run("l1s", build_l1_scores,
                 [dict(xc=xs[i], centT_l=ctl, lng=l1f_w["lng"],
                       lnb=l1f_w["lnb"], **base) for i in cores])
        scores = [np.asarray(rs[i]["sco"]) for i in cores]
        g = _host_gather(scores)
        rb = run("l1f", build_l1_ffn,
                 [dict(xc=xs[i], scin=scores[i], gin=g, **base, **l1f_w)
                  for i in cores])
        return [np.asarray(rb[i]["xco"], np.float32) for i in cores]

    def l1_attn_step(xs):
        zs = np.zeros((64, 1024), np.float32)
        ra = run("l1aA", lambda: build_l1_attn(False),
                 [dict(xc=xs[i], sin=zs, **base, **l1a_w) for i in cores])
        sfins = [np.asarray(ra[i]["sfo"], np.float32) for i in cores]
        sins = [zs if i % 2 == 0 else sfins[i - 1] for i in cores]
        rb = run("l1aB", lambda: build_l1_attn(True),
                 [dict(xc=xs[i], sin=sins[i], **base, **l1a_w)
                  for i in cores])
        return [np.asarray(rb[i]["xco"], np.float32) for i in cores]

    xs = shards
    xs = l0_ffn_step(xs)
    xs = l0_attn_step(xs)
    xs = l0_ffn_step(xs)
    xs = l1_ffn_step(xs)
    xs = l1_attn_step(xs)
    xs = l1_ffn_step(xs)
    return unshard_x(xs)



def build_fused():
    nc = bass.Bass("TRN2", target_bir_lowering=False)
    T = nc.dram_tensor
    xc = T("xc", [D, TLOC], F32, kind="ExternalInput").ap()
    cb = T("cb", [160, 2688], BF16, kind="ExternalInput").ap()
    cf = T("cf", [128, 532], F32, kind="ExternalInput").ap()
    osel = T("osel", [1, 8], F32, kind="ExternalInput").ap()
    wgl = T("Wg_l", [P, 64], F32, kind="ExternalInput").ap()
    bgt = T("bg_t", [1, 256], F32, kind="ExternalInput").ap()
    w01 = T("W01", [8, D, 4096], F32, kind="ExternalInput").ap()
    b01 = T("b01l", [P, 256], F32, kind="ExternalInput").ap()
    w02 = T("W02", [8, 4096, D], F32, kind="ExternalInput").ap()
    b02 = T("b02l", [P, 64], F32, kind="ExternalInput").ap()
    wdt = T("WdT", [D, D], F32, kind="ExternalInput").ap()
    wit = T("WiT", [D, D], F32, kind="ExternalInput").ap()
    bd = T("bd", [P, NCH], F32, kind="ExternalInput").ap()
    bi = T("bi", [P, NCH], F32, kind="ExternalInput").ap()
    ctl = T("centT_l", [P, 64], F32, kind="ExternalInput").ap()
    lng = T("lng", [P, NCH], F32, kind="ExternalInput").ap()
    lnb = T("lnb", [P, NCH], F32, kind="ExternalInput").ap()
    w11 = T("W11", [8, D, 4096], F32, kind="ExternalInput").ap()
    b11 = T("b11l", [P, 256], F32, kind="ExternalInput").ap()
    w12 = T("W12", [8, 4096, D], F32, kind="ExternalInput").ap()
    b12 = T("b12l", [P, 64], F32, kind="ExternalInput").ap()
    wt1 = T("WT", [D, D], F32, kind="ExternalInput").ap()
    grow = T("grow", [1, 1024], F32, kind="ExternalInput").ap()
    brow = T("brow", [1, 1024], F32, kind="ExternalInput").ap()
    sca = T("sca", [1, 2], F32, kind="ExternalInput").ap()
    xco = T("xco", [D, TLOC], F32, kind="ExternalOutput").ap()
    xs = T("xs", [D, TLOC], F32, kind="Internal").ap()
    w01b = T("W01b", [8, D, 4096], BF16, kind="Internal").ap()
    w02b = T("W02b", [8, 4096, D], BF16, kind="Internal").ap()
    w11b = T("W11b", [8, D, 4096], BF16, kind="Internal").ap()
    w12b = T("W12b", [8, 4096, D], BF16, kind="Internal").ap()
    fin_in = T("fin_in", [P, NCH], F32, kind="Internal")
    fin_out = T("fin_out", [NCORES * P, NCH], F32, kind="Internal")
    sc_in = [T(f"sc_in{i}", [8, TLOC], BF16, kind="Internal")
             for i in range(2)]
    sc_out = [T(f"sc_out{i}", [NCORES * 8, TLOC], BF16, kind="Internal")
              for i in range(2)]
    s_in = T("s_in", [64, 1024], F32, kind="Internal")
    s_out = T("s_out", [NCORES * 64, 1024], F32, kind="Internal")
    gath_d = T("gath_d", [P, 2048], BF16, kind="Internal")
    dscr = None
    if SPARSE_L1:
        dscr = dict(
            xn_tm=T("xn_tm", [TLOC + P, 1024], BF16, kind="Internal").ap(),
            xdelta=T("xdelta", [TLOC + P, 1024], F32, kind="Internal").ap(),
            gidx=T("gidx_d", [CT + P, 1], I32, kind="Internal").ap())

    p = Prog(nc)
    com = Common(p, cb, cf)
    xn_sb = p.sbuf("xn", [P, NCH * TLOC], BF16)
    carry_sb = p.sbuf("carry", [P, NCH], F32)
    sin_sb = com.rowp[64:128, 0:1024]
    oselb = p.sbuf("oselb", [P, 8], F32)
    lng_sb = p.sbuf("lng", [P, NCH], F32)
    lnb_sb = p.sbuf("lnb", [P, NCH], F32)

    osel_sb = p.sbuf("osel", [1, 8], F32)
    p.dma("sync", xs, xc, inc="qIn")
    p.dma("sync", lng_sb[:], lng, inc="qIn")
    p.dma("sync", lnb_sb[:], lnb, inc="qIn")
    p.dma("sync", osel_sb[:], osel, inc="qIn")
    x_ready = [("qIn", p.cnt("qIn"))]
    gb_rdy = x_ready
    p.op("tensor", lambda t: t.matmul(com.pp[5][:, 0:8], com.ones_rf[:],
                                      osel_sb[:], start=True, stop=True),
         waits=x_ready + list(com.readyv), inc="tS")
    p.op("scalar", lambda sc: sc.copy(oselb[:], com.pp[5][:, 0:8]),
         waits=[("tS", p.cnt("tS"))], inc="sS")

    stash = {}
    for name, src, dst in (("01", w01, w01b), ("02", w02, w02b),
                           ("11", w11, w11b), ("12", w12, w12b)):
        for e in range(8):
            p.dma("gpsimd", dst[e], src[e], inc="qStash")
        stash[name] = p.cnt("qStash")
    wts0 = dict(Wg_l=wgl, bg_t=bgt, W1=w01, b1l=b01, W2=w02, b2l=b02)
    wts0a = dict(WdT=wdt, WiT=wit, bd=bd, bi=bi)
    wts1 = dict(W1=w11, b1l=b11, W2=w12, b2l=b12)
    wts1a = dict(WT=wt1, grow=grow, brow=brow, sca=sca)

    def scoped(fn):
        p.begin_scope(p.snapshot())
        es = p.scope_es()
        with es:
            fn()
        p._scope_stack = None

    def s1():
        mrow, rrow, ln_done = emit_ln(p, com, xs, xn_sb, "l1", x_ready)
        emit_l0_ffn(p, com, xs, xn_sb, ln_done, wts0, (mrow, rrow),
                    tag="f1", wb16=(w01b, w02b), stash_rdy=stash["02"])
    scoped(s1)

    def s2():
        mrow, rrow, ln_done = emit_ln(p, com, xs, xn_sb, "l2", ())
        p.op("vector", lambda v: v.memset(carry_sb[:], 0.0), inc="vS3")
        zrdy = p.cnt("vS3")
        fin, fdone = emit_l0_attn(p, com, xs, xn_sb, ln_done, wts0a,
                                  carry_sb, [("vS3", zrdy)], tag="a2",
                                  update_x=False)
        p.dma("gpsimd", fin_in.ap(), fin[:], waits=fdone, inc="qCC")
        p.op("gpsimd", lambda g: g.collective_compute(
            "AllGather", ALU.bypass,
            replica_groups=[list(range(NCORES))],
            ins=[fin_in.ap().opt()], outs=[fin_out.ap().opt()]),
            waits=[("qCC", p.cnt("qCC"))], inc="ccS")
        gf = p.sbuf("gathf", [P, 64], F32)
        p.dma("gpsimd", gf.rearrange("p (r e) -> p r e", r=8),
              fin_out.ap().rearrange("(r p) e -> p r e", p=P),
              waits=[("ccS", p.cnt("ccS"))], inc="qCC")
        p.op("vector", lambda v: v.memset(carry_sb[:], 0.0),
             waits=[("qCC", p.cnt("qCC")), ("sS", 2)])
        for r in range(8):
            p.op("vector",
                 lambda v, r=r: v.scalar_tensor_tensor(
                     carry_sb[:], gf[:, r * 8:(r + 1) * 8],
                     oselb[:, r:r + 1], carry_sb[:], ALU.mult, ALU.add),
                 inc="vS3" if r == 7 else None)
    scoped(s2)
    carry_rdy = p.cnt("vS3")

    def s3():
        fin, done = emit_l0_attn(p, com, xs, xn_sb, [], wts0a,
                                 carry_sb, [("vS3", carry_rdy)], tag="a3",
                                 update_x=True)
    scoped(s3)

    def s4():
        mrow, rrow, ln_done = emit_ln(p, com, xs, xn_sb, "l4", ())
        emit_l0_ffn(p, com, xs, xn_sb, ln_done, wts0, (mrow, rrow),
                    tag="f4", wb16=(w01b, w02b), stash_rdy=stash["02"])
    scoped(s4)

    def l1_ffn_scope(idx):
        def body():
            mrow, rrow, ln_done = emit_ln(
                p, com, xs, xn_sb, f"l5{idx}", gb_rdy,
                fuse_double=True, gvec=lng_sb, bvec=lnb_sb)
            acc = p.sbuf("acc", [P, NCH * 1024], F32)
            scT = acc[0:8, 2304:4352].bitcast(BF16)
            gath = acc[:, 6400:7424].bitcast(BF16)
            mask = com.rowp[32:40, 2560:4608].bitcast(BF16)
            sc_done = emit_l1_scores(p, com, xn_sb, ln_done, ctl, scT,
                                     tag=f"s{idx}")
            p.dma("gpsimd", sc_in[idx].ap(), scT, waits=sc_done, inc="qCC")
            p.op("gpsimd", lambda g: g.collective_compute(
                "AllGather", ALU.bypass,
                replica_groups=[list(range(NCORES))],
                ins=[sc_in[idx].ap().opt()], outs=[sc_out[idx].ap().opt()]),
                waits=[("qCC", p.cnt("qCC"))], inc="ccS")
            sh = None
            if SPARSE_L1:
                sh = emit_l1s_pre(p, com, xn_sb, ln_done, sc_done, dscr,
                                  acc, tag=f"sp{idx}")
            p.dma("gpsimd",
                  gath_d.ap().rearrange("(e r h) t2 -> e r h t2",
                                        e=8, r=8),
                  sc_out[idx].ap().rearrange(
                      "(r e) (h t2) -> e r h t2", e=8, h=2),
                  waits=[("ccS", p.cnt("ccS"))], inc="qCC")
            p.dma("gpsimd", gath, gath_d.ap(),
                  waits=[("qCC", p.cnt("qCC"))], inc="qCC")
            gath_rdy = [("qCC", p.cnt("qCC"))]
            route_done = emit_l1_route(p, com, gath, gath_rdy, scT,
                                       sc_done, mask, acc, tag=f"r{idx}")
            if SPARSE_L1:
                emit_l1_ffn_sparse(p, com, xs, xn_sb, wts1, mask,
                                   route_done, mrow, rrow, (w11b, w12b),
                                   stash["12"], dscr, sh, acc,
                                   tag=f"sq{idx}")
            else:
                emit_l0_ffn(p, com, xs, xn_sb, ln_done, wts1, (mrow, rrow),
                            tag=f"q{idx}", gelu_tanh=True, mask_sb=mask,
                            mask_ready=route_done, extra_res=(mrow, rrow),
                            acc_ext=acc, wb16=(w11b, w12b),
                            stash_rdy=stash["12"])
        return body
    scoped(l1_ffn_scope(0))

    def s6():
        mrow, rrow, ln_done = emit_ln(p, com, xs, xn_sb, "l6", ())
        p.op("vector", lambda v: v.memset(sin_sb, 0.0), inc="vS5")
        zrdy = p.cnt("vS5")
        sfin, sdone = emit_l1_attn(p, com, xs, xn_sb, ln_done, wts1a,
                                   sin_sb, [("vS5", zrdy)], tag="t6",
                                   update_x=False)
        p.dma("gpsimd", s_in.ap(), sfin[:], waits=sdone, inc="qCC")
        p.op("gpsimd", lambda g: g.collective_compute(
            "AllGather", ALU.bypass,
            replica_groups=[list(range(NCORES))],
            ins=[s_in.ap().opt()], outs=[s_out.ap().opt()]),
            waits=[("qCC", p.cnt("qCC"))], inc="ccS")
        p.op("vector", lambda v: v.memset(sin_sb, 0.0),
             waits=[("ccS", p.cnt("ccS"))])
        gs = com.scr[3][64:128, :]
        for half in range(2):
            p.dma("gpsimd", gs.rearrange("p (r e) -> p r e", r=4),
                  s_out.ap()[half * 256:(half + 1) * 256, :].rearrange(
                      "(r p) e -> p r e", p=64),
                  waits=[("qCC", p.cnt("qCC"))] +
                  ([("vS5", p.cnt("vS5"))] if half == 1 else []),
                  inc="qCC")
            for r in range(4):
                p.op("vector",
                     lambda v, r=r, half=half: v.scalar_tensor_tensor(
                         sin_sb, gs[:, r * 1024:(r + 1) * 1024],
                         oselb[64:128, half * 4 + r:half * 4 + r + 1],
                         sin_sb, ALU.mult, ALU.add),
                     waits=[("qCC", p.cnt("qCC"))] if r == 0 else (),
                     inc="vS5" if r == 3 else None)
    scoped(s6)
    sin_rdy = p.cnt("vS5")

    def s7():
        sfin, done = emit_l1_attn(p, com, xs, xn_sb, [], wts1a,
                                  sin_sb, [("vS5", sin_rdy)], tag="t7",
                                  update_x=True, do_tanh=False)
    scoped(s7)

    scoped(l1_ffn_scope(1))

    p.begin_scope(p.snapshot())
    p.dma("gpsimd", xco, xs, inc="qO")
    p.op("gpsimd", lambda e: None, waits=[("qO", p.cnt("qO"))])
    return p.finalize()


def kernel_fused(**inputs):
    gi = lambda k: np.asarray(inputs[k], np.float32)
    shards = shard_x(inputs["L0_attn_x_seq"])
    per = []
    for i in range(NCORES):
        osel = np.zeros((1, 8), np.float32)
        if i % 2 == 1:
            osel[0, i - 1] = 1.0
        per.append(dict(
            xc=shards[i], cb=make_consts(), cf=make_constsf(), osel=osel,
            Wg_l=np.ascontiguousarray(gi("L0_ffn_Wg").reshape(8, 128, 8)
                                      .transpose(1, 0, 2).reshape(128, 64)),
            bg_t=np.ascontiguousarray(np.tile(gi("L0_ffn_bg"), 32)[None, :]),
            W01=gi("L0_ffn_W1"), W02=gi("L0_ffn_W2"),
            b01l=np.ascontiguousarray(gi("L0_ffn_b1").reshape(8, 32, 128)
                                      .transpose(2, 0, 1).reshape(128, 256)),
            b02l=np.ascontiguousarray(gi("L0_ffn_b2").reshape(8, 8, 128)
                                      .transpose(2, 0, 1).reshape(128, 64)),
            WdT=np.ascontiguousarray(gi("L0_attn_W_decay").T),
            WiT=np.ascontiguousarray(gi("L0_attn_W_input").T),
            bd=chunk_cols(gi("L0_attn_b_decay")),
            bi=chunk_cols(gi("L0_attn_b_input")),
            centT_l=np.ascontiguousarray(
                gi("L1_ffn_centroids").T.reshape(8, 128, 8)
                .transpose(1, 0, 2).reshape(128, 64)),
            lng=chunk_cols(gi("L1_ffn_ln_g")),
            lnb=chunk_cols(gi("L1_ffn_ln_b")),
            W11=gi("L1_ffn_w1"), W12=gi("L1_ffn_w2"),
            b11l=np.ascontiguousarray(gi("L1_ffn_b1").reshape(8, 32, 128)
                                      .transpose(2, 0, 1).reshape(128, 256)),
            b12l=np.ascontiguousarray(gi("L1_ffn_b2").reshape(8, 8, 128)
                                      .transpose(2, 0, 1).reshape(128, 64)),
            WT=np.ascontiguousarray(gi("L1_attn_W").T),
            grow=np.ascontiguousarray(gi("L1_attn_ln_gamma")[None, :]),
            brow=np.ascontiguousarray(gi("L1_attn_ln_beta")[None, :]),
            sca=np.array([[float(np.asarray(inputs["L1_attn_eta"])),
                           float(np.asarray(inputs["L1_attn_lam_logit"]))]],
                         np.float32)))
    nc = _get_nc("fused", build_fused)
    res = run_bass_kernel_spmd(nc, per, core_ids=list(range(NCORES)))
    return unshard_x([np.asarray(res.results[i]["xco"], np.float32)
                      for i in range(NCORES)])


# version 08
# speedup vs baseline: 1.2242x; 1.2242x over previous
import numpy as np
import ml_dtypes
from contextlib import ExitStack

import concourse.bass as bass
import concourse.mybir as mybir
from concourse.bass import broadcast_tensor_aps
from concourse.bass_core_rust import IndirectOffsetOnAxis
from concourse.bass_utils import run_bass_kernel_spmd

F32 = mybir.dt.float32
BF16 = mybir.dt.bfloat16
AF = mybir.ActivationFunctionType
ALU = mybir.AluOpType
AX = mybir.AxisListType

P = 128
D = 1024
NCH = 8
TLOC = 4096
NT = TLOC // 512
NTG = 4
USE_FUSED = True
SPARSE_L1 = True
SPARSE_L0 = True
ATTN_NCHK = None
ATTN_NG = None
ATTN_SQR = True
ATTN_APPLY = True
ATTN_GRAM = True
ATTN_FIN = True
ATTN_OY = True
ATTN_KS2 = 3
NCORES = 8
LN_EPS = 1e-5
BF = ml_dtypes.bfloat16

ENGS = ("sync", "tensor", "vector", "scalar", "gpsimd")


class Prog:
    def __init__(self, nc):
        self.nc = nc
        self.ops = {e: [] for e in ENGS}
        self.es = ExitStack()
        self.sems = {}
        self.uid = 0
        self.last_marker = {}

    def sem(self, name):
        if name not in self.sems:
            self.sems[name] = [self.es.enter_context(self.nc.semaphore(name)), 0]
        return self.sems[name]

    def cnt(self, name):
        return self.sem(name)[1]

    def bump(self, name, n):
        rec = self.sem(name)
        rec[1] += n
        return rec[1]

    def op(self, eng, fn, waits=(), inc=None, n=1, is_dma=False):
        waits = tuple((w, v) for w, v in waits)
        if getattr(self, "_scope_waits", None) and eng not in \
                getattr(self, "_scope_seen", set()):
            self._scope_seen.add(eng)
            waits = tuple(self._scope_waits) + waits
        for w, _ in waits:
            self.sem(w)
        chain = None
        if not is_dma and eng in ("vector", "gpsimd", "scalar"):
            lm = self.last_marker.get(eng)
            if lm is not None:
                waits = waits + (lm,)
            if inc is not None:
                self.bump(inc, n)
                self.last_marker[eng] = (inc, self.cnt(inc))
                inc_done = True
            else:
                chain = f"_cq_{eng}"
                self.bump(chain, 1)
                self.last_marker[eng] = (chain, self.cnt(chain))
            for w, _ in waits:
                self.sem(w)
        else:
            if inc is not None:
                self.bump(inc, n)
        sems = self.sems

        def run(e, fn=fn, waits=waits, inc=inc, n=n, chain=chain):
            for w, v in waits:
                e.wait_ge(sems[w][0], v)
            ins = fn(e)
            upd = inc if inc is not None else chain
            if upd is not None:
                un = n if inc is not None else 1
                if ins is None:
                    e.sem_inc(sems[upd][0], un)
                else:
                    ins.then_inc(sems[upd][0], un)

        self.ops[eng].append(run)
        return self.cnt(inc) if inc is not None else None

    def wait(self, eng, waits):
        self.op(eng, lambda e: None, waits=waits)

    def snapshot(self):
        return [(name, rec[1]) for name, rec in self.sems.items()
                if rec[1] > 0]

    def begin_scope(self, waits):
        self._scope_waits = list(waits)
        self._scope_seen = set()

    def scope_es(self):
        es = ExitStack()
        self._scope_stack = es
        return es

    def dma(self, eng, out, in_, waits=(), inc=None, **kw):
        return self.op(eng, lambda e: e.dma_start(out=out, in_=in_, **kw),
                       waits=waits, inc=inc, n=16, is_dma=True)

    def sbuf(self, name, shape, dtype):
        self.uid += 1
        es = getattr(self, "_scope_stack", None) or self.es
        return es.enter_context(
            self.nc.sbuf_tensor(f"{name}_{self.uid}", shape, dtype))

    def psum(self, name, shape, dtype=F32):
        self.uid += 1
        return self.es.enter_context(
            self.nc.psum_tensor(f"{name}_{self.uid}", shape, dtype))

    def finalize(self):
        ops = self.ops
        with self.nc.Block() as block:
            @block.sync
            def _(e):
                for f in ops["sync"]:
                    f(e)

            @block.tensor
            def _(e):
                for f in ops["tensor"]:
                    f(e)

            @block.vector
            def _(e):
                for f in ops["vector"]:
                    f(e)

            @block.scalar
            def _(e):
                for f in ops["scalar"]:
                    f(e)

            @block.gpsimd
            def _(e):
                for f in ops["gpsimd"]:
                    f(e)
        self.es.close()
        return self.nc


def xc3(xc_ap, tt=None):
    v = xc_ap.rearrange("(c p) t -> p c t", p=P)
    if tt is None:
        return v
    return v[:, :, tt * 512:(tt + 1) * 512]


def sb3(sb, tt=None, w=512):
    v = sb.rearrange("p (c t) -> p c t", c=NCH)
    if tt is None:
        return v
    return v[:, :, tt * w:(tt + 1) * w]



def make_consts():
    cb = np.zeros((160, 2688), dtype=BF)
    cb[0:128, 0:128] = np.eye(128, dtype=BF)
    cb[128, 0:128] = 1
    rs = np.zeros((8, 8, 128), dtype=BF)
    for e in range(8):
        rs[e, e, :] = 1
    cb[129:137, 0:1024] = rs.transpose(1, 0, 2).reshape(8, 1024)
    gsel = np.zeros((128, 8), dtype=BF)
    for e in range(8):
        gsel[16 * e:16 * e + 16, e] = 1
    cb[0:128, 1024:1032] = gsel
    cb[129:137, 1024:1152] = gsel.T
    gb = np.zeros((8, 128, 128), dtype=BF)
    for e in range(8):
        for j in range(16):
            for e2 in range(8):
                gb[e, 16 * e + j, 16 * e2 + j] = 1
    cb[0:128, 1152:2176] = gb.transpose(1, 0, 2).reshape(128, 1024)
    low = np.tril(np.ones((8, 8)), -1).T.astype(BF)
    cb[129:137, 2176:2184] = low
    ii = np.arange(128)
    cb[0:128, 2304:2432] = (ii[:, None] >= ii[None, :]).astype(BF)
    cb[0:128, 2432:2560] = (ii[:, None] > ii[None, :]).astype(BF)
    cb[0:128, 2560:2688] = (ii[:, None] < ii[None, :]).astype(BF)
    return cb


def make_constsf():
    cf = np.zeros((128, 532), dtype=np.float32)
    cf[:, 144:272] = np.eye(128, dtype=np.float32)
    ii = np.arange(128).astype(np.float32)
    d = ii[:, None] - ii[None, :]
    cf[:, 272:400] = np.where(d >= 0, d, 1e30)
    cf[:, 400:528] = np.where(-d >= 0, -d, 1e30)
    cf[:, 528] = ii + 1
    cf[:, 529] = 127 - ii
    cf[:, 530] = 128
    cf[:, 531] = ii
    gsel = np.zeros((128, 8), dtype=np.float32)
    for e in range(8):
        gsel[16 * e:16 * e + 16, e] = 1
    cf[:, 0:8] = gsel
    cf[0:8, 8:136] = gsel.T
    cf[0:8, 136:144] = np.eye(8, dtype=np.float32)
    return cf


class Common:
    def __init__(self, p: Prog, cb, cf=None):
        self.ident = p.sbuf("c_ident", [P, P], BF16)
        self.ones_row = p.sbuf("c_ones", [1, P], BF16)
        self.ones_f = p.sbuf("c_onesf", [P, 1], F32)
        self.ones_rf = p.sbuf("c_onesrf", [1, P], F32)

        self.gsel = p.sbuf("c_gsel", [P, 8], BF16)
        self.gselT = p.sbuf("c_gselT", [8, P], BF16)
        self.scr = [p.sbuf(f"scrf_{i}", [P, TLOC], F32) for i in range(4)]
        self.rowp = p.sbuf("rowp", [P, 2 * TLOC], F32)
        self.pp = [p.psum(f"bank{i}", [P, 512], F32) for i in range(8)]
        p.dma("sync", self.ident[:], cb[0:128, 0:128], inc="qC")
        p.dma("sync", self.ones_row[:], cb[128:129, 0:128], inc="qC")
        self.rowsel = self.rowp[32:40, 0:512].bitcast(BF16)
        p.dma("sync", self.rowsel, cb[129:137, 0:1024], inc="qC")
        p.dma("sync", self.gsel[:], cb[0:128, 1024:1032], inc="qC")
        p.dma("sync", self.gselT[:], cb[129:137, 1024:1152], inc="qC")
        self.lowT = self.rowp[32:40, 4608:4612].bitcast(BF16)
        self.cb = cb
        p.dma("sync", self.lowT, cb[129:137, 2176:2184], inc="qC")
        if cf is not None:
            self.cf = p.sbuf("c_cf", [P, 532], F32)
            p.dma("sync", self.cf[:], cf, inc="qC")
            self.gself = self.cf[:, 0:8]
            self.gselTf = self.cf[0:8, 8:136]
            self.oh8 = self.cf[0:8, 136:144]
            self.identf = self.cf[:, 144:272]
            self.idxdl = self.cf[:, 272:400]
            self.idxdlt = self.cf[:, 400:528]
            self.idx1 = self.cf[:, 528:529]
            self.revidx = self.cf[:, 529:530]
            self.c128 = self.cf[:, 530:531]
            self.rowidx = self.cf[:, 531:532]
            self.tri_incl = cb[0:128, 2304:2432]
            self.tri_strict = cb[0:128, 2432:2560]
        self.ready = [("qC", p.cnt("qC"))]
        p.op("vector", lambda v: v.memset(self.ones_f[:], 1.0),
             waits=self.ready)
        p.op("vector", lambda v: v.memset(self.ones_rf[:], 1.0), inc="vS")
        self.readyv = [("vS", p.cnt("vS"))]



def emit_ln(p: Prog, com: Common, xc_ap, xn_sb, tag, x_ready,
            fuse_double=False, gvec=None, bvec=None):
    mrow = com.rowp[0:1, 0:TLOC]
    rrow = com.rowp[0:1, TLOC:2 * TLOC]
    xt = [com.scr[0], com.scr[1]]
    xsq = com.scr[2]
    crow = xsq[0:1, 0:TLOC]
    Rrow = rrow if not fuse_double else crow
    pst = com.pp[0]
    pst2 = com.pp[1]
    pbc = com.pp[2]
    pbc2 = com.pp[3]

    sums = mrow
    ssq = rrow
    tS_after = []
    for tt in range(NT):
        tb = tt % 2
        ld = p.dma("sync", xt[tb][:], xc3(xc_ap, tt),
                   waits=(list(x_ready) if tt == 0 else
                          ([("tS", tS_after[tt - 2])] if tt >= 2 else [])),
                   inc=f"qL{tb}")
        p.op("vector",
             lambda v, tb=tb: v.tensor_tensor(
                 sb3(xsq), sb3(xt[tb]), sb3(xt[tb]), ALU.mult),
             waits=[(f"qL{tb}", ld)] + list(com.readyv) +
                   ([("tS", tS_after[tt - 1])] if tt >= 1 else []),
             inc="vS")
        for c in range(NCH):
            p.op("tensor",
                 lambda t, c=c, tb=tb: t.matmul(
                     pst[0:1, :], com.ones_f[:], sb3(xt[tb])[:, c, :],
                     start=(c == 0), stop=(c == NCH - 1)),
                 waits=([("vS", p.cnt("vS"))] +
                        ([("sS", p.cnt("sS"))] if tt > 0 else []))
                 if c == 0 else ())
        for c in range(NCH):
            p.op("tensor",
                 lambda t, c=c: t.matmul(
                     pst2[0:1, :], com.ones_f[:], sb3(xsq)[:, c, :],
                     start=(c == 0), stop=(c == NCH - 1)),
                 inc="tS" if c == NCH - 1 else None)
        tS_after.append(p.cnt("tS"))
        sl = slice(tt * 512, (tt + 1) * 512)
        p.op("scalar", lambda s, sl=sl: s.copy(sums[:, sl], pst[0:1, :]),
             waits=[("tS", p.cnt("tS"))])
        p.op("scalar", lambda s, sl=sl: s.copy(ssq[:, sl], pst2[0:1, :]),
             inc="sS")
    p.op("vector", lambda v: v.tensor_scalar_mul(mrow[:], sums[:], 1.0 / D),
         waits=[("sS", p.cnt("sS"))])
    p.op("vector", lambda v: v.tensor_mul(crow[:], mrow[:], mrow[:]))
    p.op("vector", lambda v: v.scalar_tensor_tensor(
        rrow[:], ssq[:], 1.0 / D, crow[:], ALU.mult, ALU.subtract))
    p.op("vector", lambda v: v.tensor_scalar_add(rrow[:], rrow[:], LN_EPS),
         inc="vS")
    p.op("scalar", lambda s: s.activation(rrow[:], rrow[:], AF.Sqrt),
         waits=[("vS", p.cnt("vS"))], inc="sS")
    p.op("vector", lambda v: v.reciprocal(rrow[:], rrow[:]),
         waits=[("sS", p.cnt("sS"))], inc="vS")
    if fuse_double:
        p.op("vector", lambda v: v.tensor_mul(Rrow[:], rrow[:], rrow[:]))
        p.op("vector", lambda v: v.tensor_scalar(
            Rrow[:], Rrow[:], -LN_EPS, 1.0 + LN_EPS, ALU.mult, ALU.add),
            inc="vS")
        p.op("scalar", lambda s: s.activation(Rrow[:], Rrow[:], AF.Sqrt),
             waits=[("vS", p.cnt("vS"))], inc="sS")
        p.op("vector", lambda v: v.reciprocal(Rrow[:], Rrow[:]),
             waits=[("sS", p.cnt("sS"))])
        p.op("vector", lambda v: v.tensor_mul(Rrow[:], Rrow[:], rrow[:]),
             inc="vS")

    v2_after = []
    for tt in range(NT):
        tb = tt % 2
        ld = p.dma("sync", xt[tb][:], xc3(xc_ap, tt),
                   waits=[("vS2", v2_after[tt - 2])] if tt >= 2 else
                         [("tS", p.cnt("tS"))],
                   inc=f"qL{tb}")
        sl = slice(tt * 512, (tt + 1) * 512)
        p.op("tensor", lambda t, sl=sl: t.matmul(
            pbc[:], com.ones_rf[:], mrow[:, sl], start=True, stop=True),
            waits=[("vS", p.cnt("vS"))] +
                  ([("vS2", v2_after[tt - 1])] if tt >= 1 else []))
        p.op("tensor", lambda t, sl=sl: t.matmul(
            pbc2[:], com.ones_rf[:], Rrow[:, sl], start=True, stop=True),
            inc="tS")
        mb3 = pbc.ap().rearrange("p (o t) -> p o t", o=1)
        rb3 = pbc2.ap().rearrange("p (o t) -> p o t", o=1)
        tmp3 = sb3(xt[tb])

        def sub_pass(v, tb=tb, mb3=mb3, tmp3=tmp3):
            a, b = broadcast_tensor_aps(sb3(xt[tb]), mb3)
            return v.tensor_tensor(tmp3, a, b, ALU.subtract)
        p.op("vector", sub_pass,
             waits=[("tS", p.cnt("tS")), (f"qL{tb}", ld)])

        if not fuse_double:
            def mul_pass(v, tt=tt, rb3=rb3, tmp3=tmp3):
                a, b = broadcast_tensor_aps(tmp3, rb3)
                return v.tensor_tensor(sb3(xn_sb, tt), a, b, ALU.mult)
            p.op("vector", mul_pass, inc="vS2")
        else:
            def mul_pass(v, rb3=rb3, tmp3=tmp3):
                a, b = broadcast_tensor_aps(tmp3, rb3)
                return v.tensor_tensor(tmp3, a, b, ALU.mult)
            p.op("vector", mul_pass, inc="vS2")
            for c in range(NCH):
                def aff(s, c=c, tt=tt, tmp3=tmp3):
                    return s.activation(
                        sb3(xn_sb, tt)[:, c, :], tmp3[:, c, :],
                        AF.Identity, bias=bvec[:, c:c + 1],
                        scale=gvec[:, c:c + 1])

                p.op("scalar", aff,
                     waits=[("vS2", p.cnt("vS2"))] if c == 0 else (),
                     inc="vS2" if c == NCH - 1 else None)
        v2_after.append(p.cnt("vS2"))
    done = [("vS2", p.cnt("vS2"))]
    return mrow, rrow, done



def emit_l0_attn(p: Prog, com: Common, xc_ap, xn_sb, xn_done, wts, carry_sb,
                 carry_ready, tag="l0a", update_x=True):
    wdt = p.sbuf(f"{tag}_wdt", [P, NCH * P], BF16)
    wit = p.sbuf(f"{tag}_wit", [P, NCH * P], BF16)
    bd = p.sbuf(f"{tag}_bd", [P, NCH], F32)
    bi = p.sbuf(f"{tag}_bi", [P, NCH], F32)
    a_sb = com.scr[0]
    b_sb = com.scr[1]
    s_sb = com.scr[2]
    xt = [com.scr[3], com.scr[3]]
    fin = p.sbuf(f"{tag}_fin", [P, NCH], F32)
    pdec = [com.pp[4], com.pp[5]]
    pinj = [com.pp[6], com.pp[7]]

    p.dma("sync", bd[:], wts["bd"], inc=f"qB{tag}")
    p.dma("sync", bi[:], wts["bi"], inc=f"qB{tag}")
    bias_rdy = p.cnt(f"qB{tag}")

    upd_after = []
    for ec in range(NCH):
        p.dma("gpsimd",
              wdt.rearrange("p (c k) -> p c k", c=NCH),
              wts["WdT"][:, ec * P:(ec + 1) * P].rearrange(
                  "(c p) k -> p c k", p=P),
              waits=[("tS", p.cnt("tS"))] if ec > 0 else (), inc="qW")
        p.dma("gpsimd",
              wit.rearrange("p (c k) -> p c k", c=NCH),
              wts["WiT"][:, ec * P:(ec + 1) * P].rearrange(
                  "(c p) k -> p c k", p=P),
              inc="qW")
        wrdy = p.cnt("qW")
        tb = ec % 2
        if update_x:
            xrdy = p.dma("sync", xt[0][:], xc_ap[ec * P:(ec + 1) * P, :],
                         waits=([("qX0", p.cnt("qX0"))] if ec >= 1
                                else list(xn_done)),
                         inc="qA0")
        for tt in range(NT):
            pb = tt % 2
            for c in range(NCH):
                p.op("tensor",
                     lambda t, c=c, tt=tt, pb=pb: t.matmul(
                         pdec[pb][:], wdt[:, c * P:(c + 1) * P],
                         sb3(xn_sb, tt)[:, c, :],
                         start=(c == 0), stop=(c == NCH - 1)),
                     waits=(list(xn_done) + [("qW", wrdy)] +
                            [("sS", p.cnt("sS"))])
                     if c == 0 else ())
            for c in range(NCH):
                p.op("tensor",
                     lambda t, c=c, tt=tt, pb=pb: t.matmul(
                         pinj[pb][:], wit[:, c * P:(c + 1) * P],
                         sb3(xn_sb, tt)[:, c, :],
                         start=(c == 0), stop=(c == NCH - 1)),
                     inc="tS" if c == NCH - 1 else None)
            sl = slice(tt * 512, (tt + 1) * 512)
            p.op("scalar",
                 lambda s, sl=sl, pb=pb, ec=ec: s.activation(
                     a_sb[:, sl], pdec[pb][:], AF.Sigmoid,
                     bias=bd[:, ec:ec + 1]),
                 waits=[("tS", p.cnt("tS")), (f"qB{tag}", bias_rdy),
                        ("vS3", upd_after[ec - 1] if ec > 0 else 0)])
            p.op("scalar",
                 lambda s, sl=sl, pb=pb, ec=ec: s.activation(
                     b_sb[:, sl], pinj[pb][:], AF.Identity,
                     bias=bi[:, ec:ec + 1]),
                 inc="sS")
        p.op("vector",
             lambda v, ec=ec: v.tensor_tensor_scan(
                 s_sb[:], a_sb[:], b_sb[:], carry_sb[:, ec:ec + 1],
                 ALU.mult, ALU.add),
             waits=[("sS", p.cnt("sS"))] + list(carry_ready), inc="vScn")
        p.op("vector",
             lambda v, ec=ec: v.tensor_copy(fin[:, ec:ec + 1],
                                            s_sb[:, TLOC - 1:TLOC]),
             waits=[("vScn", p.cnt("vScn"))])
        if update_x:
            p.op("vector",
                 lambda v: v.tensor_add(xt[0][:], xt[0][:], s_sb[:]),
                 waits=[("qA0", xrdy)], inc="vS3")
            upd_after.append(p.cnt("vS3"))
            p.dma("gpsimd", xc_ap[ec * P:(ec + 1) * P, :], xt[0][:],
                  waits=[("vS3", p.cnt("vS3"))], inc="qX0")
        else:
            p.op("vector", lambda v: v.engine_nop(), inc="vS3")
            upd_after.append(p.cnt("vS3"))
    done = ([("qX0", p.cnt("qX0"))] if update_x else []) +         [("vS3", p.cnt("vS3"))]
    return fin, done



def emit_l0_ffn(p: Prog, com: Common, xc_ap, xn_sb, xn_done, wts, ln_rows,
                tag="l0f", res_scale=0.5, gelu_tanh=False, mask_sb=None,
                mask_ready=(), extra_res=None, acc_ext=None, wb16=None,
                stash_rdy=0):
    T = TLOC
    if mask_sb is None:
        wg_sb = p.sbuf(f"{tag}_wg", [P, 8 * 8], F32)
        bgt_sb = p.sbuf(f"{tag}_bgt", [1, 256], BF16)
    if gelu_tanh:
        gtmp = p.sbuf(f"{tag}_gtmp", [P, 1024], F32)
        wb_sb = gtmp[:, 0:512].bitcast(BF16)
    b1sb = p.sbuf(f"{tag}_b1", [P, 8 * 32], F32)
    b2sb = p.sbuf(f"{tag}_b2", [P, 8 * 8], F32)
    acc = acc_ext if acc_ext is not None else p.sbuf(
        f"{tag}_acc", [P, NCH * 1024], F32)
    if gelu_tanh:
        cmb0 = gtmp[:, 512:1024]
        cmb = [cmb0, cmb0]
    gt = acc[:, 0:512]
    g32 = acc[:, 512:640]
    w_bf = acc[:, 640:768].bitcast(BF16)
    wT_sb = com.rowp[32:40, 512:2560].bitcast(BF16)
    w1t = [p.sbuf(f"{tag}_w1t{i}", [P, NCH * P], BF16) for i in range(2)]
    w2t = [p.sbuf(f"{tag}_w2t{i}", [P, 512], BF16) for i in range(2)]
    if not gelu_tanh:
        wb_sb = p.sbuf(f"{tag}_wb", [P, 1024], BF16)
    if not gelu_tanh:
        cmb0 = p.sbuf(f"{tag}_cmb0", [P, 512], F32)
        cmb = [cmb0, cmb0]
    xf0 = com.scr[0][:, 0:1024]
    xf = [xf0, xf0]
    pp = com.pp

    if mask_sb is None:
        p.dma("gpsimd", wg_sb[:], wts["Wg_l"], inc=f"qG{tag}")
        p.dma("gpsimd", bgt_sb[:], wts["bg_t"], inc=f"qG{tag}")
    p.dma("sync", b1sb[:], wts["b1l"], inc=f"qB{tag}")
    p.dma("sync", b2sb[:], wts["b2l"], inc=f"qB{tag}")
    small_rdy = p.cnt(f"qB{tag}")
    gate_w_rdy = p.cnt(f"qG{tag}")

    if mask_sb is None:
        mrow, rrow = ln_rows
        pl = pp[2]
        xg = com.scr[1]
        bgb_sb = acc[:, 768:1024]
        p.op("tensor", lambda t: t.matmul(
            pp[3][:, 0:256], com.ones_row[:], bgt_sb[:],
            start=True, stop=True),
            waits=[(f"qG{tag}", gate_w_rdy)] + list(xn_done) +
            list(com.ready), inc="tS")
        p.op("scalar", lambda s: s.copy(bgb_sb, pp[3][:, 0:256]),
             waits=[("tS", p.cnt("tS"))], inc="sS")
        bgb_done = p.cnt("sS")
        for tt in range(NT):
            ldg = p.dma("sync", xg[:], xc3(xc_ap, tt),
                        waits=[("tS", p.cnt("tS"))] + list(xn_done),
                        inc="qGX")
            sl = slice(tt * 512, (tt + 1) * 512)
            p.op("tensor", lambda t, sl=sl: t.matmul(
                pp[0][:], com.ones_rf[:], mrow[:, sl],
                start=True, stop=True),
                waits=[("vS4", p.cnt("vS4"))])
            p.op("tensor", lambda t, sl=sl: t.matmul(
                pp[1][:], com.ones_rf[:], rrow[:, sl],
                start=True, stop=True), inc="tS")
            mb3 = pp[0].ap().rearrange("p (o t) -> p o t", o=1)
            rb3 = pp[1].ap().rearrange("p (o t) -> p o t", o=1)

            def subp(v, mb3=mb3):
                a, b = broadcast_tensor_aps(sb3(xg), mb3)
                return v.tensor_tensor(sb3(xg), a, b, ALU.subtract)

            def mulp(v, rb3=rb3):
                a, b = broadcast_tensor_aps(sb3(xg), rb3)
                return v.tensor_tensor(sb3(xg), a, b, ALU.mult)
            p.op("vector", subp,
                 waits=[("tS", p.cnt("tS")), ("qGX", ldg)])
            p.op("vector", mulp, inc="vS4")
            for blk in range(4):
                for c in range(NCH):
                    p.op("tensor",
                         lambda t, tt=tt, blk=blk, c=c: t.matmul(
                             pl[:, (tt * 4 + blk) * 8:
                                (tt * 4 + blk) * 8 + 8],
                             sb3(xg)[:, c, blk * P:(blk + 1) * P],
                             wg_sb[:, c * 8:(c + 1) * 8],
                             start=(c == 0), stop=(c == NCH - 1)),
                         waits=[("vS4", p.cnt("vS4"))]
                         if blk == 0 and c == 0 else (),
                         inc="tS" if (blk == 3 and c == NCH - 1) else None)
        exps = gt[:, 0:256]
        probs = gt[:, 256:512]
        knock = gt[:, 0:256]
        wql = gt[:, 0:256]
        sums = g32[:, 0:32]
        rr = g32[:, 32:64]
        m1 = g32[:, 64:96]
        m2 = g32[:, 96:128]
        p.op("vector", lambda v: v.tensor_add(probs[:, :], pl[:, 0:256],
                                              bgb_sb),
             waits=[("tS", p.cnt("tS")), ("sS", bgb_done)], inc="vS4")
        p.op("scalar", lambda s: s.activation(exps[:, :], probs[:, :],
                                              AF.Exp),
             waits=[("vS4", p.cnt("vS4"))], inc="sS")
        gate_evac = p.cnt("sS")

        def r3(ap2):
            return ap2.rearrange("p (g f) -> p g f", g=32)

        def bc3(ap2):
            v = ap2.rearrange("p (g f) -> p g f", f=1)
            return v

        def tt_b(v, out, a, b, op):
            ab, bb = broadcast_tensor_aps(r3(a), bc3(b))
            return v.tensor_tensor(r3(out), ab, bb, op)

        p.op("vector", lambda v: v.reduce_sum(sums[:, :], r3(exps),
                                              axis=AX.X),
             waits=[("sS", p.cnt("sS"))])
        p.op("vector", lambda v: v.reciprocal(rr[:, :], sums[:, :]))
        p.op("vector", lambda v: tt_b(v, probs, exps, rr, ALU.mult))
        p.op("vector", lambda v: v.reduce_max(m1[:, :], r3(probs),
                                              axis=AX.X))
        p.op("vector", lambda v: tt_b(v, knock, probs, m1, ALU.is_ge))
        p.op("vector", lambda v: v.scalar_tensor_tensor(
            knock[:, :], knock[:, :], -2.0, probs[:, :], ALU.mult, ALU.add))
        p.op("vector", lambda v: v.reduce_max(m2[:, :], r3(knock),
                                              axis=AX.X))
        p.op("vector", lambda v: tt_b(v, wql, probs, m2, ALU.is_ge))
        p.op("vector", lambda v: v.tensor_mul(wql[:, :], wql[:, :],
                                              probs[:, :]))
        p.op("vector", lambda v: v.tensor_copy(w_bf[:, :], wql[:, :]),
             inc="vS4")
        for grp in range(8):
            for j in range(4):
                blk = grp * 4 + j
                p.op("tensor",
                     lambda t, blk=blk, j=j: t.matmul(
                         pp[3][0:8, j * P:(j + 1) * P],
                         w_bf[:, blk * 8:(blk + 1) * 8], com.ident[:],
                         start=True, stop=True),
                     waits=([("vS4", p.cnt("vS4"))] if grp == 0 and j == 0
                            else ([("sS", p.cnt("sS"))] if j == 0 else ())),
                     inc="tS" if j == 3 else None)
            p.op("scalar",
                 lambda s, grp=grp: s.copy(
                     wT_sb[0:8, grp * 512:(grp + 1) * 512], pp[3][0:8, :]),
                 waits=[("tS", p.cnt("tS"))], inc="sS")
        wT_rdy_s = p.cnt("sS")
        wT_src = wT_sb
    else:
        gate_evac = p.cnt("sS")
        wT_rdy_s = p.cnt("sS")
        wT_src = mask_sb

    h_bf = [sc.bitcast(BF16) for sc in com.scr[0:4]]

    def h_ap(hc, th):
        return h_bf[hc // 8][:, (hc % 8) * 1024 + th * 512:
                             (hc % 8) * 1024 + (th + 1) * 512]

    p.acc_dbg = acc
    use_hist = {"F0": 0, "F1": 0, "G0": 0, "G1": 0}
    for tg in range(NTG):
        p.op("gpsimd", lambda g: g.memset(acc[:, :], 0.0),
             waits=[("vX", p.cnt("vX")), ("sS", wT_rdy_s),
                    ("vS4", p.cnt("vS4")), ("qDB", p.cnt("qDB"))] +
             list(mask_ready), inc="pM")
        memset_cnt = p.cnt("pM")
        for e in range(8):
            cmb_prev = p.cnt("vC")
            sS_hist = {0: None, 1: None}
            for hc in range(32):
                hb = hc % 2
                if wb16 is None:
                    ldw = p.dma(
                        "gpsimd",
                        w1t[hb].rearrange("p (c k) -> p c k", c=NCH),
                        wts["W1"][e][:, hc * P:(hc + 1) * P].rearrange(
                            "(c p) k -> p c k", p=P),
                        waits=[("tS", use_hist[f"F{hb}"])]
                        if (tg, e, hc) > (0, 0, 1) else [],
                        inc=f"qF{hb}")
                    qf = f"qF{hb}"
                else:
                    ldw = p.dma(
                        "sync",
                        w1t[hb].rearrange("p (c k) -> p c k", c=NCH),
                        wb16[0][e][:, hc * P:(hc + 1) * P].rearrange(
                            "(c p) k -> p c k", p=P),
                        waits=[("tS", use_hist[f"F{hb}"]),
                               ("qStash", stash_rdy)]
                        if (tg, e, hc) > (0, 0, 1) else
                        [("qStash", stash_rdy)],
                        inc=f"qE{hb}")
                    qf = f"qE{hb}"
                for c in range(NCH):
                    for th in range(2):
                        w0 = []
                        if th == 0 and c == 0:
                            w0 = [(qf, ldw), (list(xn_done)[0])]
                            if sS_hist[hb] is not None:
                                w0.append(("sS", sS_hist[hb]))
                            if hc < 2:
                                w0 += [("vC", cmb_prev), ("sS", wT_rdy_s),
                                       ("vX", p.cnt("vX")),
                                       ("qXF0", p.cnt("qXF0")),
                                       ("sS", p.cnt("sS"))]
                        p.op("tensor",
                             lambda t, hc=hc, th=th, c=c, hb=hb, tg=tg:
                             t.matmul(
                                 pp[hb * 2 + th][:],
                                 w1t[hb][:, c * P:(c + 1) * P],
                                 xn_sb[:, c * T + tg * 1024 + th * 512:
                                       c * T + tg * 1024 + (th + 1) * 512],
                                 start=(c == 0), stop=(c == NCH - 1)),
                             waits=w0,
                             inc="tS" if (th == 1 and c == NCH - 1)
                             else None)
                use_hist[f"F{hb}"] = p.cnt("tS")
                for th in range(2):
                    if not gelu_tanh:
                        p.op("scalar",
                             lambda s, hc=hc, th=th, hb=hb, e=e:
                             s.activation(
                                 h_ap(hc, th), pp[hb * 2 + th][:],
                                 AF.Gelu, bias=b1sb[:, e * 32 + hc:
                                                   e * 32 + hc + 1]),
                             waits=([("tS", p.cnt("tS")),
                                     (f"qB{tag}", small_rdy)] if th == 0
                                    else ()),
                             inc="sS" if th == 1 else None)
                    else:
                        xb_ = gtmp[:, 0:512]
                        t2_ = cmb[0]
                        C0 = 0.7978845608028654
                        C1 = 0.044715
                        p.op("scalar",
                             lambda s, hc=hc, th=th, hb=hb, e=e, xb_=xb_:
                             s.activation(
                                 xb_, pp[hb * 2 + th][:],
                                 AF.Identity, bias=b1sb[:, e * 32 + hc:
                                                       e * 32 + hc + 1]),
                             waits=[("tS", p.cnt("tS")),
                                    (f"qB{tag}", small_rdy),
                                    ("vH", p.cnt("vH")),
                                    ("pA", p.cnt("pA"))],
                             inc="sS")
                        p.op("gpsimd",
                             lambda g, xb_=xb_, t2_=t2_: g.tensor_mul(
                                 t2_[:], xb_, xb_),
                             waits=[("sS", p.cnt("sS")),
                                    ("pA", p.cnt("pA"))], inc="pG")
                        p.op("vector",
                             lambda v, t2_=t2_: v.tensor_scalar(
                                 t2_[:], t2_[:], C1, 1.0, ALU.mult,
                                 ALU.add),
                             waits=[("pG", p.cnt("pG"))])
                        p.op("vector",
                             lambda v, xb_=xb_, t2_=t2_: v.tensor_mul(
                                 t2_[:], t2_[:], xb_), inc="vH")
                        p.op("scalar",
                             lambda s, t2_=t2_: s.activation(
                                 t2_[:], t2_[:], AF.Tanh, scale=C0),
                             waits=[("vH", p.cnt("vH"))], inc="sS")
                        p.op("vector",
                             lambda v, t2_=t2_: v.tensor_scalar(
                                 t2_[:], t2_[:], 0.5, 0.5, ALU.mult,
                                 ALU.add),
                             waits=[("sS", p.cnt("sS"))], inc="vG2")
                        p.op("gpsimd",
                             lambda g, hc=hc, th=th, xb_=xb_, t2_=t2_:
                             g.tensor_mul(h_ap(hc, th), t2_[:], xb_),
                             waits=[("vG2", p.cnt("vG2"))], inc="vH")
                sS_hist[hb] = p.cnt("sS")
            h_done = p.cnt("sS")
            for th in range(2):
                p.op("tensor",
                     lambda t, th=th, e=e, tg=tg: t.matmul(
                         pp[4][:], com.rowsel[:, e * P:(e + 1) * P],
                         wT_src[0:8, tg * 1024 + th * 512:
                                tg * 1024 + (th + 1) * 512],
                         start=True, stop=True),
                     waits=[("sS", p.cnt("sS")), ("vC", cmb_prev),
                            ("vH", p.cnt("vH"))] +
                     (list(mask_ready) if mask_sb is not None else [])
                     if th == 0 else [("sS", p.cnt("sS"))],
                     inc="tS")
                p.op("scalar",
                     lambda s, th=th: s.copy(
                         wb_sb[:, th * 512:(th + 1) * 512], pp[4][:]),
                     waits=[("tS", p.cnt("tS"))], inc="sS")
            wb_done = p.cnt("sS")
            for pas in range(2):
                for hc in range(32):
                    gb = hc % 2
                    if wb16 is None:
                        ldw2 = p.dma(
                            "gpsimd", w2t[gb][:],
                            wts["W2"][e][hc * P:(hc + 1) * P,
                                         pas * 512:(pas + 1) * 512],
                            waits=[("tS", use_hist[f"G{gb}"])]
                            if (tg, e, pas, hc) > (0, 0, 0, 1) else [],
                            inc=f"qG{gb}")
                        qn = f"qG{gb}"
                    else:
                        ldw2 = p.dma(
                            "sync", w2t[gb][:],
                            wb16[1][e][hc * P:(hc + 1) * P,
                                       pas * 512:(pas + 1) * 512],
                            waits=[("tS", use_hist[f"G{gb}"]),
                                   ("qStash", stash_rdy)]
                            if (tg, e, pas, hc) > (0, 0, 0, 1) else
                            [("qStash", stash_rdy)],
                            inc=f"qH{gb}")
                        qn = f"qH{gb}"
                    for dq in range(4):
                        for th in range(2):
                            w0 = []
                            if dq == 0 and th == 0:
                                w0 = [(qn, ldw2)]
                                if hc == 0:
                                    w0 += [("sS", wb_done),
                                           ("vC", p.cnt("vC")),
                                           ("vH", p.cnt("vH"))]
                            p.op("tensor",
                                 lambda t, hc=hc, dq=dq, th=th, gb=gb:
                                 t.matmul(
                                     pp[dq * 2 + th][:],
                                     w2t[gb][:, dq * P:(dq + 1) * P],
                                     h_ap(hc, th),
                                     start=(hc == 0), stop=(hc == 31)),
                                 waits=w0,
                                 inc="tS" if (dq == 3 and th == 1)
                                 else None)
                    use_hist[f"G{gb}"] = p.cnt("tS")
                mm_done = p.cnt("tS")
                for dq in range(4):
                    dc = pas * 4 + dq
                    for th in range(2):
                        cb = (dq * 2 + th) % 2
                        p.op("vector",
                             lambda v, dq=dq, th=th, cb=cb, e=e, dc=dc:
                             v.scalar_tensor_tensor(
                                 cmb[cb][:], pp[dq * 2 + th][:],
                                 b2sb[:, e * 8 + dc:e * 8 + dc + 1],
                                 wb_sb[:, th * 512:(th + 1) * 512],
                                 ALU.add, ALU.mult),
                             waits=[("tS", mm_done),
                                    ("pA", p.cnt("pA")),
                                    (f"qB{tag}", small_rdy)],
                             inc="vC")
                        p.op("gpsimd",
                             lambda g, dc=dc, th=th, cb=cb, tg=tg:
                             g.tensor_add(
                                 acc[:, dc * 1024 + th * 512:
                                     dc * 1024 + (th + 1) * 512],
                                 acc[:, dc * 1024 + th * 512:
                                     dc * 1024 + (th + 1) * 512],
                                 cmb[cb][:]),
                             waits=[("vC", p.cnt("vC")),
                                    ("pM", memset_cnt)], inc="pA")
        pool_done = p.cnt("pA")
        if getattr(p, "dbg_acc0", None) is not None and tg == 0:
            p.dma("gpsimd", p.dbg_acc0, acc[:],
                  waits=[("pA", pool_done)], inc="qDB")
        for c in range(NCH):
            xb = c % 2
            xb = 0
            ldx = p.dma("sync", xf[xb][:],
                        xc_ap[c * P:(c + 1) * P, tg * 1024:(tg + 1) * 1024],
                        waits=[("qXF0", p.cnt("qXF0")), ("pA", pool_done)],
                        inc="qLX0")
            if extra_res is None:
                p.op("vector",
                     lambda v, c=c, xb=xb, tg=tg: v.scalar_tensor_tensor(
                         xf[xb][:], acc[:, c * 1024:(c + 1) * 1024],
                         res_scale, xf[xb][:], ALU.mult, ALU.add),
                     waits=[("qLX0", ldx), ("pA", pool_done)],
                     inc="vX")
            else:
                mrow, rrow = extra_res
                if c == 0:
                    p.op("tensor",
                         lambda t, tg=tg: t.matmul(
                             pp[0][:], com.ones_rf[:],
                             mrow[:, tg * 1024:tg * 1024 + 512],
                             start=True, stop=True),
                         waits=[("pA", pool_done), ("vC", p.cnt("vC"))])
                    p.op("tensor",
                         lambda t, tg=tg: t.matmul(
                             pp[1][:], com.ones_rf[:],
                             mrow[:, tg * 1024 + 512:(tg + 1) * 1024],
                             start=True, stop=True))
                    p.op("tensor",
                         lambda t, tg=tg: t.matmul(
                             pp[2][:], com.ones_rf[:],
                             rrow[:, tg * 1024:tg * 1024 + 512],
                             start=True, stop=True))
                    p.op("tensor",
                         lambda t, tg=tg: t.matmul(
                             pp[3][:], com.ones_rf[:],
                             rrow[:, tg * 1024 + 512:(tg + 1) * 1024],
                             start=True, stop=True), inc="tS")
                for th in range(2):
                    p.op("vector",
                         lambda v, xb=xb, th=th: v.tensor_sub(
                             cmb[th][:],
                             xf[xb][:, th * 512:(th + 1) * 512],
                             pp[th][:]),
                         waits=[("tS", p.cnt("tS")), ("qLX0", ldx)])
                    p.op("vector",
                         lambda v, th=th: v.tensor_mul(
                             cmb[th][:], cmb[th][:], pp[2 + th][:]))
                    p.op("vector",
                         lambda v, c=c, th=th: v.tensor_add(
                             cmb[th][:], cmb[th][:],
                             acc[:, c * 1024 + th * 512:
                                 c * 1024 + (th + 1) * 512]))
                    p.op("vector",
                         lambda v, xb=xb, th=th: v.scalar_tensor_tensor(
                             xf[xb][:, th * 512:(th + 1) * 512],
                             cmb[th][:], res_scale,
                             xf[xb][:, th * 512:(th + 1) * 512],
                             ALU.mult, ALU.add),
                         inc="vX")
            p.dma("gpsimd",
                  xc_ap[c * P:(c + 1) * P, tg * 1024:(tg + 1) * 1024],
                  xf[xb][:], waits=[("vX", p.cnt("vX"))], inc="qXF0")
        vX_after_tg = p.cnt("vX")
    done = [("qXF0", p.cnt("qXF0"))]
    return done



def emit_l1_scores(p: Prog, com: Common, xn_sb, xn_done, centT_l_ap,
                   scT_sb, tag="l1s"):
    ct_sb = p.sbuf(f"{tag}_ct", [P, 64], BF16)
    p.dma("gpsimd", ct_sb[:], centT_l_ap, inc=f"qB{tag}")
    crdy = p.cnt(f"qB{tag}")
    pb = com.pp[2]
    for tt in range(NT):
        for c in range(NCH):
            p.op("tensor",
                 lambda t, tt=tt, c=c: t.matmul(
                     pb[0:8, :], ct_sb[:, c * 8:(c + 1) * 8],
                     xn_sb[:, c * TLOC + tt * 512:c * TLOC + (tt + 1) * 512],
                     start=(c == 0), stop=(c == NCH - 1)),
                 waits=(list(xn_done) + [(f"qB{tag}", crdy)]
                        + ([("sS", p.cnt("sS"))] if tt > 0 else []))
                 if c == 0 else (),
                 inc="tS" if c == NCH - 1 else None)
        p.op("scalar",
             lambda sc, tt=tt: sc.copy(scT_sb[0:8, tt * 512:(tt + 1) * 512],
                                       pb[0:8, :]),
             waits=[("tS", p.cnt("tS"))], inc="sS")
    return [("sS", p.cnt("sS"))]


def emit_l1_route(p: Prog, com: Common, gath_sb, gath_ready, scT_sb,
                  scT_ready, mask_sb, acc, tag="l1r", cap=4096, iters=34):
    pp = com.pp
    dump = acc[:, 1024:2048].bitcast(BF16)
    bis = acc[0:8, 2048:2112]
    lo = bis[:, 0:1]
    hi = bis[:, 1:2]
    mid = bis[:, 2:3]
    cnt8 = bis[:, 3:4]
    ge8 = bis[:, 4:5]
    t8 = bis[:, 5:6]
    thv = bis[:, 6:7]
    am8 = bis[:, 7:8]
    rowc = acc[:, 2112:2113]
    midb = acc[:, 2113:2114]
    arow = acc[0:1, 2120:2248]
    m_bf = acc[32:40, 2304:4352].bitcast(BF16)
    cum_bf = acc[32:40, 4352:6400].bitcast(BF16)
    gbsel = acc[:, 7424:7936].bitcast(BF16)
    p.dma("sync", gbsel, com.cb[0:128, 1152:2176], inc=f"qB{tag}")
    gb_rdy = p.cnt(f"qB{tag}")

    p.op("vector", lambda v: v.memset(thv[:], 0.0))
    p.op("vector", lambda v: v.reduce_max(rowc, gath_sb, axis=AX.X,
                                          apply_absolute_value=True),
         waits=list(gath_ready), inc="vR")
    p.op("tensor", lambda t: t.transpose(pp[3][0:1, 0:128], rowc,
                                         com.identf[:]),
         waits=[("vR", p.cnt("vR"))], inc="tS")
    p.op("scalar", lambda sc: sc.copy(arow, pp[3][0:1, 0:128]),
         waits=[("tS", p.cnt("tS"))], inc="sS")
    p.op("vector", lambda v: v.reduce_max(acc[0:1, 2248:2249], arow,
                                          axis=AX.X),
         waits=[("sS", p.cnt("sS"))])
    p.op("vector", lambda v: v.engine_nop(), inc="vR")
    p.op("tensor", lambda t: t.matmul(
        pp[4][0:8, 0:1], com.ones_rf[0:1, 0:8], acc[0:1, 2248:2249],
        start=True, stop=True), waits=[("vR", p.cnt("vR"))], inc="tS")
    p.op("scalar", lambda sc: sc.activation(
        am8[:], pp[4][0:8, 0:1], AF.Identity, scale=1.001),
        waits=[("tS", p.cnt("tS"))], inc="sS")
    amax_rdy_s = p.cnt("sS")

    for e in range(8):
        p.op("vector", lambda v: v.tensor_scalar_mul(lo[:], am8[:], -1.0),
             waits=[("sS", amax_rdy_s)])
        p.op("vector", lambda v: v.tensor_scalar_mul(hi[:], am8[:], 1.0))
        p.op("vector", lambda v: v.memset(mid[:], 0.0))
        p.op("vector", lambda v: v.memset(midb[:], 0.0), inc="vR")
        for it in range(iters):
            p.op("vector",
                 lambda v: v.tensor_scalar(dump[:, :], gath_sb[:, :],
                                           midb, 0.0, ALU.is_gt,
                                           op1=ALU.add, accum_out=rowc),
                 waits=[("vR", p.cnt("vR")), ("sS", p.cnt("sS"))],
                 inc="vR")
            p.op("tensor", lambda t: t.matmul(
                pp[4][0:8, 0:1], com.gself, rowc, start=True, stop=True),
                waits=[("vR", p.cnt("vR"))], inc="tS")
            p.op("vector", lambda v: v.tensor_copy(cnt8[:], pp[4][0:8, 0:1]),
                 waits=[("tS", p.cnt("tS"))], inc="sS")
            p.op("vector", lambda v: v.tensor_single_scalar(
                ge8[:], cnt8[:], float(cap), ALU.is_ge),
                waits=[("sS", p.cnt("sS"))])
            p.op("vector", lambda v: v.tensor_sub(t8[:], mid[:], lo[:]))
            p.op("vector", lambda v: v.tensor_mul(t8[:], t8[:], ge8[:]))
            p.op("vector", lambda v: v.tensor_add(lo[:], lo[:], t8[:]))
            p.op("vector", lambda v: v.tensor_sub(t8[:], hi[:], mid[:]))
            p.op("vector", lambda v: v.tensor_mul(t8[:], t8[:], ge8[:]))
            p.op("vector", lambda v: v.tensor_add(hi[:], mid[:], t8[:]))
            p.op("vector", lambda v: v.scalar_tensor_tensor(
                mid[:], lo[:], 0.5, hi[:], ALU.mult, ALU.bypass)
                if False else v.tensor_add(mid[:], lo[:], hi[:]))
            p.op("vector", lambda v: v.tensor_scalar_mul(mid[:], mid[:],
                                                         0.5), inc="vR")
            p.op("tensor", lambda t: t.matmul(
                pp[4][:, 0:1], com.gselTf, mid[:], start=True, stop=True),
                waits=[("vR", p.cnt("vR"))], inc="tS")
            p.op("vector", lambda v: v.tensor_copy(midb, pp[4][:, 0:1]),
                 waits=[("tS", p.cnt("tS"))], inc="sS")
        p.op("vector", lambda v, e=e: v.tensor_mul(
            t8[:], lo[:], com.oh8[:, e:e + 1]),
            waits=[("sS", p.cnt("sS"))])
        p.op("vector", lambda v: v.tensor_add(thv[:], thv[:], t8[:])
             if True else None)
        p.op("tensor", lambda t: t.matmul(
            pp[4][:, 0:1], com.gselTf, lo[:], start=True, stop=True),
            waits=[("sS", p.cnt("sS")), ("vR", p.cnt("vR"))], inc="tS")
        p.op("scalar", lambda sc: sc.copy(midb, pp[4][:, 0:1]),
             waits=[("tS", p.cnt("tS"))], inc="sS")
        p.op("vector",
             lambda v: v.tensor_scalar(dump[:, :], gath_sb[:, :], midb,
                                       None, ALU.is_gt),
             waits=[("sS", p.cnt("sS"))], inc="vR")
        for q in range(4):
            p.op("tensor",
                 lambda t, q=q, e=e: t.matmul(
                     pp[5][:, :], gbsel[:, e * P:(e + 1) * P],
                     dump[:, q * 512:(q + 1) * 512],
                     start=True, stop=True),
                 waits=[("vR", p.cnt("vR")), (f"qB{tag}", gb_rdy)] + (
                     [("vR2", p.cnt("vR2"))] if (e, q) > (0, 0) else []),
                 inc="tS")
            p.op("vector",
                 lambda v, q=q: v.scalar_tensor_tensor(
                     gath_sb[:, q * 512:(q + 1) * 512], pp[5][:, :],
                     -1e30, gath_sb[:, q * 512:(q + 1) * 512],
                     ALU.mult, ALU.add),
                 waits=[("tS", p.cnt("tS"))], inc="vR2")
    p.op("vector", lambda v: v.tensor_scalar(
        m_bf[:, :], scT_sb[:, :], thv, None, ALU.is_gt),
        waits=list(scT_ready) + [("vR2", p.cnt("vR2"))], inc="vR")
    for q in range(NT):
        p.op("tensor", lambda t, q=q: t.matmul(
            pp[6][0:8, :], com.lowT[:], m_bf[:, q * 512:(q + 1) * 512],
            start=True, stop=True),
            waits=[("vR", p.cnt("vR"))] + (
                [("sS", p.cnt("sS"))] if q > 0 else []),
            inc="tS")
        p.op("scalar", lambda sc, q=q: sc.copy(
            cum_bf[:, q * 512:(q + 1) * 512], pp[6][0:8, :]),
            waits=[("tS", p.cnt("tS"))], inc="sS")
    p.op("vector", lambda v: v.tensor_single_scalar(
        cum_bf[:, :], cum_bf[:, :], 0.5, ALU.is_lt),
        waits=[("sS", p.cnt("sS"))])
    p.op("vector", lambda v: v.tensor_mul(mask_sb[:, :], m_bf[:, :],
                                          cum_bf[:, :]), inc="vR")
    if getattr(p, "dbg_bis", None) is not None:
        p.dma("gpsimd", p.dbg_bis, acc[0:8, 2048:2112],
              waits=[("vR", p.cnt("vR"))], inc="qDB")
    return [("vR", p.cnt("vR"))]



def emit_l1_attn(p: Prog, com: Common, xc_ap, xn_sb, xn_done, wts, sin_sb,
                 sin_ready, tag="l1a", update_x=True, do_tanh=True):
    C = P
    NCHK = (TLOC // C) if ATTN_NCHK is None else ATTN_NCHK
    HG = 8
    NG = (16 // HG) if ATTN_NG is None else ATTN_NG

    wt_sb = p.sbuf(f"{tag}_wt", [P, NCH * 1024], BF16)
    sca_sb = p.sbuf(f"{tag}_sca", [1, 16], F32)
    tri_i = p.sbuf(f"{tag}_tri", [P, P], BF16)
    tri_s = p.sbuf(f"{tag}_trs", [P, P], BF16)
    tri_sT = p.sbuf(f"{tag}_trsT", [P, P], BF16)
    grow = p.sbuf(f"{tag}_gr", [1, 1024], F32)
    brow = p.sbuf(f"{tag}_br", [1, 1024], F32)
    gb_sb = p.sbuf(f"{tag}_gb", [P, 1024], F32)
    bb_sb = p.sbuf(f"{tag}_bb", [P, 1024], F32)
    mst = p.sbuf(f"{tag}_mst", [P, P], BF16)
    mstT = p.sbuf(f"{tag}_mstT", [P, P], BF16)
    oim = p.sbuf(f"{tag}_oim", [P, P], BF16)
    etmp = p.sbuf(f"{tag}_etmp", [P, P], F32)
    cols = p.sbuf(f"{tag}_cols", [P, 8], F32)
    lamvec = cols[:, 0:1]
    mblam = cols[:, 1:2]
    lamrev = cols[:, 2:3]
    betac = cols[:, 3:4]
    lnlamc = cols[:, 4:5]
    sc8 = p.sbuf(f"{tag}_sc8", [1, 16], F32)
    lam1 = sc8[:, 0:1]
    beta1 = sc8[:, 1:2]
    lnlam1 = sc8[:, 2:3]
    lamC1 = sc8[:, 3:4]
    one1 = sc8[:, 4:5]
    lamC64 = p.sbuf(f"{tag}_lc64", [64, 1], F32)
    s_all = p.sbuf(f"{tag}_sall", [64, 16 * 64], F32)
    s_bf = p.sbuf(f"{tag}_sbf", [64, 16 * 64], BF16)

    vraw = com.scr[1][:, 0:1024]
    vb = com.scr[1][:, 1024:2048]
    y_sb = com.scr[1][:, 2048:3072]
    xfrag = com.scr[1][:, 3072:4096]
    zf = com.scr[2][:, 0:512]
    yf = com.scr[2][:, 512:1024]
    osb = com.scr[2][:, 1024:1536]
    usb = com.scr[2][:, 1536:2048]
    lnm = com.scr[2][:, 3072:3073]
    lnr = com.scr[2][:, 3073:3074]
    lns = com.scr[2][:, 3074:3075]
    ysq = com.scr[2][:, 2048:3072]
    s0b = com.scr[0].bitcast(BF16)
    s3b = com.scr[3].bitcast(BF16)
    nst = [s0b[:, k * 1024:(k + 1) * 1024] for k in range(7)]
    mcur = s0b[:, 7168:8192]
    a4 = s3b[:, 0:1024]
    at4 = s3b[:, 1024:2048]
    oit4 = s3b[:, 2048:3072]
    kstage = s3b[0:64, 3072:4096]
    ubf = s3b[:, 4096:4608]
    zbf = s3b[:, 4608:5120]
    k2bf = s3b[:, 5120:5632]
    pp = com.pp

    p.dma("gpsimd", wt_sb.rearrange("p (c k) -> p c k", c=NCH),
          wts["WT"].rearrange("(c p) k -> p c k", p=P), inc=f"qW{tag}")
    p.dma("sync", sca_sb[:, 0:2], wts["sca"], inc=f"qB{tag}")
    p.dma("sync", tri_i[:], com.cb[0:128, 2304:2432], inc=f"qB{tag}")
    p.dma("sync", tri_s[:], com.cb[0:128, 2432:2560], inc=f"qB{tag}")
    p.dma("sync", tri_sT[:], com.cb[0:128, 2560:2688], inc=f"qB{tag}")
    p.dma("sync", grow[:], wts["grow"], inc=f"qB{tag}")
    p.dma("sync", brow[:], wts["brow"], inc=f"qB{tag}")
    ld_rdy = p.cnt(f"qB{tag}")
    ldw_rdy = p.cnt(f"qW{tag}")

    p.op("scalar", lambda sc: sc.activation(lam1, sca_sb[:, 1:2],
                                            AF.Sigmoid),
         waits=[(f"qB{tag}", ld_rdy)], inc="sS")
    p.op("vector", lambda v: v.memset(one1, 1.0),
         waits=[("sS", p.cnt("sS"))])
    p.op("vector", lambda v: v.tensor_sub(beta1, one1, lam1))
    p.op("vector", lambda v: v.tensor_mul(beta1, beta1, sca_sb[:, 0:1]))
    p.op("vector", lambda v: v.tensor_scalar_mul(beta1, beta1, 1.0 / 64),
         inc="vS5")
    p.op("scalar", lambda sc: sc.activation(lnlam1, lam1, AF.Ln),
         waits=[("vS5", p.cnt("vS5"))], inc="sS")
    p.op("tensor", lambda t: t.matmul(pp[0][0:128, 0:2],
                                      com.ones_rf[:], sc8[:, 1:3],
                                      start=True, stop=True),
         waits=[("sS", p.cnt("sS"))], inc="tS")
    p.op("scalar", lambda sc: sc.copy(cols[:, 5:7], pp[0][0:128, 0:2]),
         waits=[("tS", p.cnt("tS"))], inc="sS")
    p.op("vector", lambda v: v.tensor_copy(betac, cols[:, 5:6]),
         waits=[("sS", p.cnt("sS"))])
    p.op("vector", lambda v: v.tensor_copy(lnlamc, cols[:, 6:7]),
         inc="vS5")
    p.op("scalar", lambda sc: sc.activation(lamvec, com.idx1, AF.Exp,
                                            scale=lnlamc),
         waits=[("vS5", p.cnt("vS5"))])
    p.op("scalar", lambda sc: sc.activation(lamrev, com.revidx, AF.Exp,
                                            scale=lnlamc))
    p.op("scalar", lambda sc: sc.activation(cols[:, 7:8], com.c128, AF.Exp,
                                            scale=lnlamc), inc="sS")
    p.op("vector", lambda v: v.tensor_mul(mblam, cols[:, 5:6], lamvec),
         waits=[("sS", p.cnt("sS"))])
    p.op("vector", lambda v: v.tensor_scalar_mul(mblam, mblam, -1.0))
    p.op("vector", lambda v: v.tensor_copy(lamC64[:], cols[0:64, 7:8]),
         inc="vS5")
    p.op("scalar", lambda sc: sc.activation(etmp[:], com.idxdl,
                                            AF.Exp, scale=lnlamc),
         waits=[("vS5", p.cnt("vS5")), (f"qB{tag}", ld_rdy)], inc="sS")
    p.op("vector", lambda v: v.tensor_mul(mst[:], etmp[:], tri_s[:]),
         waits=[("sS", p.cnt("sS"))])
    p.op("vector", lambda v: v.tensor_scalar(mst[:], mst[:], betac, None,
                                             ALU.mult))
    p.op("vector", lambda v: v.tensor_mul(oim[:], etmp[:], tri_i[:]),
         inc="vS5")
    p.op("scalar", lambda sc: sc.activation(etmp[:], com.idxdlt,
                                            AF.Exp, scale=lnlamc),
         waits=[("vS5", p.cnt("vS5"))], inc="sS")
    p.op("vector", lambda v: v.tensor_mul(mstT[:], etmp[:], tri_sT[:]),
         waits=[("sS", p.cnt("sS"))])
    p.op("vector", lambda v: v.tensor_scalar(mstT[:], mstT[:], betac, None,
                                             ALU.mult), inc="vS5")
    for half in range(2):
        p.op("tensor", lambda t, half=half: t.matmul(
            pp[2][:, :], com.ones_rf[:], grow[:, half * 512:(half + 1) * 512],
            start=True, stop=True),
            waits=[("vS5", p.cnt("vS5")), ("sS", p.cnt("sS"))], inc="tS")
        p.op("scalar", lambda sc, half=half: sc.copy(
            gb_sb[:, half * 512:(half + 1) * 512], pp[2][:, :]),
            waits=[("tS", p.cnt("tS"))], inc="sS")
        p.op("tensor", lambda t, half=half: t.matmul(
            pp[2][:, :], com.ones_rf[:], brow[:, half * 512:(half + 1) * 512],
            start=True, stop=True),
            waits=[("sS", p.cnt("sS"))], inc="tS")
        p.op("scalar", lambda sc, half=half: sc.copy(
            bb_sb[:, half * 512:(half + 1) * 512], pp[2][:, :]),
            waits=[("tS", p.cnt("tS"))], inc="sS")
    sin_ap = sin_sb[:, :] if hasattr(sin_sb, "ap") else sin_sb
    p.op("vector", lambda v: v.tensor_copy(s_all[:, :], sin_ap),
         waits=list(sin_ready))
    p.op("vector", lambda v: v.tensor_copy(s_bf[:, :], sin_ap),
         inc="vS5")
    pre_done_v = p.cnt("vS5")
    pre_done_s = p.cnt("sS")

    for c in (range(NCH) if do_tanh else []):
        p.op("scalar",
             lambda sc, c=c: sc.activation(
                 xn_sb[:, c * TLOC:(c + 1) * TLOC],
                 xn_sb[:, c * TLOC:(c + 1) * TLOC], AF.Tanh),
             waits=list(xn_done) if c == 0 else (),
             inc="sS" if c == NCH - 1 else None)
    xa_done = p.cnt("sS")

    def kT(h, ck):
        c = h // 2
        b = 64 * (h % 2)
        return xn_sb[b:b + 64,
                     c * TLOC + ck * C:c * TLOC + (ck + 1) * C]

    for ck in range(NCHK):
        for half in range(2):
            for c in range(NCH):
                p.op("tensor",
                     lambda t, ck=ck, half=half, c=c: t.matmul(
                         pp[half][:, :],
                         xn_sb[:, c * TLOC + ck * C:c * TLOC + (ck + 1) * C],
                         wt_sb[:, c * 1024 + half * 512:
                               c * 1024 + (half + 1) * 512],
                         start=(c == 0), stop=(c == NCH - 1)),
                     waits=[("sS", p.cnt("sS")), ("vS5", pre_done_v),
                            (f"qW{tag}", ldw_rdy)]
                     if (half == 0 and c == 0) else (),
                     inc="tS" if (half == 1 and c == NCH - 1) else None)
        p.op("scalar", lambda sc: sc.copy(vraw[:, 0:512], pp[0][:, :]),
             waits=[("tS", p.cnt("tS")), ("vO", p.cnt("vO")),
                    ("vG", p.cnt("vG")), ("vY", p.cnt("vY"))])
        p.op("scalar", lambda sc: sc.copy(vraw[:, 512:1024], pp[1][:, :]))
        p.op("scalar", lambda sc: sc.activation(vb[:, 0:512], pp[0][:, :],
                                                AF.Identity, scale=betac))
        p.op("scalar", lambda sc: sc.activation(vb[:, 512:1024], pp[1][:, :],
                                                AF.Identity, scale=betac),
             inc="sS")
        v_done = p.cnt("sS")
        for g in range(NG):
            heads = range(g * HG, (g + 1) * HG)
            for j, h in enumerate(heads):
                p.op("vector",
                     lambda v, h=h, ck=ck, j=j: v.tensor_copy(
                         kstage[:, j * 128:(j + 1) * 128], kT(h, ck)),
                     waits=[("sS", v_done), ("tS", p.cnt("tS"))]
                     if j == 0 else (),
                     inc="vK" if j == HG - 1 else None)
            kst = p.cnt("vK")
            for j, h in (list(enumerate(heads)) if ATTN_GRAM else []):
                p.op("tensor",
                     lambda t, j=j: t.matmul(
                         pp[2 + j // 4][:, (j % 4) * 128:
                                        (j % 4 + 1) * 128],
                         kstage[:, j * 128:(j + 1) * 128],
                         kstage[:, j * 128:(j + 1) * 128],
                         start=True, stop=True),
                     waits=[("vK", kst), ("vG", p.cnt("vG")),
                            ("vY", p.cnt("vY"))]
                     if j == 0 else ())
            for j, h in (list(enumerate(heads)) if ATTN_GRAM else []):
                p.op("tensor",
                     lambda t, j=j, h=h: t.matmul(
                         pp[5][:, j * 64:(j + 1) * 64],
                         kstage[:, j * 128:(j + 1) * 128],
                         s_bf[:, h * 64:(h + 1) * 64],
                         start=True, stop=True),
                     waits=[("vS5", p.cnt("vS5")), ("vO", p.cnt("vO"))]
                     if j == 0 else (),
                     inc="tS" if j == HG - 1 else None)
            if not ATTN_GRAM:
                p.op("tensor", lambda t: None, inc="tS")
            gram_done = p.cnt("tS")
            def mk_mask(v, out, mask, hb):
                i0 = pp[2 + hb].ap().rearrange("p (g t) -> p g t", g=4)
                i1 = mask.rearrange("p (o t) -> p o t", o=1)
                a, b = broadcast_tensor_aps(i0, i1)
                return v.tensor_tensor(
                    out[:, hb * 512:(hb + 1) * 512].rearrange(
                        "p (g t) -> p g t", g=4), a, b, ALU.mult)
            if ATTN_GRAM:
                p.op("vector", lambda v: mk_mask(v, a4, mst[:], 0),
                     waits=[("tS", gram_done)])
                p.op("vector", lambda v: mk_mask(v, a4, mst[:], 1))
                p.op("vector", lambda v: mk_mask(v, at4, mstT[:], 0))
                p.op("vector", lambda v: mk_mask(v, at4, mstT[:], 1))
                p.op("vector", lambda v: mk_mask(v, oit4, oim[:], 0))
                p.op("vector", lambda v: mk_mask(v, oit4, oim[:], 1),
                     inc="vG")
            else:
                p.op("vector", lambda v: v.engine_nop(),
                     waits=[("tS", gram_done)], inc="vG")
            mask_done = p.cnt("vG")
            def mkY(v, g=g):
                return v.scalar_tensor_tensor(
                    yf[:, :], pp[5][:, 0:512], mblam,
                    vb[:, g * 512:(g + 1) * 512], ALU.mult, ALU.add)
            if not ATTN_GRAM:
                p.op("vector", lambda v: v.engine_nop(),
                     waits=[("tS", gram_done)], inc="vG")
            p.op("vector", (mkY if ATTN_GRAM else
                            (lambda v: v.engine_nop())),
                 waits=[("tS", gram_done)])
            p.op("vector", lambda v: v.tensor_copy(zf[:, :], yf[:, :]))
            p.op("vector", lambda v: v.tensor_copy(zbf[:, :], yf[:, :]),
                 inc="vG")
            p.op("vector", lambda v: v.tensor_copy(mcur[:, :], a4[:, :]),
                 waits=[("vG", p.cnt("vG"))])
            p.op("vector", lambda v: v.tensor_copy(nst[0][:, :], at4[:, :]),
                 inc="vG")
            for lev in (range(1, 7) if ATTN_SQR else []):
                for j in range(HG):
                    p.op("tensor",
                         lambda t, j=j, lev=lev: t.matmul(
                             pp[3 + j // 4][:, (j % 4) * 128:
                                            (j % 4 + 1) * 128],
                             nst[lev - 1][:, j * 128:(j + 1) * 128],
                             mcur[:, j * 128:(j + 1) * 128],
                             start=True, stop=True),
                         waits=([("vG", p.cnt("vG"))] +
                                ([("vS5", p.cnt("vS5"))] if lev == 1
                                 else [])) if j == 0 else ())
                for j in range(HG):
                    p.op("tensor",
                         lambda t, j=j, lev=lev: t.matmul(
                             pp[6 + j // 4][:, (j % 4) * 128:
                                            (j % 4 + 1) * 128],
                             mcur[:, j * 128:(j + 1) * 128],
                             nst[lev - 1][:, j * 128:(j + 1) * 128],
                             start=True, stop=True),
                         inc="tS" if j == HG - 1 else None)
                p.op("vector", lambda v: v.tensor_copy(
                    mcur[:, 0:512], pp[3][:, :]),
                    waits=[("tS", p.cnt("tS"))])
                p.op("vector", lambda v: v.tensor_copy(
                    mcur[:, 512:1024], pp[4][:, :]))
                p.op("vector", lambda v, lev=lev: v.tensor_copy(
                    nst[lev][:, 0:512], pp[6][:, :]))
                p.op("vector", lambda v, lev=lev: v.tensor_copy(
                    nst[lev][:, 512:1024], pp[7][:, :]), inc="vG")
            for lev in (range(6, 0, -1) if ATTN_APPLY else []):
                for j in range(HG):
                    p.op("tensor",
                         lambda t, j=j, lev=lev: t.matmul(
                             pp[3][:, j * 64:(j + 1) * 64],
                             nst[lev][:, j * 128:(j + 1) * 128],
                             zbf[:, j * 64:(j + 1) * 64],
                             start=True, stop=True),
                         waits=[("vG", p.cnt("vG"))] if j == 0 else (),
                         inc="tS" if j == HG - 1 else None)
                p.op("vector", lambda v: v.tensor_add(zf[:, :], zf[:, :],
                                                      pp[3][:, 0:512]),
                     waits=[("tS", p.cnt("tS"))])
                p.op("vector", lambda v: v.tensor_copy(zbf[:, :], zf[:, :]),
                     inc="vG")
            for j in (range(HG) if ATTN_FIN else []):
                p.op("tensor",
                     lambda t, j=j: t.matmul(
                         pp[3][:, j * 64:(j + 1) * 64],
                         nst[0][:, j * 128:(j + 1) * 128],
                         zbf[:, j * 64:(j + 1) * 64],
                         start=True, stop=True),
                     waits=[("vG", p.cnt("vG"))] if j == 0 else (),
                     inc="tS" if j == HG - 1 else None)
            if ATTN_FIN:
                p.op("vector", lambda v: v.tensor_sub(usb[:, :], zf[:, :],
                                                      pp[3][:, 0:512]),
                     waits=[("tS", p.cnt("tS"))])
            p.op("vector", lambda v: v.tensor_copy(ubf[:, :], usb[:, :]),
                 inc="vG")
            u_done = p.cnt("vG")
            if update_x and ATTN_OY:
                for j in range(HG):
                    p.op("tensor",
                         lambda t, j=j: t.matmul(
                             pp[6][:, j * 64:(j + 1) * 64],
                             oit4[:, j * 128:(j + 1) * 128],
                             ubf[:, j * 64:(j + 1) * 64],
                             start=True, stop=True),
                         waits=[("vG", u_done)] if j == 0 else (),
                         inc="tS" if j == HG - 1 else None)
                p.op("scalar", lambda sc: sc.copy(osb[:, :], pp[6][:, 0:512]),
                     waits=[("tS", p.cnt("tS"))], inc="sS")
                p.op("vector",
                     lambda v: v.scalar_tensor_tensor(
                         osb[:, :], pp[5][:, 0:512], lamvec, osb[:, :],
                         ALU.mult, ALU.add),
                     waits=[("sS", p.cnt("sS"))])
                p.op("vector",
                     lambda v, g=g: v.tensor_add(
                         y_sb[:, g * 512:(g + 1) * 512],
                         vraw[:, g * 512:(g + 1) * 512], osb[:, :]),
                     inc="vO")
            else:
                p.op("vector", lambda v: v.engine_nop(), inc="vO")
            for j, h in (list(enumerate(heads)) if ATTN_KS2 >= 1 else []):
                b = 64 * (h % 2)
                p.op("tensor",
                     lambda t, j=j, h=h, b=b, ck=ck: t.matmul(
                         (pp[7] if b == 0 else pp[6])[:, j * 64:(j + 1) * 64],
                         kT(h, ck), com.ident[b:b + 64, b:b + 64],
                         start=True, stop=True),
                     waits=[("vO", p.cnt("vO")), ("sS", p.cnt("sS"))]
                     if j == 0 else (),
                     inc="tS" if j == HG - 1 else None)
            if not ATTN_KS2:
                p.op("vector", lambda v: v.engine_nop(), inc="vS5")
                continue
            for j, h in enumerate(heads):
                b = 64 * (h % 2)
                p.op("scalar",
                     lambda sc, j=j, b=b: sc.activation(
                         k2bf[:, j * 64:(j + 1) * 64],
                         (pp[7] if b == 0 else pp[6])[:, j * 64:(j + 1) * 64],
                         AF.Identity, scale=lamrev),
                     waits=[("tS", p.cnt("tS"))] if j == 0 else (),
                     inc="sS" if j == HG - 1 else None)
            k2_done = p.cnt("sS")
            for j, h in (list(enumerate(heads)) if ATTN_KS2 >= 3 else []):
                p.op("tensor",
                     lambda t, j=j: t.matmul(
                         pp[7][0:64, j * 64:(j + 1) * 64],
                         k2bf[:, j * 64:(j + 1) * 64],
                         ubf[:, j * 64:(j + 1) * 64],
                         start=True, stop=True),
                     waits=[("sS", k2_done)] if j == 0 else (),
                     inc="tS" if j == HG - 1 else None)
            if ATTN_KS2 >= 3:
                p.op("vector",
                     lambda v, g=g: v.scalar_tensor_tensor(
                         s_all[:, g * 512:(g + 1) * 512],
                         s_all[:, g * 512:(g + 1) * 512], lamC64[:],
                         pp[7][0:64, 0:512], ALU.mult, ALU.add),
                     waits=[("tS", p.cnt("tS"))])
            p.op("vector",
                 lambda v, g=g: v.tensor_copy(
                     s_bf[:, g * 512:(g + 1) * 512],
                     s_all[:, g * 512:(g + 1) * 512]), inc="vS5")
        if update_x:
            yl_prev = p.cnt("vY")
            p.op("vector", lambda v: v.reduce_sum(lnm, y_sb, axis=AX.X),
                 waits=[("vO", p.cnt("vO"))])
            p.op("vector", lambda v: v.tensor_scalar_mul(lnm, lnm, 1.0 / D))
            p.op("vector", lambda v: v.tensor_mul(ysq, y_sb, y_sb))
            p.op("vector", lambda v: v.reduce_sum(lns, ysq, axis=AX.X))
            p.op("vector", lambda v: v.tensor_mul(lnr, lnm, lnm))
            p.op("vector", lambda v: v.scalar_tensor_tensor(
                lnr, lns, 1.0 / D, lnr, ALU.mult, ALU.subtract))
            p.op("vector", lambda v: v.tensor_scalar_add(lnr, lnr, LN_EPS),
                 inc="vY")
            p.op("scalar", lambda sc: sc.activation(lnr, lnr, AF.Sqrt),
                 waits=[("vY", p.cnt("vY"))], inc="sS")
            p.op("vector", lambda v: v.reciprocal(lnr, lnr),
                 waits=[("sS", p.cnt("sS"))])
            p.op("vector", lambda v: v.tensor_scalar(
                ysq, y_sb, lnm, lnr, ALU.subtract, ALU.mult))
            p.op("vector", lambda v: v.tensor_mul(ysq, ysq, gb_sb[:, :]))
            p.op("vector", lambda v: v.tensor_add(ysq, ysq, bb_sb[:, :]),
                 inc="vY")
            yn_done = p.cnt("vY")
            ldx = p.dma("sync", xfrag.rearrange("p (c k) -> p c k", c=NCH),
                        xc_ap[:, ck * C:(ck + 1) * C].rearrange(
                            "(c p) k -> p c k", p=P),
                        waits=[("vY", yl_prev), ("qAXW", p.cnt("qAXW"))] +
                        list(xn_done),
                        inc="qAX")
            for c in range(NCH):
                p.op("tensor",
                     lambda t, c=c: t.transpose(
                         pp[2 if c % 2 == 0 else 3][:, 0:128],
                         ysq[:, c * 128:(c + 1) * 128], com.identf[:]),
                     waits=([("vY", yn_done), ("vG", p.cnt("vG")),
                             ("tS", p.cnt("tS"))] if c <= 1 else
                            [("vY", p.cnt("vY"))]),
                     inc="tS")
                p.op("vector",
                     lambda v, c=c: v.tensor_add(
                         xfrag[:, c * 128:(c + 1) * 128],
                         xfrag[:, c * 128:(c + 1) * 128],
                         pp[2 if c % 2 == 0 else 3][:, 0:128]),
                     waits=[("tS", p.cnt("tS")), ("qAX", ldx)],
                     inc="vY")
            p.dma("gpsimd", xc_ap[:, ck * C:(ck + 1) * C].rearrange(
                "(c p) k -> p c k", p=P),
                xfrag.rearrange("p (c k) -> p c k", c=NCH),
                waits=[("vY", p.cnt("vY"))], inc="qAXW")
        else:
            p.op("vector", lambda v: v.engine_nop(), inc="vY")
    p.op("vector", lambda v: v.engine_nop(), inc="vS5")
    done = ([("qAXW", p.cnt("qAXW"))] if update_x else []) + \
        [("vS5", p.cnt("vS5"))]
    return s_all, done


def one_bc(com):
    return com.ones_rf



def chunk_cols(v):
    return np.ascontiguousarray(np.asarray(v, np.float32).reshape(NCH, P).T)


def shard_x(x):
    xt = np.asarray(x, dtype=np.float32).reshape(-1, D)
    return [np.ascontiguousarray(xt[i * TLOC:(i + 1) * TLOC].T)
            for i in range(NCORES)]


def unshard_x(parts):
    xt = np.concatenate([np.asarray(pp, np.float32).T for pp in parts], axis=0)
    return np.ascontiguousarray(xt.reshape(4, 8192, D))



CAP = 768
CT = 8 * CAP
I32 = mybir.dt.int32


def emit_l1s_pre(p: Prog, com: Common, xn_sb, xn_done, sc_done, dscr, acc,
                 tag):
    sh = {}
    gtmp = p.sbuf(f"{tag}_gtmp", [P, 1024], F32)
    zbuf = acc[:, 0:1024]
    sh["gtmp"] = gtmp
    sh["zbuf"] = zbuf
    xn_tm = dscr["xn_tm"]
    gtb = gtmp.bitcast(BF16)
    p.op("vector", lambda v: v.memset(gtmp[0:49, 0:128], float(TLOC)),
         waits=list(com.ready), inc="vPz")
    p.op("vector", lambda v: v.tensor_scalar(
        gtmp[0:49, 0:128], gtmp[0:49, 0:128], com.rowidx[0:49, :], None,
        ALU.add), inc="vPz")
    p.op("vector", lambda v: v.tensor_copy(
        gtmp[0:49, 128:256].bitcast(I32), gtmp[0:49, 0:128]), inc="vPz")
    p.dma("sync", dscr["gidx"].rearrange("(r k) o -> r (k o)", k=P),
          gtmp[0:49, 128:256].bitcast(I32),
          waits=[("vPz", p.cnt("vPz"))], inc="qPF")
    sh["qPF"] = p.cnt("qPF")
    p.op("vector", lambda v: v.memset(zbuf[:], 0.0), inc="vPz")
    sh["vPz"] = p.cnt("vPz")
    ev_hist = [0, 0]
    qtm_hist = [0, 0]
    for tt in range(32):
        half = tt % 2
        bkb = com.pp[half].ap().bitcast(BF16)
        for c in range(NCH):
            w0 = []
            if c == 0:
                w0 = [("sT", ev_hist[half])]
                if tt == 0:
                    w0 += list(xn_done)
            p.op("tensor",
                 lambda t, tt=tt, c=c, bkb=bkb: t.transpose(
                     bkb[:, c * P:(c + 1) * P],
                     xn_sb[:, c * TLOC + tt * P:c * TLOC + (tt + 1) * P],
                     com.ident[:]),
                 waits=w0, inc="tT" if c == NCH - 1 else None)
        dst = gtb[:, half * 1024:(half + 1) * 1024]
        wv = [("tT", p.cnt("tT")), (f"qTM{half}", qtm_hist[half])]
        if tt == 0:
            wv.append(("qPF", sh["qPF"]))
        p.op("scalar",
             lambda s, dst=dst, bkb=bkb: s.copy(dst[:], bkb[:]),
             waits=wv, inc="sT")
        ev_hist[half] = p.cnt("sT")
        p.dma("sync", xn_tm[tt * P:(tt + 1) * P, :], dst,
              waits=[("sT", ev_hist[half])], inc=f"qTM{half}")
        qtm_hist[half] = p.cnt(f"qTM{half}")
    p.dma("sync", xn_tm[TLOC:TLOC + P, :], zbuf.bitcast(BF16)[:, 0:1024],
          waits=[("vPz", sh["vPz"])], inc="qTMz")
    sh["qTM"] = [("qTM0", qtm_hist[0]), ("qTM1", qtm_hist[1]),
                 ("qTMz", p.cnt("qTMz"))]
    sh["sT"] = p.cnt("sT")
    sh["tT"] = p.cnt("tT")
    return sh


def emit_l1_ffn_sparse(p: Prog, com: Common, xc_ap, xn_sb, wts, mask_sb,
                       mask_ready, mrow, rrow, wb16, stash_rdy, dscr, sh,
                       acc, tag):
    pp = com.pp
    xn_tm, xdelta, gidx_d = dscr["xn_tm"], dscr["xdelta"], dscr["gidx"]
    gtmp, zbuf = sh["gtmp"], sh["zbuf"]
    gtb = gtmp.bitcast(BF16)
    scr3 = com.scr[3]
    b1sb = p.sbuf(f"{tag}_b1", [P, 256], F32)
    b2sb = p.sbuf(f"{tag}_b2", [P, 64], F32)
    w1t = [p.sbuf(f"{tag}_w1t{i}", [P, NCH * P], BF16) for i in range(2)]
    w2t = [p.sbuf(f"{tag}_w2t{i}", [P, 256], BF16) for i in range(2)]
    gidx_sb = p.sbuf(f"{tag}_gi", [P, 49], I32)
    p.dma("sync", b1sb[:], wts["b1l"], inc=f"qB{tag}")
    p.dma("sync", b2sb[:], wts["b2l"], inc=f"qB{tag}")
    small_rdy = p.cnt(f"qB{tag}")

    for zt in range(33):
        p.dma("scalar", xdelta[zt * P:(zt + 1) * P, :], zbuf[:],
              waits=[("vPz", sh["vPz"])] if zt == 0 else (), inc="qZ")
    qz_total = p.cnt("qZ")

    ones8 = acc[32:40, 0:4096]
    cums = acc[32:40, 4096:8192]
    keep = ones8
    ecap = scr3[32:40, 2048:2049]
    srow = scr3[0:1, 0:512]
    arow = scr3[0:1, 512:1024]
    dtmp = scr3[0:1, 1024:1152]
    slotf = acc[:, 1024:1056]
    sloti = acc[:, 1056:1088].bitcast(I32)
    tokf = acc[:, 1088:1120]
    toki = acc[:, 1120:1152].bitcast(I32)

    p.op("vector", lambda v: v.memset(ones8[:], 1.0),
         waits=list(mask_ready) + [("qZ", qz_total)] + sh["qTM"],
         inc="vI")
    p.op("vector", lambda v: v.tensor_tensor_scan(
        cums[:], ones8[:], mask_sb[:, :], 0.0, ALU.mult, ALU.add), inc="vI")
    p.op("vector", lambda v: v.tensor_single_scalar(
        keep[:], cums[:], CAP + 0.5, ALU.is_lt), inc="vI")
    p.op("vector", lambda v: v.tensor_mul(keep[:], keep[:], mask_sb[:, :]),
         inc="vI")
    p.op("vector", lambda v: v.tensor_scalar(
        ecap, com.rowidx[32:40, :], -32.0, float(CAP), ALU.add, ALU.mult))
    p.op("vector", lambda v: v.tensor_scalar(
        cums[:], cums[:], ecap, -1.0, ALU.add, ALU.add))
    p.op("vector", lambda v: v.tensor_mul(cums[:], cums[:], keep[:]),
         inc="vI")
    for q in range(NT):
        p.op("tensor", lambda t, q=q: t.matmul(
            pp[4][0:1, 0:512], com.ones_f[32:40, :],
            cums[:, q * 512:(q + 1) * 512], start=True, stop=True),
            waits=[("vI", p.cnt("vI"))] + (
                [("vI2", p.cnt("vI2"))] if q > 0 else []))
        p.op("tensor", lambda t, q=q: t.matmul(
            pp[5][0:1, 0:512], com.ones_f[32:40, :],
            keep[:, q * 512:(q + 1) * 512], start=True, stop=True),
            inc="tS")
        mmq = p.cnt("tS")
        p.op("vector", lambda v: v.tensor_copy(srow, pp[4][0:1, 0:512]),
             waits=[("tS", mmq)])
        p.op("vector", lambda v: v.tensor_copy(arow, pp[5][0:1, 0:512]))
        p.op("vector", lambda v: v.scalar_tensor_tensor(
            srow, arow, -float(CT), srow, ALU.mult, ALU.add))
        p.op("vector", lambda v: v.tensor_single_scalar(
            srow, srow, float(CT), ALU.add), inc="vI2")
        for b4 in range(4):
            blk = q * 4 + b4
            sb4 = srow[:, b4 * P:(b4 + 1) * P]
            p.op("vector", lambda v, sb4=sb4: v.tensor_single_scalar(
                dtmp, sb4, float(CT) - 0.5, ALU.is_ge))
            p.op("vector", lambda v, sb4=sb4, blk=blk:
                 v.scalar_tensor_tensor(
                     sb4, dtmp, float(blk), sb4, ALU.mult, ALU.add),
                 inc="vI2")
            p.op("tensor", lambda t, q=q, b4=b4, sb4=sb4: t.matmul(
                pp[3][:, q * 4 + b4:q * 4 + b4 + 1],
                sb4, com.ones_rf[0:1, 0:1],
                start=True, stop=True),
                waits=[("vI2", p.cnt("vI2"))],
                inc="tS" if b4 == 3 else None)
    p.op("vector", lambda v: v.tensor_copy(slotf[:, 0:32], pp[3][:, 0:32]),
         waits=[("tS", p.cnt("tS"))])
    p.op("vector", lambda v: v.tensor_copy(sloti[:, 0:32], slotf[:, 0:32]),
         inc="vI")
    for blk in range(32):
        p.op("vector", lambda v, blk=blk: v.tensor_single_scalar(
            tokf[:, blk:blk + 1], com.rowidx, float(blk * P), ALU.add))
    p.op("vector", lambda v: v.tensor_copy(toki[:, 0:32], tokf[:, 0:32]),
         inc="vI")
    idx_rdy = p.cnt("vI")
    for blk in range(32):
        p.op("gpsimd", lambda g, blk=blk: g.indirect_dma_start(
            out=gidx_d[:, :],
            out_offset=IndirectOffsetOnAxis(sloti[:, blk:blk + 1], 0),
            in_=toki[:, blk:blk + 1], in_offset=None),
            waits=[("qPF", sh["qPF"]), ("vI", idx_rdy)] if blk == 0 else (),
            inc="qSC", n=16, is_dma=True)
    sc_total = p.cnt("qSC")
    for g in range(48):
        p.dma("sync", gidx_sb[:, g:g + 1], gidx_d[g * P:(g + 1) * P, :],
              waits=[("qSC", sc_total)] if g == 0 else (), inc="qGI")
    ld_gi = p.cnt("qGI")

    s2b = com.scr[2].bitcast(BF16)
    xeT3 = s2b[:, 0:6144].rearrange("p (c s) -> p c s", c=NCH)

    def outT(dc):
        return com.scr[dc // 4][:, (dc % 4) * 768:(dc % 4) * 768 + 768]

    def hsp(hc):
        return xn_sb[:, hc * 1024:hc * 1024 + 768]

    otm = [scr3[:, 0:1024], scr3[:, 1024:2048]]
    xb_ = scr3[:, 2048:2560]
    t2_ = scr3[:, 2560:3072]
    C0 = 0.7978845608028654
    C1 = 0.044715

    use_hist = {"F0": 0, "F1": 0, "G0": 0, "G1": 0}
    gbuf_hist = [0, 0]
    tin_ev_hist = [0, 0]
    ga_hist = {}
    mm1_last = 0
    outT_free = 0
    otm_free = [0, 0]
    evac_all = 0
    otm_ev_prev = 0
    first_scat = True

    def emit_gather(gg):
        ee, jj = divmod(gg, 6)
        half = jj % 2
        w0 = [("tS", gbuf_hist[half])]
        if gg < 2:
            w0 += [("qGI", ld_gi), ("qSC", sc_total)] + sh["qTM"]
        p.op("gpsimd", lambda g, gg=gg, half=half:
             g.indirect_dma_start(
                 out=gtb[:, half * 1024:(half + 1) * 1024],
                 out_offset=None, in_=xn_tm[:, :],
                 in_offset=IndirectOffsetOnAxis(
                     gidx_sb[:, gg:gg + 1], 0)),
             waits=w0, inc=f"qGA{half}", n=16, is_dma=True)
        ga_hist[(ee, jj)] = p.cnt(f"qGA{half}")

    emit_gather(0)
    emit_gather(1)
    for e in range(8):
        for j in range(6):
            half = j % 2
            bkb = pp[half].ap().bitcast(BF16)
            for c in range(NCH):
                w0 = []
                if c == 0:
                    w0 = [(f"qGA{half}", ga_hist[(e, j)]),
                          ("vT", tin_ev_hist[half])]
                    if j < 2:
                        w0.append(("tS", mm1_last))
                    if e == 0 and j < 2:
                        w0 += [("sT", sh["sT"]), ("vI2", p.cnt("vI2")),
                               ("vI", idx_rdy)]
                p.op("tensor",
                     lambda t, c=c, bkb=bkb, half=half: t.transpose(
                         bkb[:, c * P:(c + 1) * P],
                         gtb[:, half * 1024 + c * P:
                             half * 1024 + (c + 1) * P],
                         com.ident[:]),
                     waits=w0, inc="tS" if c == NCH - 1 else None)
            gbuf_hist[half] = p.cnt("tS")
            p.op("vector", lambda v, j=j, bkb=bkb: v.tensor_copy(
                xeT3[:, :, j * P:(j + 1) * P],
                bkb.rearrange("p (c s) -> p c s", c=NCH)),
                waits=[("tS", p.cnt("tS"))], inc="vT")
            tin_ev_hist[half] = p.cnt("vT")
            if e * 6 + j + 2 < 48:
                emit_gather(e * 6 + j + 2)
        xe_rdy = p.cnt("vT")

        sS_hist = {0: None, 1: None}
        for hc in range(32):
            hb = hc % 2
            ldw = p.dma("sync",
                        w1t[hb].rearrange("p (c k) -> p c k", c=NCH),
                        wb16[0][e][:, hc * P:(hc + 1) * P].rearrange(
                            "(c p) k -> p c k", p=P),
                        waits=[("tS", use_hist[f"F{hb}"]),
                               ("qStash", stash_rdy)], inc=f"qE{hb}")
            for c in range(NCH):
                for ti, (o0, n0) in enumerate(((0, 512), (512, 256))):
                    w0 = []
                    if c == 0 and ti == 0:
                        w0 = [(f"qE{hb}", ldw), ("vT", xe_rdy)]
                        if sS_hist[hb] is not None:
                            w0.append(("sS", sS_hist[hb]))
                        if e == 0 and hc < 2:
                            w0.append(("vI", idx_rdy))
                    ps = (pp[hb * 2] if ti == 0 else pp[hb * 2 + 1])
                    p.op("tensor",
                         lambda t, hc=hc, c=c, hb=hb, o0=o0, n0=n0, ps=ps:
                         t.matmul(
                             ps[:, 0:n0], w1t[hb][:, c * P:(c + 1) * P],
                             xeT3[:, c, o0:o0 + n0],
                             start=(c == 0), stop=(c == NCH - 1)),
                         waits=w0,
                         inc="tS" if (c == NCH - 1 and ti == 1) else None)
            use_hist[f"F{hb}"] = p.cnt("tS")
            for ti, (o0, n0) in enumerate(((0, 512), (512, 256))):
                ps = (pp[hb * 2] if ti == 0 else pp[hb * 2 + 1])
                wv = [("tS", p.cnt("tS")), (f"qB{tag}", small_rdy),
                      ("vH", p.cnt("vH")), ("pG", p.cnt("pG"))]
                if e == 0 and hc == 0 and ti == 0:
                    wv.append(("tT", sh["tT"]))
                p.op("scalar",
                     lambda s, hc=hc, e=e, n0=n0, ps=ps: s.activation(
                         xb_[:, 0:n0], ps[:, 0:n0], AF.Identity,
                         bias=b1sb[:, e * 32 + hc:e * 32 + hc + 1]),
                     waits=wv, inc="sS")
                p.op("gpsimd",
                     lambda g, n0=n0: g.tensor_mul(
                         t2_[:, 0:n0], xb_[:, 0:n0], xb_[:, 0:n0]),
                     waits=[("sS", p.cnt("sS"))], inc="pG")
                p.op("vector",
                     lambda v, n0=n0: v.tensor_scalar(
                         t2_[:, 0:n0], t2_[:, 0:n0], C1, 1.0, ALU.mult,
                         ALU.add),
                     waits=[("pG", p.cnt("pG"))])
                p.op("vector",
                     lambda v, n0=n0: v.tensor_mul(
                         t2_[:, 0:n0], t2_[:, 0:n0], xb_[:, 0:n0]),
                     inc="vH")
                p.op("scalar",
                     lambda s, n0=n0: s.activation(
                         t2_[:, 0:n0], t2_[:, 0:n0], AF.Tanh, scale=C0),
                     waits=[("vH", p.cnt("vH"))], inc="sS")
                p.op("vector",
                     lambda v, n0=n0: v.tensor_scalar(
                         t2_[:, 0:n0], t2_[:, 0:n0], 0.5, 0.5, ALU.mult,
                         ALU.add),
                     waits=[("sS", p.cnt("sS"))], inc="vG2")
                p.op("gpsimd",
                     lambda g, hc=hc, o0=o0, n0=n0: g.tensor_mul(
                         hsp(hc)[:, o0:o0 + n0], t2_[:, 0:n0],
                         xb_[:, 0:n0]),
                     waits=[("vG2", p.cnt("vG2"))], inc="vH")
            sS_hist[hb] = p.cnt("sS")
        mm1_last = max(use_hist["F0"], use_hist["F1"])
        h_done = p.cnt("vH")

        for dcq in range(4):
            for hc in range(32):
                gb = hc % 2
                ldw2 = p.dma("sync", w2t[gb][:],
                             wb16[1][e][hc * P:(hc + 1) * P,
                                        dcq * 256:(dcq + 1) * 256],
                             waits=[("tS", use_hist[f"G{gb}"]),
                                    ("qStash", stash_rdy)], inc=f"qH{gb}")
                for du in range(2):
                    for ti, (o0, n0) in enumerate(((0, 512), (512, 256))):
                        w0 = []
                        if du == 0 and ti == 0:
                            w0 = [(f"qH{gb}", ldw2)]
                            if hc == 0:
                                w0 += [("vH", h_done), ("sE", p.cnt("sE")),
                                       ("vC", otm_ev_prev),
                                       ("tS", outT_free)]
                                if e == 0 and dcq == 0:
                                    w0 += [("vI2", p.cnt("vI2")),
                                           ("sT", sh["sT"])]
                        ps = (pp[4 + 2 * du] if ti == 0
                              else pp[5 + 2 * du])
                        p.op("tensor",
                             lambda t, hc=hc, du=du, gb=gb, o0=o0, n0=n0,
                             ps=ps: t.matmul(
                                 ps[:, 0:n0], w2t[gb][:, du * P:(du + 1) * P],
                                 hsp(hc)[:, o0:o0 + n0],
                                 start=(hc == 0), stop=(hc == 31)),
                             waits=w0,
                             inc="tS" if (du == 1 and ti == 1) else None)
                use_hist[f"G{gb}"] = p.cnt("tS")
            mm2q = p.cnt("tS")
            for du in range(2):
                dc = dcq * 2 + du
                bias = b2sb[:, e * 8 + dc:e * 8 + dc + 1]
                p.op("scalar",
                     lambda s, dc=dc, du=du, bias=bias: s.activation(
                         outT(dc)[:, 0:512], pp[4 + 2 * du][:],
                         AF.Identity, bias=bias),
                     waits=[("tS", mm2q), (f"qB{tag}", small_rdy)]
                     if du == 0 else (), inc="sE")
                p.op("scalar",
                     lambda s, dc=dc, du=du, bias=bias: s.activation(
                         outT(dc)[:, 512:768], pp[5 + 2 * du][:, 0:256],
                         AF.Identity, bias=bias), inc="sE")
        evac_all = p.cnt("sE")

        for j in range(6):
            jh = j % 2
            for dc in range(8):
                bk = pp[4] if dc < 4 else pp[5]
                w0 = []
                if dc == 0:
                    w0 = [("sE", evac_all)]
                    if j >= 1:
                        w0.append(("vC", p.cnt("vC")))
                p.op("tensor",
                     lambda t, dc=dc, j=j, bk=bk: t.transpose(
                         bk[:, (dc % 4) * P:(dc % 4 + 1) * P],
                         outT(dc)[:, j * P:(j + 1) * P], com.identf[:]),
                     waits=w0, inc="tS" if dc == 7 else None)
            tr_done = p.cnt("tS")
            wv = [("tS", tr_done), (f"qSD{jh}", otm_free[jh])]
            p.op("vector", lambda v, jh=jh: v.tensor_copy(
                otm[jh][:, 0:512], pp[4][:, 0:512]), waits=wv)
            p.op("vector", lambda v, jh=jh: v.tensor_copy(
                otm[jh][:, 512:1024], pp[5][:, 0:512]), inc="vC")
            ev = p.cnt("vC")
            ws = [("vC", ev)]
            if first_scat:
                ws.append(("qZ", qz_total))
                first_scat = False
            p.op("gpsimd", lambda g, e=e, j=j, jh=jh:
                 g.indirect_dma_start(
                     out=xdelta[:, :],
                     out_offset=IndirectOffsetOnAxis(
                         gidx_sb[:, e * 6 + j:e * 6 + j + 1], 0),
                     in_=otm[jh][:], in_offset=None),
                 waits=ws, inc=f"qSD{jh}", n=16, is_dma=True)
            otm_free[jh] = p.cnt(f"qSD{jh}")
        outT_free = p.cnt("tS")
        otm_ev_prev = p.cnt("vC")

    qsd_tot = [("qSD0", p.cnt("qSD0")), ("qSD1", p.cnt("qSD1"))]

    dve_hist = [0, 0]
    wb_hist = [0, 0]
    last_sS = p.cnt("sS")
    last_vT = p.cnt("vT")
    for tt in range(32):
        h = tt % 2
        xdl = [scr3[:, 0:1024], scr3[:, 1024:2048]][h]
        xfr = [scr3[:, 2048:3072], scr3[:, 3072:4096]][h]
        bA, bB = [(pp[0], pp[1]), (pp[2], pp[3])][h]
        ldd = p.dma("sync", xdl, xdelta[tt * P:(tt + 1) * P, :],
                    waits=(list(qsd_tot) if tt < 2 else []) +
                    [("vX", dve_hist[h]), ("vC", p.cnt("vC"))],
                    inc=f"qXD{h}")
        ldx = p.dma("sync", xfr.rearrange("p (c k) -> p c k", c=NCH),
                    xc_ap[:, tt * P:(tt + 1) * P].rearrange(
                        "(c p) k -> p c k", p=P),
                    waits=[(f"qXU{h}", wb_hist[h])], inc=f"qXD{h}")
        ld_all = p.cnt(f"qXD{h}")
        for c in range(NCH):
            bk = bA if c < 4 else bB
            w0 = []
            if c == 0:
                w0 = [(f"qXD{h}", ld_all), ("vX", dve_hist[h])]
                if tt < 2:
                    w0 += [("sS", last_sS), ("vT", last_vT)]
            p.op("tensor", lambda t, c=c, bk=bk, xdl=xdl: t.transpose(
                bk[:, (c % 4) * P:(c % 4 + 1) * P],
                xdl[:, c * P:(c + 1) * P], com.identf[:]),
                waits=w0, inc="tS" if c == NCH - 1 else None)
        p.op("tensor", lambda t, tt=tt: t.matmul(
            pp[6][:, 0:128], com.ones_rf[0:1, :],
            mrow[:, tt * P:(tt + 1) * P], start=True, stop=True),
            waits=[("vX", p.cnt("vX"))] +
            ([("sE", p.cnt("sE"))] if tt == 0 else []))
        p.op("tensor", lambda t, tt=tt: t.matmul(
            pp[6][:, 128:256], com.ones_rf[0:1, :],
            rrow[:, tt * P:(tt + 1) * P], start=True, stop=True), inc="tS")
        tdone = p.cnt("tS")
        mb3 = pp[6].ap()[:, 0:128].rearrange("p (o t) -> p o t", o=1)
        rb3 = pp[6].ap()[:, 128:256].rearrange("p (o t) -> p o t", o=1)
        xv = xfr.rearrange("p (c k) -> p c k", c=NCH)
        fv = xdl.rearrange("p (c k) -> p c k", c=NCH)

        def sub_p(v, xv=xv, fv=fv, mb3=mb3):
            a, b = broadcast_tensor_aps(xv, mb3)
            return v.tensor_tensor(fv, a, b, ALU.subtract)
        p.op("vector", sub_p, waits=[("tS", tdone)])

        def mul_p(v, fv=fv, rb3=rb3):
            a, b = broadcast_tensor_aps(fv, rb3)
            return v.tensor_tensor(fv, a, b, ALU.mult)
        p.op("vector", mul_p)
        p.op("vector", lambda v, xdl=xdl, bA=bA: v.tensor_add(
            xdl[:, 0:512], xdl[:, 0:512], bA.ap()[:, 0:512]))
        p.op("vector", lambda v, xdl=xdl, bB=bB: v.tensor_add(
            xdl[:, 512:1024], xdl[:, 512:1024], bB.ap()[:, 0:512]))
        p.op("vector", lambda v, xfr=xfr, xdl=xdl: v.scalar_tensor_tensor(
            xfr[:], xdl[:], 0.5, xfr[:], ALU.mult, ALU.add), inc="vX")
        dve_hist[h] = p.cnt("vX")
        p.dma("gpsimd", xc_ap[:, tt * P:(tt + 1) * P].rearrange(
            "(c p) k -> p c k", p=P),
            xfr.rearrange("p (c k) -> p c k", c=NCH),
            waits=[("vX", p.cnt("vX"))], inc=f"qXU{h}")
        wb_hist[h] = p.cnt(f"qXU{h}")
    return [("qXU0", p.cnt("qXU0")), ("qXU1", p.cnt("qXU1"))]



CAPL = 1280
CT0 = 8 * CAPL
NV = 16
CAPV = 640
ROWS0 = 2 * 4224


def emit_l0s_pre(p: Prog, com: Common, xn_sb, xn_done, dscr, zbuf, gtmp,
                 bank_free, tag):
    sh = {}
    xn_tm = dscr["xn_tm0"]
    gtb = gtmp.bitcast(BF16)
    p.op("vector", lambda v: v.memset(gtmp[0:81, 0:128], float(ROWS0)),
         waits=list(com.ready), inc="vPz")
    p.op("vector", lambda v: v.tensor_scalar(
        gtmp[0:81, 0:128], gtmp[0:81, 0:128], com.rowidx[0:81, :], None,
        ALU.add), inc="vPz")
    p.op("vector", lambda v: v.tensor_copy(
        gtmp[0:81, 128:256].bitcast(I32), gtmp[0:81, 0:128]), inc="vPz")
    p.dma("sync", dscr["gidxB"].rearrange("(r k) o -> r (k o)", k=P),
          gtmp[0:81, 128:256].bitcast(I32),
          waits=[("vPz", p.cnt("vPz"))], inc="qPF")
    p.op("vector", lambda v: v.memset(zbuf[:], 0.0), inc="vPz")
    sh["vPz"] = p.cnt("vPz")
    p.dma("sync", dscr["gidxW"].rearrange("(r k) o -> r (k o)", k=P),
          zbuf[0:81, 0:128], waits=[("vPz", sh["vPz"])], inc="qPF")
    sh["qPF"] = p.cnt("qPF")
    ev_hist = [0, 0]
    qtm_hist = [0, 0]
    for tt in range(32):
        half = tt % 2
        bkb = com.pp[half].ap().bitcast(BF16)
        for c in range(NCH):
            w0 = []
            if c == 0:
                w0 = [("sT", ev_hist[half])]
                if tt == 0:
                    w0 += list(xn_done) + list(bank_free)
            p.op("tensor",
                 lambda t, tt=tt, c=c, bkb=bkb: t.transpose(
                     bkb[:, c * P:(c + 1) * P],
                     xn_sb[:, c * TLOC + tt * P:c * TLOC + (tt + 1) * P],
                     com.ident[:]),
                 waits=w0, inc="tT" if c == NCH - 1 else None)
        dst = gtb[:, half * 1024:(half + 1) * 1024]
        wv = [("tT", p.cnt("tT")), (f"qTN{half}", qtm_hist[half])]
        if tt == 0:
            wv.append(("qPF", sh["qPF"]))
        p.op("scalar",
             lambda s, dst=dst, bkb=bkb: s.copy(dst[:], bkb[:]),
             waits=wv, inc="sT")
        ev_hist[half] = p.cnt("sT")
        p.dma("sync", xn_tm[tt * P:(tt + 1) * P, :], dst,
              waits=[("sT", ev_hist[half])], inc=f"qTN{half}")
        p.dma("sync", xn_tm[4224 + tt * P:4224 + (tt + 1) * P, :], dst,
              inc=f"qTN{half}")
        qtm_hist[half] = p.cnt(f"qTN{half}")
    for r0 in (4096, 8320, ROWS0):
        p.dma("sync", xn_tm[r0:r0 + P, :], zbuf.bitcast(BF16)[:, 0:1024],
              waits=[("vPz", sh["vPz"])] if r0 == 4096 else (), inc="qTNz")
    sh["qTM"] = [("qTN0", qtm_hist[0]), ("qTN1", qtm_hist[1]),
                 ("qTNz", p.cnt("qTNz"))]
    sh["sT"] = p.cnt("sT")
    sh["tT"] = p.cnt("tT")
    return sh


def emit_l0s_gate(p: Prog, com: Common, xc_ap, xn_sb, xn_done, wts,
                  ln_rows, accS, dscr, sh, tag):
    pp = com.pp
    scr3 = com.scr[3]
    gidxB, gidxW = dscr["gidxB"], dscr["gidxW"]
    wg_sb = p.sbuf(f"{tag}_wg", [P, 64], F32)
    bgt_sb = p.sbuf(f"{tag}_bgt", [1, 256], BF16)
    mrow, rrow = ln_rows
    exps = accS[:, 0:256]
    probs = accS[:, 256:512]
    t1 = accS[:, 512:768]
    t12 = accS[:, 768:1024]
    g32 = accS[:, 1024:1152]
    w_bf = accS[:, 1152:1280].bitcast(BF16)
    bgb_sb = accS[:, 1280:1536]
    wselc = [accS[:, 1536:1568], accS[:, 1568:1600]]
    slotf = accS[:, 1600:1664]
    sloti = accS[:, 1664:1728].bitcast(I32)
    tokf = accS[:, 1728:1792]
    toki = accS[:, 1792:1856].bitcast(I32)
    tsc = accS[:, 1856:2112]
    m1T = com.rowp[32:40, 512:2560].bitcast(BF16)
    m12T = com.rowp[32:40, 2560:4608].bitcast(BF16)
    cums = com.scr[0][32:40, 0:4096]
    keep = com.scr[1][32:40, 0:4096]
    pr = com.scr[2][32:40, 0:4096]
    ecap = scr3[32:40, 2048:2049]
    srow = scr3[0:1, 0:512]
    dtmp = scr3[0:1, 1024:1152]

    p.dma("gpsimd", wg_sb[:], wts["Wg_l"], inc=f"qG{tag}")
    p.dma("gpsimd", bgt_sb[:], wts["bg_t"], inc=f"qG{tag}")
    gate_w_rdy = p.cnt(f"qG{tag}")

    pl = pp[2]
    xg = com.scr[1]
    p.op("tensor", lambda t: t.matmul(
        pp[3][:, 0:256], com.ones_row[:], bgt_sb[:],
        start=True, stop=True),
        waits=[(f"qG{tag}", gate_w_rdy)] + list(xn_done) +
        list(com.ready), inc="tS")
    p.op("scalar", lambda s: s.copy(bgb_sb, pp[3][:, 0:256]),
         waits=[("tS", p.cnt("tS"))], inc="sS")
    bgb_done = p.cnt("sS")
    for tt in range(NT):
        ldg = p.dma("sync", xg[:], xc3(xc_ap, tt),
                    waits=[("tS", p.cnt("tS"))] + list(xn_done),
                    inc="qGX")
        sl = slice(tt * 512, (tt + 1) * 512)
        p.op("tensor", lambda t, sl=sl: t.matmul(
            pp[0][:], com.ones_rf[:], mrow[:, sl],
            start=True, stop=True),
            waits=[("vS4", p.cnt("vS4"))] +
            ([("sT", sh["sT"])] if tt == 0 else []))
        p.op("tensor", lambda t, sl=sl: t.matmul(
            pp[1][:], com.ones_rf[:], rrow[:, sl],
            start=True, stop=True), inc="tS")
        mb3 = pp[0].ap().rearrange("p (o t) -> p o t", o=1)
        rb3 = pp[1].ap().rearrange("p (o t) -> p o t", o=1)

        def subp(v, mb3=mb3):
            a, b = broadcast_tensor_aps(sb3(xg), mb3)
            return v.tensor_tensor(sb3(xg), a, b, ALU.subtract)

        def mulp(v, rb3=rb3):
            a, b = broadcast_tensor_aps(sb3(xg), rb3)
            return v.tensor_tensor(sb3(xg), a, b, ALU.mult)
        p.op("vector", subp,
             waits=[("tS", p.cnt("tS")), ("qGX", ldg)])
        p.op("vector", mulp, inc="vS4")
        for blk in range(4):
            for c in range(NCH):
                p.op("tensor",
                     lambda t, tt=tt, blk=blk, c=c: t.matmul(
                         pl[:, (tt * 4 + blk) * 8:
                            (tt * 4 + blk) * 8 + 8],
                         sb3(xg)[:, c, blk * P:(blk + 1) * P],
                         wg_sb[:, c * 8:(c + 1) * 8],
                         start=(c == 0), stop=(c == NCH - 1)),
                     waits=[("vS4", p.cnt("vS4"))]
                     if blk == 0 and c == 0 else (),
                     inc="tS" if (blk == 3 and c == NCH - 1) else None)
    vS4_done = p.cnt("vS4")

    sums = g32[:, 0:32]
    rr = g32[:, 32:64]
    m1 = g32[:, 64:96]
    m2 = g32[:, 96:128]
    p.op("vector", lambda v: v.tensor_add(probs[:, :], pl[:, 0:256],
                                          bgb_sb),
         waits=[("tS", p.cnt("tS")), ("sS", bgb_done)], inc="vS4")
    p.op("scalar", lambda s: s.activation(exps[:, :], probs[:, :], AF.Exp),
         waits=[("vS4", p.cnt("vS4"))], inc="sS")

    def r3(ap2):
        return ap2.rearrange("p (g f) -> p g f", g=32)

    def bc3(ap2):
        return ap2.rearrange("p (g f) -> p g f", f=1)

    def tt_b(v, out, a, b, op):
        ab, bb = broadcast_tensor_aps(r3(a), bc3(b))
        return v.tensor_tensor(r3(out), ab, bb, op)

    p.op("vector", lambda v: v.reduce_sum(sums[:, :], r3(exps), axis=AX.X),
         waits=[("sS", p.cnt("sS"))])
    p.op("vector", lambda v: v.reciprocal(rr[:, :], sums[:, :]))
    p.op("vector", lambda v: tt_b(v, probs, exps, rr, ALU.mult))
    p.op("vector", lambda v: v.reduce_max(m1[:, :], r3(probs), axis=AX.X))
    p.op("vector", lambda v: tt_b(v, t1, probs, m1, ALU.is_ge))
    p.op("vector", lambda v: v.scalar_tensor_tensor(
        exps[:, :], t1[:, :], -2.0, probs[:, :], ALU.mult, ALU.add))
    p.op("vector", lambda v: v.reduce_max(m2[:, :], r3(exps), axis=AX.X))
    p.op("vector", lambda v: tt_b(v, t12, probs, m2, ALU.is_ge))
    p.op("vector", lambda v: v.tensor_mul(tsc[:, :], t1[:, :],
                                          probs[:, :]))
    p.op("vector", lambda v: v.reduce_sum(
        wselc[0][:, :], r3(tsc), axis=AX.X))
    p.op("vector", lambda v: v.tensor_mul(tsc[:, :], t12[:, :],
                                          probs[:, :]))
    p.op("vector", lambda v: v.reduce_sum(
        wselc[1][:, :], r3(tsc), axis=AX.X))
    p.op("vector", lambda v: v.tensor_sub(wselc[1][:, :], wselc[1][:, :],
                                          wselc[0][:, :]), inc="vS4")

    for role, (src, dstT) in enumerate(((t1, m1T), (t12, m12T))):
        p.op("vector", lambda v, src=src: v.tensor_copy(w_bf[:, :], src),
             waits=[("sS", p.cnt("sS")), ("vS4", p.cnt("vS4"))], inc="vS4")
        for grp in range(8):
            for jj in range(4):
                blk = grp * 4 + jj
                p.op("tensor",
                     lambda t, blk=blk, jj=jj: t.matmul(
                         pp[3][0:8, jj * P:(jj + 1) * P],
                         w_bf[:, blk * 8:(blk + 1) * 8], com.ident[:],
                         start=True, stop=True),
                     waits=([("vS4", p.cnt("vS4"))] if grp == 0 and jj == 0
                            else ([("sS", p.cnt("sS"))] if jj == 0
                                  else ())),
                     inc="tS" if jj == 3 else None)
            p.op("scalar",
                 lambda s, grp=grp, dstT=dstT: s.copy(
                     dstT[0:8, grp * 512:(grp + 1) * 512], pp[3][0:8, :]),
                 waits=[("tS", p.cnt("tS"))], inc="sS")
    rows_done = p.cnt("sS")

    p.op("vector", lambda v: v.memset(keep[:], 1.0),
         waits=[("sS", rows_done)], inc="vI")
    p.op("vector", lambda v: v.tensor_tensor_scan(
        cums[:], keep[:], m12T[:, :], 0.0, ALU.mult, ALU.add), inc="vI")
    p.op("vector", lambda v: v.tensor_single_scalar(
        keep[:], cums[:], CAPL + 0.5, ALU.is_lt), inc="vI")
    p.op("vector", lambda v: v.tensor_scalar(
        ecap, com.rowidx[32:40, :], -32.0, float(CAPL), ALU.add, ALU.mult))
    p.op("vector", lambda v: v.tensor_scalar(
        cums[:], cums[:], ecap, None, ALU.add), inc="vI")
    p.op("vector", lambda v: v.tensor_sub(m12T[:, :], m12T[:, :],
                                          m1T[:, :]), inc="vI")
    for role in range(2):
        mT = (m1T, m12T)[role]
        p.op("vector", lambda v, mT=mT: v.tensor_mul(
            pr[:], keep[:], mT[:, :]),
            waits=[("vI", p.cnt("vI"))] + (
                [("vI2", p.cnt("vI2"))] if role else []))
        p.op("vector", lambda v: v.tensor_mul(pr[:], pr[:], cums[:]),
             inc="vI")
        for q in range(NT):
            p.op("tensor", lambda t, q=q: t.matmul(
                pp[4][0:1, 0:512], com.ones_f[32:40, :],
                pr[:, q * 512:(q + 1) * 512], start=True, stop=True),
                waits=[("vI", p.cnt("vI"))] + (
                    [("vI2", p.cnt("vI2"))] if (role, q) != (0, 0) else []),
                inc="tS")
            mmq = p.cnt("tS")
            p.op("vector", lambda v: v.tensor_copy(srow, pp[4][0:1, 0:512]),
                 waits=[("tS", mmq)])
            p.op("vector", lambda v: v.tensor_single_scalar(
                srow, srow, -1.0, ALU.add), inc="vI2")
            for b4 in range(4):
                blk = q * 4 + b4
                sb4 = srow[:, b4 * P:(b4 + 1) * P]
                p.op("vector", lambda v, sb4=sb4: v.tensor_single_scalar(
                    dtmp, sb4, -0.5, ALU.is_lt))
                p.op("vector", lambda v, sb4=sb4, blk=blk, role=role:
                     v.scalar_tensor_tensor(
                         sb4, dtmp, float(CT0 + 1 + 32 * role + blk),
                         sb4, ALU.mult, ALU.add),
                     inc="vI2")
                p.op("tensor",
                     lambda t, q=q, b4=b4, sb4=sb4, role=role: t.matmul(
                         pp[3][:, role * 32 + q * 4 + b4:
                               role * 32 + q * 4 + b4 + 1],
                         sb4, com.ones_rf[0:1, 0:1],
                         start=True, stop=True),
                     waits=[("vI2", p.cnt("vI2"))],
                     inc="tS" if b4 == 3 else None)
    p.op("vector", lambda v: v.tensor_copy(slotf[:, 0:64], pp[3][:, 0:64]),
         waits=[("tS", p.cnt("tS"))])
    p.op("vector", lambda v: v.tensor_copy(sloti[:, 0:64], slotf[:, 0:64]),
         inc="vI")
    for blk in range(32):
        p.op("vector", lambda v, blk=blk: v.tensor_single_scalar(
            tokf[:, blk:blk + 1], com.rowidx, float(blk * P), ALU.add))
    p.op("vector", lambda v: v.tensor_single_scalar(
        tokf[:, 32:64], tokf[:, 0:32], 4224.0, ALU.add))
    p.op("vector", lambda v: v.tensor_copy(toki[:, 0:64], tokf[:, 0:64]),
         inc="vI")
    idx_rdy = p.cnt("vI")
    first = True
    for role in range(2):
        for blk in range(32):
            w0 = []
            if first:
                w0 = [("qPF", sh["qPF"]), ("vI", idx_rdy)]
                first = False
            p.op("gpsimd", lambda g, role=role, blk=blk:
                 g.indirect_dma_start(
                     out=gidxB[:, :],
                     out_offset=IndirectOffsetOnAxis(
                         sloti[:, role * 32 + blk:role * 32 + blk + 1], 0),
                     in_=toki[:, role * 32 + blk:role * 32 + blk + 1],
                     in_offset=None),
                 waits=w0, inc="qSB", n=16, is_dma=True)
            p.op("gpsimd", lambda g, role=role, blk=blk:
                 g.indirect_dma_start(
                     out=gidxW[:, :],
                     out_offset=IndirectOffsetOnAxis(
                         sloti[:, role * 32 + blk:role * 32 + blk + 1], 0),
                     in_=wselc[role][:, blk:blk + 1], in_offset=None),
                 inc="qSW", n=16, is_dma=True)
    return dict(vI2=p.cnt("vI2"), vI=idx_rdy, qSB=p.cnt("qSB"),
                qSW=p.cnt("qSW"), vS4=vS4_done)


def emit_l0x_sparse(p: Prog, com: Common, xc_ap, xn_sb, wts, wb16,
                    stash_rdy, dscr, sh, gres, zbuf, gtmp, tag,
                    st1=None, st2=None):
    pp = com.pp
    xn_tm, xdelta = dscr["xn_tm0"], dscr["xdelta0"]
    gidxB, gidxW = dscr["gidxB"], dscr["gidxW"]
    gtb = gtmp.bitcast(BF16)
    scr3 = com.scr[3]
    b1sb = p.sbuf(f"{tag}_b1", [P, 256], F32)
    b2sb = p.sbuf(f"{tag}_b2", [P, 64], F32)
    w1t = [p.sbuf(f"{tag}_w1t{i}", [P, NCH * P], BF16) for i in range(2)]
    w2t = [p.sbuf(f"{tag}_w2t{i}", [P, 256], BF16) for i in range(2)]
    gidx_sb = p.sbuf(f"{tag}_gi", [P, 80], I32)
    wq_sb = p.sbuf(f"{tag}_wq", [P, 80], F32)
    p.dma("sync", b1sb[:], wts["b1l"], inc=f"qB{tag}")
    p.dma("sync", b2sb[:], wts["b2l"], inc=f"qB{tag}")
    small_rdy = p.cnt(f"qB{tag}")

    for zt in range(67):
        p.dma("scalar", xdelta[zt * P:(zt + 1) * P, :], zbuf[:],
              waits=[("vPz", sh["vPz"])] if zt == 0 else (), inc="qZ")
    qz_total = p.cnt("qZ")

    for g in range(80):
        p.dma("sync", gidx_sb[:, g:g + 1], gidxB[g * P:(g + 1) * P, :],
              waits=[("qSB", gres["qSB"])] if g == 0 else (), inc="qGI")
        p.dma("sync", wq_sb[:, g:g + 1],
              gidxW[g * P:(g + 1) * P, :],
              waits=[("qSW", gres["qSW"])] if g == 0 else (), inc="qGI")
    ld_gi = p.cnt("qGI")

    s2b = com.scr[2].bitcast(BF16)
    xeT3 = s2b[:, 0:5120].rearrange("p (c s) -> p c s", c=NCH)

    def outT(dc):
        return com.scr[dc // 4][:, (dc % 4) * 640:(dc % 4) * 640 + 640]

    def hsp(hc):
        return xn_sb[:, hc * 1024:hc * 1024 + 640]

    otm = [scr3[:, 0:1024], scr3[:, 1024:2048]]

    use_hist = {"F0": 0, "F1": 0, "G0": 0, "G1": 0}
    gbuf_hist = [0, 0]
    tin_ev_hist = [0, 0]
    ga_hist = {}
    mm1_last = 0
    outT_free = 0
    otm_free = [0, 0]
    evac_all = 0
    otm_ev_prev = 0
    first_scat = True

    def emit_gather(gg):
        half = gg % 2
        w0 = [("tS", gbuf_hist[half])]
        if gg < 2:
            w0 += [("qGI", ld_gi), ("qSB", gres["qSB"])] + sh["qTM"]
        p.op("gpsimd", lambda g, gg=gg, half=half:
             g.indirect_dma_start(
                 out=gtb[:, half * 1024:(half + 1) * 1024],
                 out_offset=None, in_=xn_tm[:, :],
                 in_offset=IndirectOffsetOnAxis(
                     gidx_sb[:, gg:gg + 1], 0)),
             waits=w0, inc=f"qGB{half}", n=16, is_dma=True)
        ga_hist[gg] = p.cnt(f"qGB{half}")

    emit_gather(0)
    emit_gather(1)
    for v in range(NV):
        e = v // 2
        for j in range(5):
            gg = v * 5 + j
            half = gg % 2
            bkb = pp[half].ap().bitcast(BF16)
            for c in range(NCH):
                w0 = []
                if c == 0:
                    w0 = [(f"qGB{half}", ga_hist[gg]),
                          ("vT", tin_ev_hist[half])]
                    if j < 2:
                        w0.append(("tS", mm1_last))
                    if v == 0 and j < 2:
                        w0 += [("sT", sh["sT"]), ("vI2", gres["vI2"]),
                               ("vI", gres["vI"])]
                p.op("tensor",
                     lambda t, c=c, bkb=bkb, half=half: t.transpose(
                         bkb[:, c * P:(c + 1) * P],
                         gtb[:, half * 1024 + c * P:
                             half * 1024 + (c + 1) * P],
                         com.ident[:]),
                     waits=w0, inc="tS" if c == NCH - 1 else None)
            gbuf_hist[half] = p.cnt("tS")
            p.op("vector", lambda v2, j=j, bkb=bkb: v2.tensor_copy(
                xeT3[:, :, j * P:(j + 1) * P],
                bkb.rearrange("p (c s) -> p c s", c=NCH)),
                waits=[("tS", p.cnt("tS"))], inc="vT")
            tin_ev_hist[half] = p.cnt("vT")
            if gg + 2 < NV * 5:
                emit_gather(gg + 2)
        xe_rdy = p.cnt("vT")

        sS_hist = {0: None, 1: None}
        for hc in range(32):
            hb = hc % 2
            ldw = p.dma("sync",
                        w1t[hb].rearrange("p (c k) -> p c k", c=NCH),
                        wb16[0][e][:, hc * P:(hc + 1) * P].rearrange(
                            "(c p) k -> p c k", p=P),
                        waits=[("tS", use_hist[f"F{hb}"]),
                               ("qStash",
                                st1[e] if st1 else stash_rdy)],
                        inc=f"qE{hb}")
            for c in range(NCH):
                for ti, (o0, n0) in enumerate(((0, 512), (512, 128))):
                    w0 = []
                    if c == 0 and ti == 0:
                        w0 = [(f"qE{hb}", ldw), ("vT", xe_rdy)]
                        if sS_hist[hb] is not None:
                            w0.append(("sS", sS_hist[hb]))
                        if v == 0 and hc < 2:
                            w0.append(("vI", gres["vI"]))
                    ps = (pp[hb * 2] if ti == 0 else pp[hb * 2 + 1])
                    p.op("tensor",
                         lambda t, hc=hc, c=c, hb=hb, o0=o0, n0=n0, ps=ps:
                         t.matmul(
                             ps[:, 0:n0], w1t[hb][:, c * P:(c + 1) * P],
                             xeT3[:, c, o0:o0 + n0],
                             start=(c == 0), stop=(c == NCH - 1)),
                         waits=w0,
                         inc="tS" if (c == NCH - 1 and ti == 1) else None)
            use_hist[f"F{hb}"] = p.cnt("tS")
            for ti, (o0, n0) in enumerate(((0, 512), (512, 128))):
                ps = (pp[hb * 2] if ti == 0 else pp[hb * 2 + 1])
                wv = [("tS", p.cnt("tS")), (f"qB{tag}", small_rdy)]
                if v == 0 and hc == 0 and ti == 0:
                    wv.append(("tT", sh["tT"]))
                p.op("scalar",
                     lambda s, hc=hc, e=e, o0=o0, n0=n0, ps=ps:
                     s.activation(
                         hsp(hc)[:, o0:o0 + n0], ps[:, 0:n0], AF.Gelu,
                         bias=b1sb[:, e * 32 + hc:e * 32 + hc + 1]),
                     waits=wv, inc="sS")
            sS_hist[hb] = p.cnt("sS")
        mm1_last = max(use_hist["F0"], use_hist["F1"])
        h_done = p.cnt("sS")

        for dcq in range(4):
            for hc in range(32):
                gb = hc % 2
                ldw2 = p.dma("sync", w2t[gb][:],
                             wb16[1][e][hc * P:(hc + 1) * P,
                                        dcq * 256:(dcq + 1) * 256],
                             waits=[("tS", use_hist[f"G{gb}"]),
                                    ("qStash",
                                     st2[e] if st2 else stash_rdy)],
                             inc=f"qH{gb}")
                for du in range(2):
                    for ti, (o0, n0) in enumerate(((0, 512), (512, 128))):
                        w0 = []
                        if du == 0 and ti == 0:
                            w0 = [(f"qH{gb}", ldw2)]
                            if hc == 0:
                                w0 += [("sS", h_done), ("sE", p.cnt("sE")),
                                       ("vC", otm_ev_prev),
                                       ("tS", outT_free)]
                                if v == 0 and dcq == 0:
                                    w0 += [("vI2", gres["vI2"]),
                                           ("sT", sh["sT"])]
                        ps = (pp[4 + 2 * du] if ti == 0
                              else pp[5 + 2 * du])
                        p.op("tensor",
                             lambda t, hc=hc, du=du, gb=gb, o0=o0, n0=n0,
                             ps=ps: t.matmul(
                                 ps[:, 0:n0],
                                 w2t[gb][:, du * P:(du + 1) * P],
                                 hsp(hc)[:, o0:o0 + n0],
                                 start=(hc == 0), stop=(hc == 31)),
                             waits=w0,
                             inc="tS" if (du == 1 and ti == 1) else None)
                use_hist[f"G{gb}"] = p.cnt("tS")
            mm2q = p.cnt("tS")
            for du in range(2):
                dc = dcq * 2 + du
                bias = b2sb[:, e * 8 + dc:e * 8 + dc + 1]
                p.op("scalar",
                     lambda s, dc=dc, du=du, bias=bias: s.activation(
                         outT(dc)[:, 0:512], pp[4 + 2 * du][:],
                         AF.Identity, bias=bias),
                     waits=[("tS", mm2q), (f"qB{tag}", small_rdy)]
                     if du == 0 else (), inc="sE")
                p.op("scalar",
                     lambda s, dc=dc, du=du, bias=bias: s.activation(
                         outT(dc)[:, 512:640], pp[5 + 2 * du][:, 0:128],
                         AF.Identity, bias=bias), inc="sE")
        evac_all = p.cnt("sE")

        for j in range(5):
            gg = v * 5 + j
            jh = gg % 2
            for dc in range(8):
                bk = pp[4] if dc < 4 else pp[5]
                w0 = []
                if dc == 0:
                    w0 = [("sE", evac_all)]
                    if j >= 1 or v > 0:
                        w0.append(("vC", p.cnt("vC")))
                p.op("tensor",
                     lambda t, dc=dc, j=j, bk=bk: t.transpose(
                         bk[:, (dc % 4) * P:(dc % 4 + 1) * P],
                         outT(dc)[:, j * P:(j + 1) * P], com.identf[:]),
                     waits=w0, inc="tS" if dc == 7 else None)
            tr_done = p.cnt("tS")
            wv = [("tS", tr_done), (f"qSE{jh}", otm_free[jh]),
                  ("qGI", ld_gi)]
            p.op("vector", lambda v2, jh=jh, gg=gg: v2.tensor_scalar(
                otm[jh][:, 0:512], pp[4][:, 0:512],
                wq_sb[:, gg:gg + 1], None, ALU.mult), waits=wv)
            p.op("vector", lambda v2, jh=jh, gg=gg: v2.tensor_scalar(
                otm[jh][:, 512:1024], pp[5][:, 0:512],
                wq_sb[:, gg:gg + 1], None, ALU.mult), inc="vC")
            ev = p.cnt("vC")
            ws = [("vC", ev)]
            if first_scat:
                ws.append(("qZ", qz_total))
                first_scat = False
            p.op("gpsimd", lambda g, gg=gg, jh=jh:
                 g.indirect_dma_start(
                     out=xdelta[:, :],
                     out_offset=IndirectOffsetOnAxis(
                         gidx_sb[:, gg:gg + 1], 0),
                     in_=otm[jh][:], in_offset=None),
                 waits=ws, inc=f"qSE{jh}", n=16, is_dma=True)
            otm_free[jh] = p.cnt(f"qSE{jh}")
        outT_free = p.cnt("tS")
        otm_ev_prev = p.cnt("vC")

    qse_tot = [("qSE0", p.cnt("qSE0")), ("qSE1", p.cnt("qSE1"))]

    dve_hist = [0, 0]
    wb_hist = [0, 0]
    last_sS = p.cnt("sS")
    last_vT = p.cnt("vT")
    xdlA = scr3[:, 0:1024]
    xdlB = scr3[:, 1024:2048]
    tr_hist = 0
    for tt in range(32):
        h = tt % 2
        xfr = [scr3[:, 2048:3072], scr3[:, 3072:4096]][h]
        bA, bB = [(pp[0], pp[1]), (pp[2], pp[3])][h]
        ldd = p.dma("sync", xdlA, xdelta[tt * P:(tt + 1) * P, :],
                    waits=(list(qse_tot) if tt == 0 else []) +
                    [("tS", tr_hist), ("vC", p.cnt("vC"))],
                    inc="qXE")
        p.dma("sync", xdlB, xdelta[4224 + tt * P:4224 + (tt + 1) * P, :],
              inc="qXE")
        ld_d = p.cnt("qXE")
        p.op("vector", lambda v2: v2.tensor_add(xdlA[:], xdlA[:],
                                                xdlB[:]),
             waits=[("qXE", ld_d)], inc="vX")
        add_done = p.cnt("vX")
        ldx = p.dma("sync", xfr.rearrange("p (c k) -> p c k", c=NCH),
                    xc_ap[:, tt * P:(tt + 1) * P].rearrange(
                        "(c p) k -> p c k", p=P),
                    waits=[(f"qXV{h}", wb_hist[h])], inc=f"qXF{h}")
        for c in range(NCH):
            bk = bA if c < 4 else bB
            w0 = []
            if c == 0:
                w0 = [("vX", add_done), ("vX", dve_hist[h])]
                if tt < 2:
                    w0 += [("sS", last_sS), ("vT", last_vT)]
            p.op("tensor", lambda t, c=c, bk=bk: t.transpose(
                bk[:, (c % 4) * P:(c % 4 + 1) * P],
                xdlA[:, c * P:(c + 1) * P], com.identf[:]),
                waits=w0, inc="tS" if c == NCH - 1 else None)
        tr_hist = p.cnt("tS")
        p.op("vector", lambda v2, xfr=xfr, bA=bA: v2.scalar_tensor_tensor(
            xfr[:, 0:512], bA.ap()[:, 0:512], 0.5, xfr[:, 0:512],
            ALU.mult, ALU.add),
            waits=[("tS", tr_hist), (f"qXF{h}", ldx)])
        p.op("vector", lambda v2, xfr=xfr, bB=bB: v2.scalar_tensor_tensor(
            xfr[:, 512:1024], bB.ap()[:, 0:512], 0.5, xfr[:, 512:1024],
            ALU.mult, ALU.add), inc="vX")
        dve_hist[h] = p.cnt("vX")
        p.dma("gpsimd", xc_ap[:, tt * P:(tt + 1) * P].rearrange(
            "(c p) k -> p c k", p=P),
            xfr.rearrange("p (c k) -> p c k", c=NCH),
            waits=[("vX", p.cnt("vX"))], inc=f"qXV{h}")
        wb_hist[h] = p.cnt(f"qXV{h}")
    return [("qXV0", p.cnt("qXV0")), ("qXV1", p.cnt("qXV1"))]



def _io(nc, outs=("xco",)):
    d = {}
    d["xc"] = nc.dram_tensor("xc", [D, TLOC], F32, kind="ExternalInput").ap()
    d["cb"] = nc.dram_tensor("cb", [160, 2688], BF16,
                             kind="ExternalInput").ap()
    d["cf"] = nc.dram_tensor("cf", [128, 532], F32,
                             kind="ExternalInput").ap()
    if "xco" in outs:
        d["xco"] = nc.dram_tensor("xco", [D, TLOC], F32,
                                  kind="ExternalOutput").ap()
        d["xs"] = nc.dram_tensor("xs", [D, TLOC], F32, kind="Internal").ap()
    return d


def build_l0_ffn():
    nc = bass.Bass("TRN2", target_bir_lowering=False)
    d = _io(nc)
    wgl = nc.dram_tensor("Wg_l", [P, 64], F32, kind="ExternalInput").ap()
    bgt = nc.dram_tensor("bg_t", [1, 256], F32, kind="ExternalInput").ap()
    w1 = nc.dram_tensor("W1", [8, D, 4096], F32, kind="ExternalInput").ap()
    b1l = nc.dram_tensor("b1l", [P, 256], F32, kind="ExternalInput").ap()
    w2 = nc.dram_tensor("W2", [8, 4096, D], F32, kind="ExternalInput").ap()
    b2l = nc.dram_tensor("b2l", [P, 64], F32, kind="ExternalInput").ap()
    p = Prog(nc)
    com = Common(p, d["cb"], d["cf"])
    xn_sb = p.sbuf("xn", [P, NCH * TLOC], BF16)
    p.dma("sync", d["xs"], d["xc"], inc="qIn")
    x_ready = [("qIn", p.cnt("qIn"))]
    mrow, rrow, ln_done = emit_ln(p, com, d["xs"], xn_sb, "a", x_ready)
    wts = dict(Wg_l=wgl, bg_t=bgt, W1=w1, b1l=b1l, W2=w2, b2l=b2l)
    done = emit_l0_ffn(p, com, d["xs"], xn_sb, ln_done, wts, (mrow, rrow))
    p.dma("gpsimd", d["xco"], d["xs"], waits=done, inc="qO")
    p.op("gpsimd", lambda e: None, waits=[("qO", p.cnt("qO"))])
    return p.finalize()


def build_l0_attn(update_x):
    nc = bass.Bass("TRN2", target_bir_lowering=False)
    d = _io(nc, outs=("xco",) if update_x else ())
    wdt = nc.dram_tensor("WdT", [D, D], F32, kind="ExternalInput").ap()
    wit = nc.dram_tensor("WiT", [D, D], F32, kind="ExternalInput").ap()
    bd = nc.dram_tensor("bd", [P, NCH], F32, kind="ExternalInput").ap()
    bi = nc.dram_tensor("bi", [P, NCH], F32, kind="ExternalInput").ap()
    carry = nc.dram_tensor("carry", [P, NCH], F32,
                           kind="ExternalInput").ap()
    fino = nc.dram_tensor("fino", [P, NCH], F32, kind="ExternalOutput").ap()
    p = Prog(nc)
    com = Common(p, d["cb"], d["cf"])
    xn_sb = p.sbuf("xn", [P, NCH * TLOC], BF16)
    carry_sb = p.sbuf("carry", [P, NCH], F32)
    xsrc = d.get("xs", d["xc"])
    if update_x:
        p.dma("sync", d["xs"], d["xc"], inc="qIn")
    p.dma("sync", carry_sb[:], carry, inc="qIn")
    x_ready = [("qIn", p.cnt("qIn"))]
    mrow, rrow, ln_done = emit_ln(p, com, xsrc, xn_sb, "a", x_ready)
    fin, done = emit_l0_attn(p, com, xsrc, xn_sb, ln_done,
                             dict(WdT=wdt, WiT=wit, bd=bd, bi=bi),
                             carry_sb, x_ready, update_x=update_x)
    if update_x:
        p.dma("gpsimd", d["xco"], d["xs"], waits=done, inc="qO")
    p.dma("gpsimd", fino, fin[:], waits=done, inc="qO")
    p.op("gpsimd", lambda e: None, waits=[("qO", p.cnt("qO"))])
    return p.finalize()


def build_l1_scores():
    nc = bass.Bass("TRN2", target_bir_lowering=False)
    d = _io(nc, outs=())
    ctl = nc.dram_tensor("centT_l", [P, 64], F32, kind="ExternalInput").ap()
    lng = nc.dram_tensor("lng", [P, NCH], F32, kind="ExternalInput").ap()
    lnb = nc.dram_tensor("lnb", [P, NCH], F32, kind="ExternalInput").ap()
    sco = nc.dram_tensor("sco", [8, TLOC], BF16, kind="ExternalOutput").ap()
    p = Prog(nc)
    com = Common(p, d["cb"], d["cf"])
    xn_sb = p.sbuf("xn", [P, NCH * TLOC], BF16)
    lng_sb = p.sbuf("lng", [P, NCH], F32)
    lnb_sb = p.sbuf("lnb", [P, NCH], F32)
    scT = p.sbuf("scT", [8, TLOC], BF16)
    p.dma("sync", lng_sb[:], lng, inc="qIn")
    p.dma("sync", lnb_sb[:], lnb, inc="qIn")
    rdy = [("qIn", p.cnt("qIn"))]
    mrow, rrow, ln_done = emit_ln(p, com, d["xc"], xn_sb, "a", rdy,
                                  fuse_double=True, gvec=lng_sb,
                                  bvec=lnb_sb)
    sc_done = emit_l1_scores(p, com, xn_sb, ln_done, ctl, scT)
    p.dma("gpsimd", sco, scT[:], waits=sc_done, inc="qO")
    p.op("gpsimd", lambda e: None, waits=[("qO", p.cnt("qO"))])
    return p.finalize()


def build_l1_ffn():
    nc = bass.Bass("TRN2", target_bir_lowering=False)
    d = _io(nc)
    lng = nc.dram_tensor("lng", [P, NCH], F32, kind="ExternalInput").ap()
    lnb = nc.dram_tensor("lnb", [P, NCH], F32, kind="ExternalInput").ap()
    scin = nc.dram_tensor("scin", [8, TLOC], BF16,
                          kind="ExternalInput").ap()
    gin = nc.dram_tensor("gin", [P, 2048], BF16, kind="ExternalInput").ap()
    w1 = nc.dram_tensor("W1", [8, D, 4096], F32, kind="ExternalInput").ap()
    b1l = nc.dram_tensor("b1l", [P, 256], F32, kind="ExternalInput").ap()
    w2 = nc.dram_tensor("W2", [8, 4096, D], F32, kind="ExternalInput").ap()
    b2l = nc.dram_tensor("b2l", [P, 64], F32, kind="ExternalInput").ap()
    p = Prog(nc)
    com = Common(p, d["cb"], d["cf"])
    xn_sb = p.sbuf("xn", [P, NCH * TLOC], BF16)
    lng_sb = p.sbuf("lng", [P, NCH], F32)
    lnb_sb = p.sbuf("lnb", [P, NCH], F32)
    acc = p.sbuf("l1f_acc", [P, NCH * 1024], F32)
    scT = acc[0:8, 2304:4352].bitcast(BF16)
    gath = acc[:, 6400:7424].bitcast(BF16)
    mask = com.rowp[32:40, 2560:4608].bitcast(BF16)
    p.dma("sync", d["xs"], d["xc"], inc="qIn")
    p.dma("sync", lng_sb[:], lng, inc="qIn")
    p.dma("sync", lnb_sb[:], lnb, inc="qIn")
    p.dma("sync", scT, scin, inc="qIn")
    p.dma("sync", gath, gin, inc="qIn")
    in_rdy = [("qIn", p.cnt("qIn"))]
    mrow, rrow, ln_done = emit_ln(p, com, d["xs"], xn_sb, "a", in_rdy,
                                  fuse_double=True, gvec=lng_sb,
                                  bvec=lnb_sb)
    route_done = emit_l1_route(p, com, gath, in_rdy, scT, in_rdy, mask, acc)
    wts = dict(W1=w1, b1l=b1l, W2=w2, b2l=b2l)
    done = emit_l0_ffn(p, com, d["xs"], xn_sb, ln_done, wts, (mrow, rrow),
                       tag="l1f", gelu_tanh=True, mask_sb=mask,
                       mask_ready=route_done, extra_res=(mrow, rrow),
                       acc_ext=acc)
    p.dma("gpsimd", d["xco"], d["xs"], waits=done, inc="qO")
    p.op("gpsimd", lambda e: None, waits=[("qO", p.cnt("qO"))])
    return p.finalize()


def build_l1_attn(update_x):
    nc = bass.Bass("TRN2", target_bir_lowering=False)
    d = _io(nc, outs=("xco",) if update_x else ())
    wt = nc.dram_tensor("WT", [D, D], F32, kind="ExternalInput").ap()
    grow = nc.dram_tensor("grow", [1, 1024], F32, kind="ExternalInput").ap()
    brow = nc.dram_tensor("brow", [1, 1024], F32, kind="ExternalInput").ap()
    sca = nc.dram_tensor("sca", [1, 2], F32, kind="ExternalInput").ap()
    sin = nc.dram_tensor("sin", [64, 1024], F32, kind="ExternalInput").ap()
    sfo = nc.dram_tensor("sfo", [64, 1024], F32, kind="ExternalOutput").ap()
    p = Prog(nc)
    com = Common(p, d["cb"], d["cf"])
    xn_sb = p.sbuf("xn", [P, NCH * TLOC], BF16)
    sin_sb = p.sbuf("sin", [64, 1024], F32)
    xsrc = d.get("xs", d["xc"])
    if update_x:
        p.dma("sync", d["xs"], d["xc"], inc="qIn")
    p.dma("sync", sin_sb[:], sin, inc="qIn")
    in_rdy = [("qIn", p.cnt("qIn"))]
    mrow, rrow, ln_done = emit_ln(p, com, xsrc, xn_sb, "a", in_rdy)
    wts = dict(WT=wt, grow=grow, brow=brow, sca=sca)
    sfin, done = emit_l1_attn(p, com, xsrc, xn_sb, ln_done, wts, sin_sb,
                              in_rdy, update_x=update_x)
    if update_x:
        p.dma("gpsimd", d["xco"], d["xs"], waits=done, inc="qO")
    p.dma("gpsimd", sfo, sfin[:], waits=done, inc="qO")
    p.op("gpsimd", lambda e: None, waits=[("qO", p.cnt("qO"))])
    return p.finalize()



_NC_CACHE = {}


def _get_nc(name, fn):
    if name not in _NC_CACHE:
        _NC_CACHE[name] = fn()
    return _NC_CACHE[name]


def _host_gather(scores):
    g = np.zeros((128, 2048), dtype=BF)
    for r in range(NCORES):
        for e in range(8):
            for h in range(2):
                g[e * 16 + r * 2 + h, :] = \
                    scores[r][e, h * 2048:(h + 1) * 2048]
    return g


def kernel(**inputs):
    if USE_FUSED:
        return kernel_fused(**inputs)
    gi = lambda k: np.asarray(inputs[k], np.float32)
    shards = shard_x(inputs["L0_attn_x_seq"])
    cbv = make_consts()
    cfv = make_constsf()
    base = dict(cb=cbv, cf=cfv)

    l0f_w = dict(
        Wg_l=np.ascontiguousarray(gi("L0_ffn_Wg").reshape(8, 128, 8)
                                  .transpose(1, 0, 2).reshape(128, 64)),
        bg_t=np.ascontiguousarray(np.tile(gi("L0_ffn_bg"), 32)[None, :]),
        W1=gi("L0_ffn_W1"), W2=gi("L0_ffn_W2"),
        b1l=np.ascontiguousarray(gi("L0_ffn_b1").reshape(8, 32, 128)
                                 .transpose(2, 0, 1).reshape(128, 256)),
        b2l=np.ascontiguousarray(gi("L0_ffn_b2").reshape(8, 8, 128)
                                 .transpose(2, 0, 1).reshape(128, 64)))
    l0a_w = dict(
        WdT=np.ascontiguousarray(gi("L0_attn_W_decay").T),
        WiT=np.ascontiguousarray(gi("L0_attn_W_input").T),
        bd=chunk_cols(gi("L0_attn_b_decay")),
        bi=chunk_cols(gi("L0_attn_b_input")))
    l1f_w = dict(
        lng=chunk_cols(gi("L1_ffn_ln_g")), lnb=chunk_cols(gi("L1_ffn_ln_b")),
        W1=gi("L1_ffn_w1"), W2=gi("L1_ffn_w2"),
        b1l=np.ascontiguousarray(gi("L1_ffn_b1").reshape(8, 32, 128)
                                 .transpose(2, 0, 1).reshape(128, 256)),
        b2l=np.ascontiguousarray(gi("L1_ffn_b2").reshape(8, 8, 128)
                                 .transpose(2, 0, 1).reshape(128, 64)))
    ctl = np.ascontiguousarray(gi("L1_ffn_centroids").T.reshape(8, 128, 8)
                               .transpose(1, 0, 2).reshape(128, 64))
    l1a_w = dict(
        WT=np.ascontiguousarray(gi("L1_attn_W").T),
        grow=np.ascontiguousarray(gi("L1_attn_ln_gamma")[None, :]),
        brow=np.ascontiguousarray(gi("L1_attn_ln_beta")[None, :]),
        sca=np.array([[float(np.asarray(inputs["L1_attn_eta"])),
                       float(np.asarray(inputs["L1_attn_lam_logit"]))]],
                     np.float32))
    cores = list(range(NCORES))

    def run(name, fn, per_core):
        nc = _get_nc(name, fn)
        res = run_bass_kernel_spmd(nc, per_core, core_ids=cores)
        return res.results

    def l0_ffn_step(xs):
        r = run("l0f", build_l0_ffn,
                [dict(xc=xs[i], **base, **l0f_w) for i in cores])
        return [np.asarray(r[i]["xco"], np.float32) for i in cores]

    def l0_attn_step(xs):
        zc = np.zeros((P, NCH), np.float32)
        ra = run("l0aA", lambda: build_l0_attn(False),
                 [dict(xc=xs[i], carry=zc, **base, **l0a_w) for i in cores])
        fins = [np.asarray(ra[i]["fino"], np.float32) for i in cores]
        carries = [zc if i % 2 == 0 else fins[i - 1] for i in cores]
        rb = run("l0aB", lambda: build_l0_attn(True),
                 [dict(xc=xs[i], carry=carries[i], **base, **l0a_w)
                  for i in cores])
        return [np.asarray(rb[i]["xco"], np.float32) for i in cores]

    def l1_ffn_step(xs):
        rs = run("l1s", build_l1_scores,
                 [dict(xc=xs[i], centT_l=ctl, lng=l1f_w["lng"],
                       lnb=l1f_w["lnb"], **base) for i in cores])
        scores = [np.asarray(rs[i]["sco"]) for i in cores]
        g = _host_gather(scores)
        rb = run("l1f", build_l1_ffn,
                 [dict(xc=xs[i], scin=scores[i], gin=g, **base, **l1f_w)
                  for i in cores])
        return [np.asarray(rb[i]["xco"], np.float32) for i in cores]

    def l1_attn_step(xs):
        zs = np.zeros((64, 1024), np.float32)
        ra = run("l1aA", lambda: build_l1_attn(False),
                 [dict(xc=xs[i], sin=zs, **base, **l1a_w) for i in cores])
        sfins = [np.asarray(ra[i]["sfo"], np.float32) for i in cores]
        sins = [zs if i % 2 == 0 else sfins[i - 1] for i in cores]
        rb = run("l1aB", lambda: build_l1_attn(True),
                 [dict(xc=xs[i], sin=sins[i], **base, **l1a_w)
                  for i in cores])
        return [np.asarray(rb[i]["xco"], np.float32) for i in cores]

    xs = shards
    xs = l0_ffn_step(xs)
    xs = l0_attn_step(xs)
    xs = l0_ffn_step(xs)
    xs = l1_ffn_step(xs)
    xs = l1_attn_step(xs)
    xs = l1_ffn_step(xs)
    return unshard_x(xs)



def build_fused():
    nc = bass.Bass("TRN2", target_bir_lowering=False)
    T = nc.dram_tensor
    xc = T("xc", [D, TLOC], F32, kind="ExternalInput").ap()
    cb = T("cb", [160, 2688], BF16, kind="ExternalInput").ap()
    cf = T("cf", [128, 532], F32, kind="ExternalInput").ap()
    osel = T("osel", [1, 8], F32, kind="ExternalInput").ap()
    wgl = T("Wg_l", [P, 64], F32, kind="ExternalInput").ap()
    bgt = T("bg_t", [1, 256], F32, kind="ExternalInput").ap()
    w01 = T("W01", [8, D, 4096], F32, kind="ExternalInput").ap()
    b01 = T("b01l", [P, 256], F32, kind="ExternalInput").ap()
    w02 = T("W02", [8, 4096, D], F32, kind="ExternalInput").ap()
    b02 = T("b02l", [P, 64], F32, kind="ExternalInput").ap()
    wdt = T("WdT", [D, D], F32, kind="ExternalInput").ap()
    wit = T("WiT", [D, D], F32, kind="ExternalInput").ap()
    bd = T("bd", [P, NCH], F32, kind="ExternalInput").ap()
    bi = T("bi", [P, NCH], F32, kind="ExternalInput").ap()
    ctl = T("centT_l", [P, 64], F32, kind="ExternalInput").ap()
    lng = T("lng", [P, NCH], F32, kind="ExternalInput").ap()
    lnb = T("lnb", [P, NCH], F32, kind="ExternalInput").ap()
    w11 = T("W11", [8, D, 4096], F32, kind="ExternalInput").ap()
    b11 = T("b11l", [P, 256], F32, kind="ExternalInput").ap()
    w12 = T("W12", [8, 4096, D], F32, kind="ExternalInput").ap()
    b12 = T("b12l", [P, 64], F32, kind="ExternalInput").ap()
    wt1 = T("WT", [D, D], F32, kind="ExternalInput").ap()
    grow = T("grow", [1, 1024], F32, kind="ExternalInput").ap()
    brow = T("brow", [1, 1024], F32, kind="ExternalInput").ap()
    sca = T("sca", [1, 2], F32, kind="ExternalInput").ap()
    xco = T("xco", [D, TLOC], F32, kind="ExternalOutput").ap()
    xs = T("xs", [D, TLOC], F32, kind="Internal").ap()
    w01b = T("W01b", [8, D, 4096], BF16, kind="Internal").ap()
    w02b = T("W02b", [8, 4096, D], BF16, kind="Internal").ap()
    w11b = T("W11b", [8, D, 4096], BF16, kind="Internal").ap()
    w12b = T("W12b", [8, 4096, D], BF16, kind="Internal").ap()
    fin_in = T("fin_in", [P, NCH], F32, kind="Internal")
    fin_out = T("fin_out", [NCORES * P, NCH], F32, kind="Internal")
    sc_in = [T(f"sc_in{i}", [8, TLOC], BF16, kind="Internal")
             for i in range(2)]
    sc_out = [T(f"sc_out{i}", [NCORES * 8, TLOC], BF16, kind="Internal")
              for i in range(2)]
    s_in = T("s_in", [64, 1024], F32, kind="Internal")
    s_out = T("s_out", [NCORES * 64, 1024], F32, kind="Internal")
    gath_d = T("gath_d", [P, 2048], BF16, kind="Internal")
    dscr0 = None
    if SPARSE_L0:
        dscr0 = dict(
            xn_tm0=T("xn_tm0", [ROWS0 + P, 1024], BF16,
                     kind="Internal").ap(),
            xdelta0=T("xdelta0", [ROWS0 + P, 1024], F32,
                      kind="Internal").ap(),
            gidxB=T("gidxB", [CT0 + P, 1], I32, kind="Internal").ap(),
            gidxW=T("gidxW", [CT0 + P, 1], F32, kind="Internal").ap())
    dscr = None
    if SPARSE_L1:
        dscr = dict(
            xn_tm=T("xn_tm", [TLOC + P, 1024], BF16, kind="Internal").ap(),
            xdelta=T("xdelta", [TLOC + P, 1024], F32, kind="Internal").ap(),
            gidx=T("gidx_d", [CT + P, 1], I32, kind="Internal").ap())

    p = Prog(nc)
    com = Common(p, cb, cf)
    xn_sb = p.sbuf("xn", [P, NCH * TLOC], BF16)
    carry_sb = p.sbuf("carry", [P, NCH], F32)
    sin_sb = com.rowp[64:128, 0:1024]
    oselb = p.sbuf("oselb", [P, 8], F32)
    lng_sb = p.sbuf("lng", [P, NCH], F32)
    lnb_sb = p.sbuf("lnb", [P, NCH], F32)

    osel_sb = p.sbuf("osel", [1, 8], F32)
    p.dma("sync", xs, xc, inc="qIn")
    p.dma("sync", lng_sb[:], lng, inc="qIn")
    p.dma("sync", lnb_sb[:], lnb, inc="qIn")
    p.dma("sync", osel_sb[:], osel, inc="qIn")
    x_ready = [("qIn", p.cnt("qIn"))]
    gb_rdy = x_ready
    p.op("tensor", lambda t: t.matmul(com.pp[5][:, 0:8], com.ones_rf[:],
                                      osel_sb[:], start=True, stop=True),
         waits=x_ready + list(com.readyv), inc="tS")
    p.op("scalar", lambda sc: sc.copy(oselb[:], com.pp[5][:, 0:8]),
         waits=[("tS", p.cnt("tS"))], inc="sS")

    stash = {}
    stash_e = {"01": [], "02": [], "11": [], "12": []}
    for grp in (("01", w01, w01b, "02", w02, w02b),
                ("11", w11, w11b, "12", w12, w12b)):
        n1, s1_, d1, n2, s2_, d2 = grp
        for e in range(8):
            p.dma("gpsimd", d1[e], s1_[e],
                  waits=[("qStash", p.cnt("qStash"))], inc="qStash")
            stash_e[n1].append(p.cnt("qStash"))
            p.dma("gpsimd", d2[e], s2_[e],
                  waits=[("qStash", p.cnt("qStash"))], inc="qStash")
            stash_e[n2].append(p.cnt("qStash"))
        stash[n1] = p.cnt("qStash")
        stash[n2] = p.cnt("qStash")
    wts0 = dict(Wg_l=wgl, bg_t=bgt, W1=w01, b1l=b01, W2=w02, b2l=b02)
    wts0a = dict(WdT=wdt, WiT=wit, bd=bd, bi=bi)
    wts1 = dict(W1=w11, b1l=b11, W2=w12, b2l=b12)
    wts1a = dict(WT=wt1, grow=grow, brow=brow, sca=sca)

    def scoped(fn):
        p.begin_scope(p.snapshot())
        es = p.scope_es()
        with es:
            fn()
        p._scope_stack = None

    def l0_body(nm, xrd):
        mrow, rrow, ln_done = emit_ln(p, com, xs, xn_sb, nm, xrd)
        if SPARSE_L0:
            accS = p.sbuf(f"accS{nm}", [P, 2112], F32)
            zbuf = p.sbuf(f"zb{nm}", [P, 1024], F32)
            gtmp0 = p.sbuf(f"gt{nm}", [P, 1024], F32)
            sh0 = emit_l0s_pre(p, com, xn_sb, ln_done, dscr0, zbuf,
                               gtmp0, [], tag=f"p{nm}")
            gres = emit_l0s_gate(p, com, xs, xn_sb, ln_done, wts0,
                                 (mrow, rrow), accS, dscr0, sh0,
                                 tag=f"g{nm}")
            emit_l0x_sparse(p, com, xs, xn_sb, wts0, (w01b, w02b),
                            stash["02"], dscr0, sh0, gres, zbuf, gtmp0,
                            tag=f"x{nm}", st1=stash_e["01"],
                            st2=stash_e["02"])
        else:
            emit_l0_ffn(p, com, xs, xn_sb, ln_done, wts0, (mrow, rrow),
                        tag=f"f{nm}", wb16=(w01b, w02b),
                        stash_rdy=stash["02"])

    def s1():
        l0_body("l1", x_ready)
    scoped(s1)

    def s2():
        mrow, rrow, ln_done = emit_ln(p, com, xs, xn_sb, "l2", ())
        p.op("vector", lambda v: v.memset(carry_sb[:], 0.0), inc="vS3")
        zrdy = p.cnt("vS3")
        fin, fdone = emit_l0_attn(p, com, xs, xn_sb, ln_done, wts0a,
                                  carry_sb, [("vS3", zrdy)], tag="a2",
                                  update_x=False)
        p.dma("gpsimd", fin_in.ap(), fin[:], waits=fdone, inc="qCC")
        p.op("gpsimd", lambda g: g.collective_compute(
            "AllGather", ALU.bypass,
            replica_groups=[list(range(NCORES))],
            ins=[fin_in.ap().opt()], outs=[fin_out.ap().opt()]),
            waits=[("qCC", p.cnt("qCC"))], inc="ccS")
        gf = p.sbuf("gathf", [P, 64], F32)
        p.dma("gpsimd", gf.rearrange("p (r e) -> p r e", r=8),
              fin_out.ap().rearrange("(r p) e -> p r e", p=P),
              waits=[("ccS", p.cnt("ccS"))], inc="qCC")
        p.op("vector", lambda v: v.memset(carry_sb[:], 0.0),
             waits=[("qCC", p.cnt("qCC")), ("sS", 2)])
        for r in range(8):
            p.op("vector",
                 lambda v, r=r: v.scalar_tensor_tensor(
                     carry_sb[:], gf[:, r * 8:(r + 1) * 8],
                     oselb[:, r:r + 1], carry_sb[:], ALU.mult, ALU.add),
                 inc="vS3" if r == 7 else None)
    scoped(s2)
    carry_rdy = p.cnt("vS3")

    def s3():
        fin, done = emit_l0_attn(p, com, xs, xn_sb, [], wts0a,
                                 carry_sb, [("vS3", carry_rdy)], tag="a3",
                                 update_x=True)
    scoped(s3)

    def s4():
        l0_body("l4", ())
    scoped(s4)

    def l1_ffn_scope(idx):
        def body():
            mrow, rrow, ln_done = emit_ln(
                p, com, xs, xn_sb, f"l5{idx}", gb_rdy,
                fuse_double=True, gvec=lng_sb, bvec=lnb_sb)
            acc = p.sbuf("acc", [P, NCH * 1024], F32)
            scT = acc[0:8, 2304:4352].bitcast(BF16)
            gath = acc[:, 6400:7424].bitcast(BF16)
            mask = com.rowp[32:40, 2560:4608].bitcast(BF16)
            sc_done = emit_l1_scores(p, com, xn_sb, ln_done, ctl, scT,
                                     tag=f"s{idx}")
            p.dma("gpsimd", sc_in[idx].ap(), scT, waits=sc_done, inc="qCC")
            p.op("gpsimd", lambda g: g.collective_compute(
                "AllGather", ALU.bypass,
                replica_groups=[list(range(NCORES))],
                ins=[sc_in[idx].ap().opt()], outs=[sc_out[idx].ap().opt()]),
                waits=[("qCC", p.cnt("qCC"))], inc="ccS")
            sh = None
            if SPARSE_L1:
                sh = emit_l1s_pre(p, com, xn_sb, ln_done, sc_done, dscr,
                                  acc, tag=f"sp{idx}")
            p.dma("gpsimd",
                  gath_d.ap().rearrange("(e r h) t2 -> e r h t2",
                                        e=8, r=8),
                  sc_out[idx].ap().rearrange(
                      "(r e) (h t2) -> e r h t2", e=8, h=2),
                  waits=[("ccS", p.cnt("ccS"))], inc="qCC")
            p.dma("gpsimd", gath, gath_d.ap(),
                  waits=[("qCC", p.cnt("qCC"))], inc="qCC")
            gath_rdy = [("qCC", p.cnt("qCC"))]
            route_done = emit_l1_route(p, com, gath, gath_rdy, scT,
                                       sc_done, mask, acc, tag=f"r{idx}")
            if SPARSE_L1:
                emit_l1_ffn_sparse(p, com, xs, xn_sb, wts1, mask,
                                   route_done, mrow, rrow, (w11b, w12b),
                                   stash["12"], dscr, sh, acc,
                                   tag=f"sq{idx}")
            else:
                emit_l0_ffn(p, com, xs, xn_sb, ln_done, wts1, (mrow, rrow),
                            tag=f"q{idx}", gelu_tanh=True, mask_sb=mask,
                            mask_ready=route_done, extra_res=(mrow, rrow),
                            acc_ext=acc, wb16=(w11b, w12b),
                            stash_rdy=stash["12"])
        return body
    scoped(l1_ffn_scope(0))

    def s6():
        mrow, rrow, ln_done = emit_ln(p, com, xs, xn_sb, "l6", ())
        p.op("vector", lambda v: v.memset(sin_sb, 0.0), inc="vS5")
        zrdy = p.cnt("vS5")
        sfin, sdone = emit_l1_attn(p, com, xs, xn_sb, ln_done, wts1a,
                                   sin_sb, [("vS5", zrdy)], tag="t6",
                                   update_x=False)
        p.dma("gpsimd", s_in.ap(), sfin[:], waits=sdone, inc="qCC")
        p.op("gpsimd", lambda g: g.collective_compute(
            "AllGather", ALU.bypass,
            replica_groups=[list(range(NCORES))],
            ins=[s_in.ap().opt()], outs=[s_out.ap().opt()]),
            waits=[("qCC", p.cnt("qCC"))], inc="ccS")
        p.op("vector", lambda v: v.memset(sin_sb, 0.0),
             waits=[("ccS", p.cnt("ccS"))])
        gs = com.scr[3][64:128, :]
        for half in range(2):
            p.dma("gpsimd", gs.rearrange("p (r e) -> p r e", r=4),
                  s_out.ap()[half * 256:(half + 1) * 256, :].rearrange(
                      "(r p) e -> p r e", p=64),
                  waits=[("qCC", p.cnt("qCC"))] +
                  ([("vS5", p.cnt("vS5"))] if half == 1 else []),
                  inc="qCC")
            for r in range(4):
                p.op("vector",
                     lambda v, r=r, half=half: v.scalar_tensor_tensor(
                         sin_sb, gs[:, r * 1024:(r + 1) * 1024],
                         oselb[64:128, half * 4 + r:half * 4 + r + 1],
                         sin_sb, ALU.mult, ALU.add),
                     waits=[("qCC", p.cnt("qCC"))] if r == 0 else (),
                     inc="vS5" if r == 3 else None)
    scoped(s6)
    sin_rdy = p.cnt("vS5")

    def s7():
        sfin, done = emit_l1_attn(p, com, xs, xn_sb, [], wts1a,
                                  sin_sb, [("vS5", sin_rdy)], tag="t7",
                                  update_x=True, do_tanh=False)
    scoped(s7)

    scoped(l1_ffn_scope(1))

    p.begin_scope(p.snapshot())
    p.dma("gpsimd", xco, xs, inc="qO")
    p.op("gpsimd", lambda e: None, waits=[("qO", p.cnt("qO"))])
    return p.finalize()


def kernel_fused(**inputs):
    gi = lambda k: np.asarray(inputs[k], np.float32)
    shards = shard_x(inputs["L0_attn_x_seq"])
    per = []
    for i in range(NCORES):
        osel = np.zeros((1, 8), np.float32)
        if i % 2 == 1:
            osel[0, i - 1] = 1.0
        per.append(dict(
            xc=shards[i], cb=make_consts(), cf=make_constsf(), osel=osel,
            Wg_l=np.ascontiguousarray(gi("L0_ffn_Wg").reshape(8, 128, 8)
                                      .transpose(1, 0, 2).reshape(128, 64)),
            bg_t=np.ascontiguousarray(np.tile(gi("L0_ffn_bg"), 32)[None, :]),
            W01=gi("L0_ffn_W1"), W02=gi("L0_ffn_W2"),
            b01l=np.ascontiguousarray(gi("L0_ffn_b1").reshape(8, 32, 128)
                                      .transpose(2, 0, 1).reshape(128, 256)),
            b02l=np.ascontiguousarray(gi("L0_ffn_b2").reshape(8, 8, 128)
                                      .transpose(2, 0, 1).reshape(128, 64)),
            WdT=np.ascontiguousarray(gi("L0_attn_W_decay").T),
            WiT=np.ascontiguousarray(gi("L0_attn_W_input").T),
            bd=chunk_cols(gi("L0_attn_b_decay")),
            bi=chunk_cols(gi("L0_attn_b_input")),
            centT_l=np.ascontiguousarray(
                gi("L1_ffn_centroids").T.reshape(8, 128, 8)
                .transpose(1, 0, 2).reshape(128, 64)),
            lng=chunk_cols(gi("L1_ffn_ln_g")),
            lnb=chunk_cols(gi("L1_ffn_ln_b")),
            W11=gi("L1_ffn_w1"), W12=gi("L1_ffn_w2"),
            b11l=np.ascontiguousarray(gi("L1_ffn_b1").reshape(8, 32, 128)
                                      .transpose(2, 0, 1).reshape(128, 256)),
            b12l=np.ascontiguousarray(gi("L1_ffn_b2").reshape(8, 8, 128)
                                      .transpose(2, 0, 1).reshape(128, 64)),
            WT=np.ascontiguousarray(gi("L1_attn_W").T),
            grow=np.ascontiguousarray(gi("L1_attn_ln_gamma")[None, :]),
            brow=np.ascontiguousarray(gi("L1_attn_ln_beta")[None, :]),
            sca=np.array([[float(np.asarray(inputs["L1_attn_eta"])),
                           float(np.asarray(inputs["L1_attn_lam_logit"]))]],
                         np.float32)))
    nc = _get_nc("fused", build_fused)
    res = run_bass_kernel_spmd(nc, per, core_ids=list(range(NCORES)))
    return unshard_x([np.asarray(res.results[i]["xco"], np.float32)
                      for i in range(NCORES)])
